# version 13
# baseline (speedup 1.0000x reference)
import numpy as np
from contextlib import ExitStack

import concourse.bass as bass
import concourse.mybir as mybir
from concourse.bass_utils import run_bass_kernel_spmd

F32 = mybir.dt.float32
F32R = mybir.dt.float32r
BF16 = mybir.dt.bfloat16
AF = mybir.ActivationFunctionType
ALU = mybir.AluOpType
AXX = mybir.AxisListType.X

D = 2048
S = 2048
T = 1024
NDT = 16
HD = 128
INV_D = 1.0 / 2048.0
EPS = 1e-5
Q_EPS = 1e-5


STOP_AFTER = 99


class Sem:
    __slots__ = ("h", "n")

    def __init__(self, h):
        self.h = h
        self.n = 0


def build_nc():
    nc = bass.Bass("TRN2", target_bir_lowering=False, debug=False,
                   detect_race_conditions=False)

    din = lambda name, shape: nc.dram_tensor(name, shape, F32, kind="ExternalInput").ap()
    xT = din("xT", [D, S])
    wq = din("wq", [D, D])
    wk = din("wk", [D, 512])
    wv = din("wv", [D, 512])
    wo = din("wo", [D, D])
    wfT = din("wfT", [D, D])
    gbb = din("gbb", [128, 48])
    kcch = din("kcch", [128, S])
    kccl = din("kccl", [128, S])
    kssh = din("kssh", [128, S])
    kssl = din("kssl", [128, S])
    qcch = din("qcch", [128, T])
    qccl = din("qccl", [128, T])
    qssh = din("qssh", [128, T])
    qssl = din("qssl", [128, T])
    eye = din("eye", [128, 128])
    y = nc.dram_tensor("y", [D, T], F32, kind="ExternalOutput").ap()
    x1s = nc.dram_tensor("x1s", [D, T], F32).ap()
    thrs = nc.dram_tensor("thrs", [1, 2], F32)

    with ExitStack() as top:
        _uid = [0]

        def SB(st_, name, shape, dt=F32):
            _uid[0] += 1
            return st_.enter_context(nc.sbuf_tensor(f"{name}_{_uid[0]}", shape, dt))

        sb = lambda name, shape, dt=F32: SB(top, name, shape, dt)

        P = [top.enter_context(nc.psum_tensor(f"ps{i}", [128, 512], F32))
             for i in range(8)]

        ds = Sem(top.enter_context(nc.semaphore("ds")))
        gd = Sem(top.enter_context(nc.semaphore("gd")))
        ts = Sem(top.enter_context(nc.semaphore("ts")))
        vs = Sem(top.enter_context(nc.semaphore("vs")))
        ss = Sem(top.enter_context(nc.semaphore("ss")))
        gs = Sem(top.enter_context(nc.semaphore("gs")))
        dw = [Sem(top.enter_context(nc.semaphore("dw0"))),
              Sem(top.enter_context(nc.semaphore("dw1")))]
        dx = [Sem(top.enter_context(nc.semaphore("dx0"))),
              Sem(top.enter_context(nc.semaphore("dx1")))]
        bar = Sem(top.enter_context(nc.semaphore("bar")))

        eye_f = sb("eye_f", [128, 128])
        ones_c_bf = sb("ones_c_bf", [128, 1], BF16)
        ones128 = sb("ones128", [128, 128], BF16)
        gb = sb("gb", [128, 48])
        smb = sb("smb", [128, 1024], BF16)
        smb2 = sb("smb2", [128, 1024], BF16)
        pub1 = sb("pub1", [1, 8])
        pub2 = sb("pub2", [1, 8])
        pub3 = sb("pub3", [1, 8])
        pub4 = sb("pub4", [1, 8])
        xbuf = [sb("xbuf0", [128, S]), sb("xbuf1", [128, S])]
        tmp1 = sb("tmp1", [128, S])
        onh = [sb(f"onh{i}", [128, T], BF16) for i in range(16)]
        onl = [sb(f"onl{i}", [128, T], BF16) for i in range(16)]

        top.enter_context(nc.Block())

        ENG = (nc.tensor, nc.vector, nc.scalar, nc.sync, nc.gpsimd)

        def dma(q, out, in_, sem=ds):
            q.dma_start(out=out, in_=in_).then_inc(sem.h, 16)
            sem.n += 16

        def tinc(inst, sem):
            inst.then_inc(sem.h, 1)
            sem.n += 1

        def wait(eng, *sems):
            for s_ in sems:
                eng.wait_ge(s_.h, s_.n)

        def pair(dst_h, dst_l, src):
            nc.vector.tensor_copy(dst_h, src)
            nc.vector.tensor_tensor(dst_l, src, dst_h, ALU.subtract)

        def vpub():
            tinc(nc.vector.tensor_copy(pub1[:], pub2[:]), vs)

        def spub():
            tinc(nc.scalar.copy(pub3[:], pub2[:]), ss)

        def gpub():
            tinc(nc.gpsimd.tensor_copy(pub4[:], pub2[:]), gs)

        def barrier():
            nc.sync.wait_ge(ds.h, ds.n)
            nc.gpsimd.wait_ge(gd.h, gd.n)
            for eng in ENG:
                for s_ in (ds, gd, ts, vs, ss, gs, dw[0], dw[1], dx[0], dx[1]):
                    eng.wait_ge(s_.h, s_.n)

        def finish_early(extra=()):
            barrier()
            dma(nc.gpsimd, y[0:128, :], tmp1[:, 0:T], gd)
            for i, ap_ in enumerate(extra):
                dma(nc.gpsimd, y[(i + 1) * 128:(i + 1) * 128 + ap_.shape[0],
                                 0:ap_.shape[1]], ap_, gd)
            nc.gpsimd.wait_ge(gd.h, gd.n)

        dma(nc.sync, eye_f[:], eye)
        dma(nc.sync, gb[:], gbb)
        wait(nc.vector, ds)
        nc.vector.tensor_scalar_mul(pub1[:], eye_f[0:1, 0:8], 0.0)
        nc.vector.tensor_scalar_mul(pub2[:], eye_f[0:1, 0:8], 0.0)
        nc.vector.tensor_scalar(ones_c_bf[:], eye_f[:, 0:1], 0.0, 1.0,
                                op0=ALU.mult, op1=ALU.add)
        nc.vector.tensor_scalar(ones128[:], eye_f[:], 0.0, 1.0,
                                op0=ALU.mult, op1=ALU.add)
        for c in range(8):
            nc.vector.tensor_scalar_mul(smb[:, c * 128:(c + 1) * 128], eye_f[:], 0.0)
            nc.vector.tensor_scalar_mul(smb2[:, c * 128:(c + 1) * 128], eye_f[:], 0.0)
        vpub()
        if STOP_AFTER <= 0:
            finish_early()
            return nc

        with ExitStack() as scA:
            sbA = lambda name, shape, dt=F32: SB(scA, name, shape, dt)
            bc_s1 = sbA("bc_s1", [128, S])
            bc_negm = sbA("bc_negm", [128, S])
            ck_cc = sbA("ck_cc", [128, S])
            ck_ss = sbA("ck_ss", [128, S])
            cq_cc = sbA("cq_cc", [128, T])
            cq_ss = sbA("cq_ss", [128, T])
            tmp2 = sbA("tmp2", [128, S])
            tmpr = sbA("tmpr", [128, T], BF16)

            with ExitStack() as sc1:
                sb1 = lambda name, shape, dt=F32: SB(sc1, name, shape, dt)
                xh1 = sb1("xh1", [128, S], BF16)
                xl1 = sb1("xl1", [128, S], BF16)
                x2h1 = sb1("x2h1", [128, S], BF16)
                x2l1 = sb1("x2l1", [128, S], BF16)
                sm = sb1("sm", [1, 2048])
                sm2 = sb1("sm2", [1, 2048])
                st_s1 = sb1("st_s1", [1, 2048])
                sumx_tgt = [P[0][0:1, :], P[1][0:1, :], P[2][0:1, :], P[3][0:1, :]]
                sumx2_tgt = [P[4][0:1, :], P[5][0:1, :], P[6][0:1, :], P[7][0:1, :]]
                for dt in range(NDT):
                    sl = dt % 2
                    nc.sync.wait_ge(vs.h, vs.n)
                    nc.sync.wait_ge(ss.h, ss.n)
                    dma(nc.sync, xbuf[sl][:], xT[dt * 128:(dt + 1) * 128, :])
                    wait(nc.scalar, ds)
                    if dt >= 1:
                        nc.scalar.wait_ge(vs.h, vs.n)
                    nc.scalar.activation(tmp2[:], xbuf[sl][:], AF.Square)
                    spub()
                    wait(nc.vector, ds)
                    if dt >= 1:
                        nc.vector.wait_ge(ts.h, ts.n)
                    pair(xh1[:], xl1[:], xbuf[sl][:])
                    wait(nc.vector, ss)
                    pair(x2h1[:], x2l1[:], tmp2[:])
                    vpub()
                    wait(nc.tensor, vs)
                    st = dt == 0
                    sp = dt == NDT - 1
                    for c in range(4):
                        cs = slice(c * 512, (c + 1) * 512)
                        nc.tensor.matmul(P[c][:], ones128[:], xh1[:, cs], start=st, stop=False)
                        nc.tensor.matmul(P[c][:], ones128[:], xl1[:, cs], start=False, stop=sp)
                        nc.tensor.matmul(P[4 + c][:], ones128[:], x2h1[:, cs], start=st, stop=False)
                        inst = nc.tensor.matmul(P[4 + c][:], ones128[:], x2l1[:, cs], start=False, stop=sp)
                    tinc(inst, ts)
                if STOP_AFTER <= 0.3:
                    finish_early()
                    return nc
                wait(nc.vector, ts)
                for c in range(4):
                    nc.vector.tensor_copy(sm[0:1, c * 512:(c + 1) * 512], sumx_tgt[c])
                    nc.vector.tensor_copy(sm2[0:1, c * 512:(c + 1) * 512], sumx2_tgt[c])
                nc.vector.tensor_scalar_mul(sm[:], sm[:], INV_D)
                nc.vector.tensor_scalar_mul(sm2[:], sm2[:], INV_D)
                nc.vector.tensor_mul(st_s1[:], sm[:], sm[:])
                nc.vector.tensor_sub(sm2[:], sm2[:], st_s1[:])
                nc.vector.tensor_scalar_add(sm2[:], sm2[:], EPS)
                vpub()
                wait(nc.scalar, vs)
                nc.scalar.activation(st_s1[:], sm2[:], AF.Ln)
                nc.scalar.activation(st_s1[:], st_s1[:], AF.Exp, scale=-0.5)
                spub()
                wait(nc.vector, ss)
                nc.vector.tensor_scalar_mul(sm[:], sm[:], -1.0)
                vpub()
                if STOP_AFTER <= 0.6:
                    finish_early()
                    return nc

                for (srcv, dstt) in ((st_s1, bc_s1), (sm, bc_negm)):
                    for half in range(2):
                        hs = slice(half * 1024, (half + 1) * 1024)
                        pair(smb[0:1, :], smb2[0:1, :], srcv[0:1, hs])
                        vpub()
                        wait(nc.tensor, vs)
                        for c in range(2):
                            s0 = slice(c * 512, (c + 1) * 512)
                            nc.tensor.matmul(P[c][:], ones128[:], smb[:, s0], start=True, stop=False)
                            inst = nc.tensor.matmul(P[c][:], ones128[:], smb2[:, s0], start=False, stop=True)
                        tinc(inst, ts)
                        wait(nc.vector, ts)
                        for c in range(2):
                            nc.vector.tensor_copy(
                                dstt[:, half * 1024 + c * 512:half * 1024 + (c + 1) * 512],
                                P[c][:])
                        vpub()

                if STOP_AFTER <= 0.8:
                    finish_early([bc_s1[:, 0:1024], bc_negm[:, 0:1024],
                                  smb[:, 0:512].bitcast(F32), st_s1[0:1, 0:1024]])
                    return nc
                for (hi_t, lo_t, dst, w, scl) in (
                        (kcch, kccl, ck_cc, S, bc_s1), (kssh, kssl, ck_ss, S, bc_s1),
                        (qcch, qccl, cq_cc, T, bc_s1), (qssh, qssl, cq_ss, T, bc_s1)):
                    nc.sync.wait_ge(vs.h, vs.n)
                    dma(nc.sync, tmp1[:, 0:w], hi_t)
                    dma(nc.sync, tmp2[:, 0:w], lo_t)
                    wait(nc.vector, ds)
                    nc.vector.tensor_add(tmp1[:, 0:w], tmp1[:, 0:w], tmp2[:, 0:w])
                    nc.vector.tensor_mul(dst[:], tmp1[:, 0:w], scl[:, 0:w])
                    vpub()
            barrier()
            if STOP_AFTER <= 1:
                finish_early([ck_cc[:, 0:1024], ck_ss[:, 0:1024],
                              cq_cc[:, 0:1024], cq_ss[:, 0:1024]])
                return nc

            with ExitStack() as sc2:
                sb2 = lambda name, shape, dt=F32: SB(sc2, name, shape, dt)
                kresh = sb2("kresh", [128, S], BF16)
                kresl = sb2("kresl", [128, S], BF16)
                vth = sb2("vth", [128, S], BF16)
                vtl = sb2("vtl", [128, S], BF16)
                phi = [sb2("phi0", [128, T], BF16), sb2("phi1", [128, T], BF16)]
                plo = [sb2("plo0", [128, T], BF16), sb2("plo1", [128, T], BF16)]
                pfull = [sb2("pf0", [128, T]), sb2("pf1", [128, T])]
                invv = sb2("invv", [1, 1024])
                csKV = sb2("csKV", [128, 2])
                csQ = sb2("csQ", [128, 4])
                negmh = sb2("negmh", [128, 4])
                k2ln = sb2("k2ln", [1, 32])

                for g in range(4):
                    with ExitStack() as sckv:
                        sbk = lambda name, shape, dt=F32: SB(sckv, name, shape, dt)
                        wkh = sbk("wkh", [128, 2048], BF16)
                        wkl = sbk("wkl", [128, 2048], BF16)
                        wvh = sbk("wvh", [128, 2048], BF16)
                        wvl = sbk("wvl", [128, 2048], BF16)
                        xph = sbk("xph", [128, 2048], BF16)
                        xpl = sbk("xpl", [128, 2048], BF16)
                        nc.sync.wait_ge(ts.h, ts.n)
                        for dt in range(NDT):
                            dma(nc.sync, xbuf[0][:, dt * 128:(dt + 1) * 128],
                                wk[dt * 128:(dt + 1) * 128, g * 128:(g + 1) * 128])
                            dma(nc.sync, xbuf[1][:, dt * 128:(dt + 1) * 128],
                                wv[dt * 128:(dt + 1) * 128, g * 128:(g + 1) * 128])
                        wait(nc.vector, ds)
                        pair(wkh[:], wkl[:], xbuf[0][:])
                        pair(wvh[:], wvl[:], xbuf[1][:])
                        vpub()
                        wait(nc.tensor, vs)
                        for dt in range(NDT):
                            st = dt == 0
                            sp = dt == NDT - 1
                            w0 = slice(dt * 128, (dt + 1) * 128)
                            nc.tensor.matmul(P[6][:, 0:1], wkh[:, w0], ones_c_bf[:], start=st, stop=False)
                            nc.tensor.matmul(P[6][:, 0:1], wkl[:, w0], ones_c_bf[:], start=False, stop=sp)
                            nc.tensor.matmul(P[7][:, 0:1], wvh[:, w0], ones_c_bf[:], start=st, stop=False)
                            inst = nc.tensor.matmul(P[7][:, 0:1], wvl[:, w0], ones_c_bf[:], start=False, stop=sp)
                        tinc(inst, ts)
                        wait(nc.vector, ts)
                        nc.vector.tensor_copy(csKV[:, 0:1], P[6][:, 0:1])
                        nc.vector.tensor_copy(csKV[:, 1:2], P[7][:, 0:1])
                        vpub()

                        kt_tgt = [P[0][:], P[1][:], P[2][:], P[3][:]]
                        vt_tgt = [P[4][:], P[5][:], P[6][:], P[7][:]]
                        slot_pair = [vs.n, vs.n]
                        slot_mm = [ts.n, ts.n]
                        for dt in range(NDT):
                            sl = dt % 2
                            nc.sync.wait_ge(vs.h, slot_pair[sl])
                            dma(nc.sync, xbuf[sl][:], xT[dt * 128:(dt + 1) * 128, :])
                            wait(nc.vector, ds)
                            nc.vector.wait_ge(ts.h, ts.n)
                            pair(xph[:], xpl[:], xbuf[sl][:])
                            vpub()
                            slot_pair[sl] = vs.n
                            wait(nc.tensor, vs)
                            st = dt == 0
                            sp = dt == NDT - 1
                            w0 = slice(dt * 128, (dt + 1) * 128)
                            for c in range(4):
                                cs = slice(c * 512, (c + 1) * 512)
                                nc.tensor.matmul(kt_tgt[c], wkh[:, w0], xph[:, cs], start=st, stop=False)
                                nc.tensor.matmul(kt_tgt[c], wkh[:, w0], xpl[:, cs], start=False, stop=False)
                                nc.tensor.matmul(kt_tgt[c], wkl[:, w0], xph[:, cs], start=False, stop=sp)
                                nc.tensor.matmul(vt_tgt[c], wvh[:, w0], xph[:, cs], start=st, stop=False)
                                nc.tensor.matmul(vt_tgt[c], wvh[:, w0], xpl[:, cs], start=False, stop=False)
                                inst = nc.tensor.matmul(vt_tgt[c], wvl[:, w0], xph[:, cs], start=False, stop=sp)
                            tinc(inst, ts)
                            slot_mm[sl] = ts.n

                        wait(nc.vector, ts)
                        for c in range(4):
                            cs = slice(c * 512, (c + 1) * 512)
                            nc.vector.scalar_tensor_tensor(tmp1[:, cs], bc_negm[:, cs], csKV[:, 0:1],
                                                           kt_tgt[c], op0=ALU.mult, op1=ALU.add)
                        vpub()
                        wait(nc.gpsimd, vs)
                        dma(nc.gpsimd, tmp2[0:64, :], tmp1[64:128, :], gd)
                        dma(nc.gpsimd, tmp2[64:128, :], tmp1[0:64, :], gd)
                        wait(nc.vector, gd)
                        nc.vector.tensor_mul(tmp1[:], tmp1[:], ck_cc[:])
                        nc.vector.tensor_mul(tmp2[:], tmp2[:], ck_ss[:])
                        nc.vector.tensor_add(tmp1[:], tmp1[:], tmp2[:])
                        pair(kresh[:], kresl[:], tmp1[:])
                        vpub()
                        wait(nc.scalar, vs)
                        for half in range(2):
                            hs = slice(half * 1024, (half + 1) * 1024)
                            if half == 1:
                                nc.scalar.wait_ge(ts.h, ts.n)
                            nc.scalar.activation(tmpr[:], tmp1[:, hs], AF.Square)
                            spub()
                            wait(nc.tensor, ss)
                            tg = [kt_tgt[2 * half], kt_tgt[2 * half + 1]]
                            nc.tensor.matmul(tg[0], ones128[:], tmpr[:, 0:512], start=True, stop=True)
                            inst = nc.tensor.matmul(tg[1], ones128[:], tmpr[:, 512:1024],
                                                    start=True, stop=True)
                            tinc(inst, ts)
                        wait(nc.vector, ts)
                        for c in range(4):
                            nc.vector.tensor_copy(xbuf[1][0:1, c * 512:(c + 1) * 512],
                                                  kt_tgt[c][0:1, :])
                        nc.vector.reduce_max(k2ln[0:1, 0:1], xbuf[1][0:1, 0:2048], axis=AXX)
                        vpub()
                        wait(nc.scalar, vs)
                        nc.scalar.activation(k2ln[0:1, 1:2], k2ln[0:1, 0:1], AF.Ln)
                        spub()

                        wait(nc.vector, ts)
                        vn = xbuf[0]
                        for c in range(4):
                            cs = slice(c * 512, (c + 1) * 512)
                            nc.vector.scalar_tensor_tensor(vn[:, cs], bc_negm[:, cs], csKV[:, 1:2],
                                                           vt_tgt[c], op0=ALU.mult, op1=ALU.add)
                        nc.vector.tensor_mul(vn[:], vn[:], bc_s1[:])
                        vpub()
                        wait(nc.tensor, vs)
                        for k in range(16):
                            tg = P[6] if k % 2 == 0 else P[7]
                            if k >= 2:
                                nc.tensor.wait_ge(vs.h, vs.n)
                            inst = nc.tensor.transpose(tg[:, 0:128], vn[:, k * 128:(k + 1) * 128], eye_f[:])
                            tinc(inst, ts)
                            wait(nc.vector, ts)
                            ksl = slice(k * 128, (k + 1) * 128)
                            nc.vector.tensor_copy(vth[:, ksl], tg[:, 0:128])
                            nc.vector.tensor_tensor(vtl[:, ksl], tg[:, 0:128], vth[:, ksl], ALU.subtract)
                            vpub()

                    if STOP_AFTER <= 2 and g == 0:
                        finish_early([csKV[:, 0:2], k2ln[0:1, 0:32],
                                      xbuf[0][:, 0:1024]])
                        return nc
                    with ExitStack() as scq:
                        sbq = lambda name, shape, dt=F32: SB(scq, name, shape, dt)
                        wqs = [sbq("wqs0", [128, 512]), sbq("wqs1", [128, 512])]
                        wqph = [sbq("wqph0", [128, 512], BF16), sbq("wqph1", [128, 512], BF16)]
                        wqpl = [sbq("wqpl0", [128, 512], BF16), sbq("wqpl1", [128, 512], BF16)]
                        xoph = sbq("xoph", [128, T], BF16)
                        xopl = sbq("xopl", [128, T], BF16)
                        qhh = [sbq(f"qhh{i}", [128, T], BF16) for i in range(4)]
                        qll = [sbq(f"qll{i}", [128, T], BF16) for i in range(4)]

                        cs_tgt = [P[0][:, 0:1], P[2][:, 0:1], P[4][:, 0:1], P[6][:, 0:1]]
                        slot_pair = [vs.n, vs.n]
                        slot_mm = [ts.n, ts.n]
                        for dt in range(NDT):
                            sl = dt % 2
                            nc.sync.wait_ge(vs.h, slot_pair[sl])
                            dma(nc.sync, wqs[sl][:], wq[dt * 128:(dt + 1) * 128, g * 512:(g + 1) * 512])
                            wait(nc.vector, ds)
                            nc.vector.wait_ge(ts.h, slot_mm[sl])
                            pair(wqph[sl][:], wqpl[sl][:], wqs[sl][:])
                            vpub()
                            slot_pair[sl] = vs.n
                            wait(nc.tensor, vs)
                            st = dt == 0
                            sp = dt == NDT - 1
                            for h in range(4):
                                hsl = slice(h * 128, (h + 1) * 128)
                                nc.tensor.matmul(cs_tgt[h], wqph[sl][:, hsl], ones_c_bf[:], start=st, stop=False)
                                inst = nc.tensor.matmul(cs_tgt[h], wqpl[sl][:, hsl], ones_c_bf[:],
                                                        start=False, stop=sp)
                            tinc(inst, ts)
                            slot_mm[sl] = ts.n
                        wait(nc.vector, ts)
                        for h in range(4):
                            nc.vector.tensor_copy(csQ[:, h:h + 1], cs_tgt[h])
                        vpub()

                        slot_pair = [vs.n, vs.n]
                        slot_mm = [ts.n, ts.n]
                        for dt in range(NDT):
                            sl = dt % 2
                            nc.sync.wait_ge(vs.h, slot_pair[sl])
                            dma(nc.sync, wqs[sl][:], wq[dt * 128:(dt + 1) * 128, g * 512:(g + 1) * 512])
                            dma(nc.sync, xbuf[sl][:, 0:T], xT[dt * 128:(dt + 1) * 128, 0:T])
                            wait(nc.vector, ds)
                            nc.vector.wait_ge(ts.h, ts.n)
                            pair(wqph[sl][:], wqpl[sl][:], wqs[sl][:])
                            pair(xoph[:], xopl[:], xbuf[sl][:, 0:T])
                            vpub()
                            slot_pair[sl] = vs.n
                            wait(nc.tensor, vs)
                            st = dt == 0
                            sp = dt == NDT - 1
                            for h in range(4):
                                hsl = slice(h * 128, (h + 1) * 128)
                                for c0 in (0, 512):
                                    rs = slice(c0, c0 + 512)
                                    tgt = P[2 * h + c0 // 512][:]
                                    nc.tensor.matmul(tgt, wqph[sl][:, hsl], xoph[:, rs], start=st, stop=False)
                                    nc.tensor.matmul(tgt, wqph[sl][:, hsl], xopl[:, rs], start=False, stop=False)
                                    inst = nc.tensor.matmul(tgt, wqpl[sl][:, hsl], xoph[:, rs],
                                                            start=False, stop=sp)
                            tinc(inst, ts)
                            slot_mm[sl] = ts.n

                        wait(nc.vector, ts)
                        for h in range(4):
                            segs = [(P[2 * h][:], 0, 512), (P[2 * h + 1][:], 512, 1024)]
                            own = [P[2 * h][0:1, :], P[2 * h + 1][0:1, :]]
                            for (ap_, a, b) in segs:
                                nc.vector.scalar_tensor_tensor(tmp1[:, a:b], bc_negm[:, a:b],
                                                               csQ[:, h:h + 1], ap_,
                                                               op0=ALU.mult, op1=ALU.add)
                            vpub()
                            wait(nc.gpsimd, vs)
                            dma(nc.gpsimd, tmp2[0:64, 0:T], tmp1[64:128, 0:T], gd)
                            dma(nc.gpsimd, tmp2[64:128, 0:T], tmp1[0:64, 0:T], gd)
                            wait(nc.vector, gd)
                            nc.vector.tensor_mul(tmp1[:, 0:T], tmp1[:, 0:T], cq_cc[:])
                            nc.vector.tensor_mul(tmp2[:, 0:T], tmp2[:, 0:T], cq_ss[:])
                            nc.vector.tensor_add(tmp1[:, 0:T], tmp1[:, 0:T], tmp2[:, 0:T])
                            pair(qhh[h][:], qll[h][:], tmp1[:, 0:T])
                            vpub()
                            wait(nc.scalar, vs)
                            nc.scalar.activation(tmpr[:], tmp1[:, 0:T], AF.Square)
                            spub()
                            wait(nc.tensor, ss, vs)
                            nc.tensor.matmul(P[2 * h][:], ones128[:], tmpr[:, 0:512], start=True, stop=True)
                            inst = nc.tensor.matmul(P[2 * h + 1][:], ones128[:], tmpr[:, 512:1024],
                                                    start=True, stop=True)
                            tinc(inst, ts)
                            wait(nc.vector, ts)
                            nc.vector.tensor_copy(invv[0:1, 0:512], own[0])
                            nc.vector.tensor_copy(invv[0:1, 512:1024], own[1])
                            nc.vector.reduce_max(k2ln[0:1, 4 + h:5 + h], invv[:], axis=AXX)
                            vpub()
                            if STOP_AFTER <= 2.5 and g == 0 and h == 0:
                                nc.vector.tensor_copy(xbuf[1][:, 0:1024], tmpr[:])
                                nc.vector.tensor_copy(xbuf[1][:, 1024:2048], qhh[0][:])
                                nc.vector.tensor_copy(xbuf[0][0:1, 0:512], own[0])
                                vpub()
                                finish_early([xbuf[1][:, 0:1024],
                                              xbuf[1][:, 1024:2048],
                                              k2ln[0:1, 0:32],
                                              xbuf[0][0:1, 0:512]])
                                return nc

                        wait(nc.scalar, vs)
                        nc.scalar.activation(k2ln[0:1, 8:12], k2ln[0:1, 4:8], AF.Ln)
                        spub()
                        wait(nc.vector, ss)
                        nc.vector.tensor_single_scalar(k2ln[0:1, 8:12], k2ln[0:1, 8:12],
                                                       k2ln[0:1, 1:2], ALU.add)
                        vpub()
                        wait(nc.scalar, vs)
                        nc.scalar.activation(k2ln[0:1, 8:12], k2ln[0:1, 8:12], AF.Exp, scale=0.5)
                        spub()
                        wait(nc.vector, ss)
                        nc.vector.tensor_scalar_mul(smb[0:1, 0:4], k2ln[0:1, 8:12], -1.0)
                        vpub()
                        wait(nc.tensor, vs)
                        tinc(nc.tensor.matmul(P[6][:, 0:4], ones128[:], smb[:, 0:4],
                                              start=True, stop=True), ts)
                        wait(nc.vector, ts)
                        nc.vector.tensor_copy(negmh[:], P[6][:, 0:4])
                        vpub()
                        nc.tensor.wait_ge(vs.h, vs.n)

                        for h in range(4):
                            exp_mark = [ss.n, ss.n]
                            lo_mark = [vs.n, vs.n]
                            lo_used = [ts.n, ts.n]

                            def emit_lo(jj):
                                kwj = slice(jj * 128, (jj + 1) * 128)
                                pjj = jj % 2
                                nc.tensor.wait_ge(vs.h, lo_mark[pjj])
                                nc.tensor.matmul(P[6][:], ones128[:], plo[pjj][:, 0:512],
                                                 start=False, stop=(jj == 15))
                                nc.tensor.matmul(P[7][:], ones128[:], plo[pjj][:, 512:1024],
                                                 start=False, stop=(jj == 15))
                                for c0 in (0, 512):
                                    rs = slice(c0, c0 + 512)
                                    nc.tensor.matmul(P[4 + c0 // 512][:], vth[:, kwj],
                                                     plo[pjj][:, rs], start=False, stop=False)
                                    inst = nc.tensor.matmul(P[4 + c0 // 512][:], vtl[:, kwj],
                                                            phi[pjj][:, rs], start=False,
                                                            stop=(jj == 15))
                                tinc(inst, ts)
                                lo_used[pjj] = ts.n

                            for j in range(16):
                                pj = j % 2
                                kw = slice(j * 128, (j + 1) * 128)
                                nc.tensor.wait_ge(ss.h, exp_mark[pj])
                                for c0 in (0, 512):
                                    rs = slice(c0, c0 + 512)
                                    tg = P[2 * pj + c0 // 512][:]
                                    nc.tensor.matmul(tg, kresh[:, kw], qhh[h][:, rs], start=True, stop=False)
                                    nc.tensor.matmul(tg, kresh[:, kw], qll[h][:, rs], start=False, stop=False)
                                    inst = nc.tensor.matmul(tg, kresl[:, kw], qhh[h][:, rs],
                                                            start=False, stop=True)
                                tinc(inst, ts)
                                nc.scalar.wait_ge(ts.h, ts.n)
                                nc.scalar.wait_ge(ts.h, lo_used[pj])
                                for c0 in (0, 512):
                                    rs = slice(c0, c0 + 512)
                                    nc.scalar.activation(phi[pj][:, rs], P[2 * pj + c0 // 512][:],
                                                         AF.Exp, bias=negmh[:, h:h + 1])
                                    nc.scalar.activation(pfull[pj][:, rs], P[2 * pj + c0 // 512][:],
                                                         AF.Exp, bias=negmh[:, h:h + 1])
                                spub()
                                exp_mark[pj] = ss.n
                                nc.vector.wait_ge(ss.h, ss.n)
                                nc.vector.wait_ge(ts.h, lo_used[pj])
                                nc.vector.tensor_tensor(plo[pj][:], pfull[pj][:], phi[pj][:],
                                                        ALU.subtract)
                                vpub()
                                lo_mark[pj] = vs.n
                                nc.tensor.wait_ge(ss.h, exp_mark[pj])
                                stj = j == 0
                                nc.tensor.matmul(P[6][:], ones128[:], phi[pj][:, 0:512],
                                                 start=stj, stop=False)
                                nc.tensor.matmul(P[7][:], ones128[:], phi[pj][:, 512:1024],
                                                 start=stj, stop=False)
                                for c0 in (0, 512):
                                    rs = slice(c0, c0 + 512)
                                    inst = nc.tensor.matmul(P[4 + c0 // 512][:], vth[:, kw],
                                                            phi[pj][:, rs], start=stj, stop=False)
                                tinc(inst, ts)
                                if j >= 1:
                                    emit_lo(j - 1)
                            emit_lo(15)
                            wait(nc.vector, ts)
                            nc.vector.tensor_copy(invv[0:1, 0:512], P[6][0:1, :])
                            nc.vector.tensor_copy(invv[0:1, 512:1024], P[7][0:1, :])
                            nc.vector.reciprocal(invv[:], invv[:])
                            pair(smb[0:1, :], smb2[0:1, :], invv[:])
                            vpub()
                            nc.tensor.wait_ge(vs.h, vs.n)
                            nc.tensor.wait_ge(ss.h, ss.n)
                            for c0 in (0, 512):
                                rs = slice(c0, c0 + 512)
                                nc.tensor.matmul(P[c0 // 512][:], ones128[:], smb[:, rs],
                                                 start=True, stop=False)
                                inst = nc.tensor.matmul(P[c0 // 512][:], ones128[:], smb2[:, rs],
                                                        start=False, stop=True)
                            tinc(inst, ts)
                            wait(nc.vector, ts)
                            nc.vector.tensor_copy(tmp1[:, 1024:1536], P[0][:])
                            nc.vector.tensor_copy(tmp1[:, 1536:2048], P[1][:])
                            nc.vector.tensor_mul(tmp1[:, 0:512], P[4][:], tmp1[:, 1024:1536])
                            nc.vector.tensor_mul(tmp1[:, 512:1024], P[5][:], tmp1[:, 1536:2048])
                            pair(onh[g * 4 + h][:], onl[g * 4 + h][:], tmp1[:, 0:1024])
                            vpub()
                            nc.tensor.wait_ge(vs.h, vs.n)
                    if STOP_AFTER <= 3 and g == 0:
                        finish_early([negmh[:, 0:4], k2ln[0:1, 0:32], invv[:],
                                      csQ[:, 0:4]])
                        return nc
        barrier()
        if STOP_AFTER <= 4:
            finish_early()
            return nc

        with ExitStack() as sc4:
            sb4 = lambda name, shape, dt=F32: SB(sc4, name, shape, dt)
            wos = [sb4("wos0", [128, 256]), sb4("wos1", [128, 256])]
            woh = [sb4("woh0", [128, 4096], BF16), sb4("woh1", [128, 4096], BF16)]
            wol = [sb4("wol0", [128, 4096], BF16), sb4("wol1", [128, 4096], BF16)]
            used_mk = [ts.n, ts.n]
            prep_mk = [vs.n, vs.n]
            pdone_mk = [vs.n, vs.n]
            xb_mk = [ds.n, ds.n]
            xbfree_mk = [vs.n, vs.n]
            evac_mk = [vs.n, vs.n]
            st_mk = gd.n

            def prep4(hc):
                b = hc % 2
                cw = hc * 256
                nc.vector.wait_ge(ts.h, used_mk[b])
                for kt in range(NDT):
                    s2 = kt % 2
                    nc.sync.wait_ge(vs.h, pdone_mk[s2])
                    dma(nc.sync, wos[s2][:], wo[kt * 128:(kt + 1) * 128, cw:cw + 256],
                        dw[s2])
                    nc.vector.wait_ge(dw[s2].h, dw[s2].n)
                    pair(woh[b][:, kt * 256:(kt + 1) * 256],
                         wol[b][:, kt * 256:(kt + 1) * 256], wos[s2][:])
                    vpub()
                    pdone_mk[s2] = vs.n
                prep_mk[b] = vs.n

            prep4(0)
            for hc in range(8):
                b = hc % 2
                if hc < 7:
                    prep4(hc + 1)
                for ot in range(2):
                    gidx = hc * 2 + ot
                    sl = gidx % 2
                    nc.sync.wait_ge(vs.h, xbfree_mk[sl])
                    dma(nc.sync, xbuf[sl][:, 0:T], xT[gidx * 128:(gidx + 1) * 128, 0:T],
                        dx[sl])
                    xb_mk[sl] = dx[sl].n
                for ot in range(2):
                    gidx = hc * 2 + ot
                    sl = gidx % 2
                    pb = 2 * (gidx % 2)
                    nc.tensor.wait_ge(vs.h, prep_mk[b])
                    nc.tensor.wait_ge(vs.h, evac_mk[gidx % 2])
                    for kt in range(NDT):
                        wsl = slice(kt * 256 + ot * 128, kt * 256 + (ot + 1) * 128)
                        st = kt == 0
                        sp = kt == NDT - 1
                        for c0 in (0, 512):
                            rs = slice(c0, c0 + 512)
                            tg = P[pb + c0 // 512][:]
                            nc.tensor.matmul(tg, woh[b][:, wsl], onh[kt][:, rs], start=st, stop=False)
                            nc.tensor.matmul(tg, woh[b][:, wsl], onl[kt][:, rs], start=False, stop=False)
                            inst = nc.tensor.matmul(tg, wol[b][:, wsl], onh[kt][:, rs],
                                                    start=False, stop=sp)
                    tinc(inst, ts)
                    used_mk[b] = ts.n
                    nc.vector.wait_ge(ts.h, ts.n)
                    nc.vector.wait_ge(dx[sl].h, xb_mk[sl])
                    nc.vector.wait_ge(gd.h, st_mk)
                    nc.vector.tensor_add(tmp1[:, 0:512], P[pb][:], xbuf[sl][:, 0:512])
                    nc.vector.tensor_add(tmp1[:, 512:1024], P[pb + 1][:], xbuf[sl][:, 512:1024])
                    vpub()
                    evac_mk[gidx % 2] = vs.n
                    xbfree_mk[sl] = vs.n
                    nc.gpsimd.wait_ge(vs.h, vs.n)
                    dma(nc.gpsimd, x1s[gidx * 128:(gidx + 1) * 128, :], tmp1[:, 0:T], gd)
                    st_mk = gd.n
            if STOP_AFTER <= 4.7:
                nc.vector.wait_ge(ts.h, ts.n)
                nc.vector.wait_ge(gd.h, gd.n)
                nc.vector.tensor_copy(tmp1[:, 0:1024], wol[1][:, 2048:3072])
                nc.vector.tensor_copy(tmp1[:, 1024:2048], wol[1][:, 3072:4096])
                nc.vector.tensor_copy(xbuf[0][:, 0:1024], woh[1][:, 2048:3072])
                nc.vector.tensor_copy(xbuf[0][:, 1024:2048], woh[1][:, 3072:4096])
                nc.vector.tensor_copy(xbuf[1][:, 0:1024], wol[1][:, 0:1024])
                nc.vector.tensor_copy(xbuf[1][:, 1024:2048], wol[1][:, 1024:2048])
                vpub()
                finish_early([tmp1[:, 1024:2048], xbuf[0][:, 0:1024],
                              xbuf[0][:, 1024:2048], xbuf[1][:, 0:1024],
                              xbuf[1][:, 1024:2048]])
                return nc
        barrier()
        if STOP_AFTER <= 5:
            finish_early()
            return nc
        if STOP_AFTER <= 5.5:
            dma(nc.gpsimd, y[:, :], x1s, gd)
            nc.gpsimd.wait_ge(gd.h, gd.n)
            return nc

        with ExitStack() as sc56:
            sb56 = lambda name, shape, dt=F32: SB(sc56, name, shape, dt)
            xnh = [sb56(f"xnh{i}", [128, T], BF16) for i in range(16)]
            xnl = [sb56(f"xnl{i}", [128, T], BF16) for i in range(16)]

            with ExitStack() as sc5:
                sb5 = lambda name, shape, dt=F32: SB(sc5, name, shape, dt)
                xh5 = sb5("xh5", [128, T], BF16)
                xl5 = sb5("xl5", [128, T], BF16)
                x2h5 = sb5("x2h5", [128, T], BF16)
                x2l5 = sb5("x2l5", [128, T], BF16)
                sm5 = sb5("sm5", [1, 2048])
                bcS = sb5("bcS", [128, T])
                bcB = sb5("bcB", [128, T])
                for dt in range(NDT):
                    sl = dt % 2
                    nc.sync.wait_ge(vs.h, vs.n)
                    nc.sync.wait_ge(ss.h, ss.n)
                    dma(nc.sync, xbuf[sl][:, 0:T], x1s[dt * 128:(dt + 1) * 128, :])
                    wait(nc.scalar, ds)
                    if dt >= 1:
                        nc.scalar.wait_ge(vs.h, vs.n)
                    nc.scalar.activation(tmp1[:, 0:T], xbuf[sl][:, 0:T], AF.Square)
                    spub()
                    wait(nc.vector, ds)
                    if dt >= 1:
                        nc.vector.wait_ge(ts.h, ts.n)
                    pair(xh5[:], xl5[:], xbuf[sl][:, 0:T])
                    wait(nc.vector, ss)
                    pair(x2h5[:], x2l5[:], tmp1[:, 0:T])
                    vpub()
                    wait(nc.tensor, vs)
                    st = dt == 0
                    sp = dt == NDT - 1
                    for (tgt, hh, ll) in ((P[0][:], xh5[:, 0:512], xl5[:, 0:512]),
                                          (P[1][:], xh5[:, 512:1024], xl5[:, 512:1024]),
                                          (P[2][:], x2h5[:, 0:512], x2l5[:, 0:512]),
                                          (P[3][:], x2h5[:, 512:1024], x2l5[:, 512:1024])):
                        nc.tensor.matmul(tgt, ones128[:], hh, start=st, stop=False)
                        inst = nc.tensor.matmul(tgt, ones128[:], ll, start=False, stop=sp)
                    tinc(inst, ts)
                wait(nc.vector, ts)
                m1 = sm5[0:1, 0:1024]
                vv = sm5[0:1, 1024:2048]
                nc.vector.tensor_copy(m1[0:1, 0:512], P[0][0:1, :])
                nc.vector.tensor_copy(m1[0:1, 512:1024], P[1][0:1, :])
                nc.vector.tensor_copy(vv[0:1, 0:512], P[2][0:1, :])
                nc.vector.tensor_copy(vv[0:1, 512:1024], P[3][0:1, :])
                nc.vector.tensor_scalar_mul(m1, m1, INV_D)
                nc.vector.tensor_scalar_mul(vv, vv, INV_D)
                tS = bcS[0:1, 0:1024]
                tB = bcB[0:1, 0:1024]
                nc.vector.tensor_mul(tS, m1, m1)
                nc.vector.tensor_sub(vv, vv, tS)
                nc.vector.tensor_scalar_add(tS, vv, EPS)
                nc.vector.reciprocal(tB, tS)
                nc.vector.tensor_mul(vv, vv, tB)
                nc.vector.tensor_scalar_add(vv, vv, EPS)
                vpub()
                wait(nc.scalar, vs)
                nc.scalar.activation(tS, tS, AF.Ln)
                nc.scalar.activation(tS, tS, AF.Exp, scale=-0.5)
                nc.scalar.activation(vv, vv, AF.Ln)
                nc.scalar.activation(vv, vv, AF.Exp, scale=-0.5)
                spub()
                wait(nc.vector, ss)
                nc.vector.tensor_mul(tS, tS, vv)
                nc.vector.tensor_mul(m1, m1, tS)
                nc.vector.tensor_scalar_mul(m1, m1, -1.0)
                vpub()
                for (srcv, pb0) in ((tS, 0), (m1, 2)):
                    nc.vector.wait_ge(ts.h, ts.n)
                    pair(smb[0:1, :], smb2[0:1, :], srcv)
                    vpub()
                    wait(nc.tensor, vs)
                    for c0 in (0, 512):
                        rs = slice(c0, c0 + 512)
                        nc.tensor.matmul(P[pb0 + c0 // 512][:], ones128[:], smb[:, rs],
                                         start=True, stop=False)
                        inst = nc.tensor.matmul(P[pb0 + c0 // 512][:], ones128[:], smb2[:, rs],
                                                start=False, stop=True)
                    tinc(inst, ts)
                wait(nc.vector, ts)
                nc.vector.tensor_copy(bcS[:, 0:512], P[0][:])
                nc.vector.tensor_copy(bcS[:, 512:1024], P[1][:])
                nc.vector.tensor_copy(bcB[:, 0:512], P[2][:])
                nc.vector.tensor_copy(bcB[:, 512:1024], P[3][:])
                vpub()
                for dt in range(NDT):
                    sl = dt % 2
                    nc.sync.wait_ge(vs.h, vs.n)
                    dma(nc.sync, xbuf[sl][:, 0:T], x1s[dt * 128:(dt + 1) * 128, :])
                    wait(nc.vector, ds)
                    nc.vector.tensor_mul(tmp1[:, 0:T], xbuf[sl][:, 0:T], bcS[:])
                    nc.vector.tensor_add(tmp1[:, 0:T], tmp1[:, 0:T], bcB[:])
                    nc.vector.tensor_scalar(tmp1[:, 0:T], tmp1[:, 0:T], gb[:, dt:dt + 1],
                                            gb[:, 16 + dt:17 + dt], op0=ALU.mult, op1=ALU.add)
                    pair(xnh[dt][:], xnl[dt][:], tmp1[:, 0:T])
                    vpub()
                barrier()
                if STOP_AFTER <= 6:
                    finish_early()
                    return nc

                with ExitStack() as sc6:
                    sb6 = lambda name, shape, dt=F32: SB(sc6, name, shape, dt)
                    wfb = [sb6("wfb0", [128, 512]), sb6("wfb1", [128, 512])]
                    wabs = sb6("wabs", [128, 1])
                    thrb = sb6("thrb", [128, 2])
                    tq = sb6("tq", [128, 512])
                    tq2 = sb6("tq2", [128, 512])
                    sc6w = top.enter_context(ExitStack())
                    wabh = SB(sc6w, "wabh", [128, 2048], BF16)
                    wabl = SB(sc6w, "wabl", [128, 2048], BF16)
                    for dt in range(NDT):
                        sl = dt % 2
                        nc.sync.wait_ge(vs.h, vs.n)
                        nc.sync.wait_ge(ss.h, ss.n)
                        dma(nc.sync, xbuf[sl][:], wfT[dt * 128:(dt + 1) * 128, :])
                        wait(nc.scalar, ds)
                        if dt >= 1:
                            nc.scalar.wait_ge(vs.h, vs.n)
                        nc.scalar.activation(tmp1[:], xbuf[sl][:], AF.Abs)
                        spub()
                        wait(nc.vector, ss)
                        if dt >= 1:
                            nc.vector.wait_ge(ts.h, ts.n)
                        pair(wabh[:], wabl[:], tmp1[:])
                        vpub()
                        wait(nc.tensor, vs)
                        for c in range(4):
                            cs = slice(c * 512, (c + 1) * 512)
                            nc.tensor.matmul(P[c][:], ones128[:], wabh[:, cs],
                                             start=(dt == 0), stop=False)
                            inst = nc.tensor.matmul(P[c][:], ones128[:], wabl[:, cs],
                                                    start=False, stop=(dt == NDT - 1))
                        tinc(inst, ts)
                    wait(nc.vector, ts)
                    for c in range(4):
                        nc.vector.tensor_copy(xbuf[1][0:1, c * 512:(c + 1) * 512], P[c][0:1, :])
                    nc.vector.reduce_sum(wabs[0:1, 0:1], xbuf[1][0:1, 0:2048], axis=AXX)
                    vpub()
                    nc.vector.wait_ge(vs.h, vs.n)
                    nc.vector.tensor_scalar(tq2[0:1, 0:1], wabs[0:1, 0:1],
                                            0.5 / (2048.0 * 2048.0), 0.5 * Q_EPS,
                                            op0=ALU.mult, op1=ALU.add)
                    vpub()
                    nc.vector.wait_ge(vs.h, vs.n)
                    nc.vector.tensor_scalar_mul(tq2[0:1, 1:2], tq2[0:1, 0:1], -1.0)
                    vpub()
                    nc.vector.wait_ge(vs.h, vs.n)
                    wait(nc.gpsimd, vs)
                    dma(nc.gpsimd, thrs.ap(), tq2[0:1, 0:2], gd)
                    nc.gpsimd.wait_ge(gd.h, gd.n)
                    dma(nc.gpsimd, thrb[:], thrs.ap().broadcast_to([128, 2]), gd)
                    wait(nc.vector, gd)
                    nc.vector.tensor_copy(tq2[0:1, 4:6], thrb[0:1, 0:2])
                    vpub()
                    nc.vector.wait_ge(vs.h, vs.n)
                    sc6w.close()
                    wqq = [sb6("wqq0", [128, 4096], BF16), sb6("wqq1", [128, 4096], BF16)]
                    if STOP_AFTER <= 6.5:
                        nc.vector.tensor_copy(tmp1[:, 0:2], thrb[:])
                        nc.vector.tensor_copy(tmp1[0:1, 2:3], wabs[0:1, 0:1])
                        vpub()
                        finish_early()
                        return nc

                    used6_mk = [ts.n, ts.n]
                    quant_mk = [vs.n, vs.n]
                    qdone_mk = [vs.n, vs.n]
                    xb6_mk = [dx[0].n, dx[1].n]
                    xb6free_mk = [vs.n, vs.n]
                    evac6_mk = [vs.n, vs.n]
                    st6_mk = gd.n

                    def prep6(hq):
                        b = hq % 2
                        cw = hq * 256
                        nc.vector.wait_ge(ts.h, used6_mk[b])
                        for dt in range(NDT):
                            s2 = dt % 2
                            nc.sync.wait_ge(vs.h, qdone_mk[s2])
                            dma(nc.sync, wfb[s2][:, 0:256],
                                wfT[dt * 128:(dt + 1) * 128, cw:cw + 256], dw[s2])
                            nc.vector.wait_ge(dw[s2].h, dw[s2].n)
                            nc.vector.tensor_single_scalar(tq[:, 0:256], wfb[s2][:, 0:256],
                                                           thrb[:, 0:1], ALU.is_ge)
                            nc.vector.tensor_single_scalar(tq2[:, 0:256], wfb[s2][:, 0:256],
                                                           thrb[:, 1:2], ALU.is_le)
                            nc.vector.tensor_tensor(wqq[b][:, dt * 256:(dt + 1) * 256],
                                                    tq[:, 0:256], tq2[:, 0:256], ALU.subtract)
                            vpub()
                            qdone_mk[s2] = vs.n
                        quant_mk[b] = vs.n

                    prep6(0)
                    for hq in range(8):
                        b = hq % 2
                        if hq < 7:
                            prep6(hq + 1)
                        for ot in range(2):
                            gidx = hq * 2 + ot
                            sl = gidx % 2
                            nc.sync.wait_ge(vs.h, xb6free_mk[sl])
                            dma(nc.sync, xbuf[sl][:, 0:T], x1s[gidx * 128:(gidx + 1) * 128, :],
                                dx[sl])
                            xb6_mk[sl] = dx[sl].n
                        for ot in range(2):
                            gidx = hq * 2 + ot
                            sl = gidx % 2
                            pb = 2 * (gidx % 2)
                            nc.tensor.wait_ge(vs.h, quant_mk[b])
                            nc.tensor.wait_ge(vs.h, evac6_mk[gidx % 2])
                            for dt in range(NDT):
                                wsl = slice(dt * 256 + ot * 128, dt * 256 + (ot + 1) * 128)
                                st = dt == 0
                                sp = dt == NDT - 1
                                for c0 in (0, 512):
                                    rs = slice(c0, c0 + 512)
                                    tg = P[pb + c0 // 512][:]
                                    nc.tensor.matmul(tg, wqq[b][:, wsl], xnh[dt][:, rs],
                                                     start=st, stop=False)
                                    inst = nc.tensor.matmul(tg, wqq[b][:, wsl], xnl[dt][:, rs],
                                                            start=False, stop=sp)
                            tinc(inst, ts)
                            used6_mk[b] = ts.n
                            nc.vector.wait_ge(ts.h, ts.n)
                            nc.vector.wait_ge(dx[sl].h, xb6_mk[sl])
                            nc.vector.wait_ge(gd.h, st6_mk)
                            for c0 in (0, 512):
                                nc.vector.scalar_tensor_tensor(tmp1[:, c0:c0 + 512],
                                                               xbuf[sl][:, c0:c0 + 512],
                                                               gb[:, 32 + gidx:33 + gidx],
                                                               P[pb + c0 // 512][:],
                                                               op0=ALU.add, op1=ALU.add)
                            vpub()
                            evac6_mk[gidx % 2] = vs.n
                            xb6free_mk[sl] = vs.n
                            nc.gpsimd.wait_ge(vs.h, vs.n)
                            dma(nc.gpsimd, y[gidx * 128:(gidx + 1) * 128, :], tmp1[:, 0:T], gd)
                            st6_mk = gd.n
                    nc.gpsimd.wait_ge(gd.h, gd.n)
                    nc.gpsimd.wait_ge(gd.h, gd.n)
    return nc


_CACHE = {}


def _tables():
    pos = np.arange(S, dtype=np.float64)
    inv_freq = 1.0 / (10000.0 ** (np.arange(0, HD, 2, dtype=np.float64) / HD))
    fr = np.outer(pos, inv_freq)
    cos64 = np.cos(fr).T
    sin64 = np.sin(fr).T
    cc = np.vstack([cos64, cos64])
    ssg = np.vstack([-sin64, sin64])
    qs = 1.0 / np.sqrt(128.0)

    def hl(a):
        h = a.astype(np.float32)
        l = (a - h.astype(np.float64)).astype(np.float32)
        return h, l

    return hl(cc), hl(ssg), hl(cc * qs), hl(ssg * qs)


def kernel(L0_attn_hidden_states, L0_attn_Wq, L0_attn_Wk, L0_attn_Wv, L0_attn_Wo,
           L0_ffn_weight, L0_ffn_bias, L0_ffn_ln_gamma, L0_ffn_ln_beta):
    x = np.asarray(L0_attn_hidden_states, dtype=np.float32)
    perm = np.empty(128, dtype=np.int64)
    perm[:64] = np.arange(64) * 2
    perm[64:] = np.arange(64) * 2 + 1
    permq = np.concatenate([h * 128 + perm for h in range(16)])
    permk = np.concatenate([h * 128 + perm for h in range(4)])
    wq_p = np.ascontiguousarray(np.asarray(L0_attn_Wq, np.float32)[:, permq])
    wk_p = np.ascontiguousarray(np.asarray(L0_attn_Wk, np.float32)[:, permk])
    wv = np.ascontiguousarray(np.asarray(L0_attn_Wv, np.float32))
    wo = np.ascontiguousarray(np.asarray(L0_attn_Wo, np.float32))
    wfT = np.ascontiguousarray(np.asarray(L0_ffn_weight, np.float32).T)
    gbb = np.zeros((128, 48), np.float32)
    gbb[:, 0:16] = np.asarray(L0_ffn_ln_gamma, np.float32).reshape(16, 128).T
    gbb[:, 16:32] = np.asarray(L0_ffn_ln_beta, np.float32).reshape(16, 128).T
    gbb[:, 32:48] = np.asarray(L0_ffn_bias, np.float32).reshape(16, 128).T
    eye = np.eye(128, dtype=np.float32)

    (cch, ccl), (ssh, ssl), (qcch, qccl), (qssh, qssl) = _tables()

    if "nc" not in _CACHE:
        _CACHE["nc"] = build_nc()
    nc = _CACHE["nc"]

    in_maps = []
    for c in range(8):
        b, half = c // 2, c % 2
        order = np.r_[half * 1024:half * 1024 + 1024,
                      (1 - half) * 1024:(1 - half) * 1024 + 1024]
        in_maps.append({
            "xT": np.ascontiguousarray(x[b].T[:, order]),
            "wq": wq_p, "wk": wk_p, "wv": wv, "wo": wo, "wfT": wfT,
            "gbb": gbb, "eye": eye,
            "kcch": np.ascontiguousarray(cch[:, order]),
            "kccl": np.ascontiguousarray(ccl[:, order]),
            "kssh": np.ascontiguousarray(ssh[:, order]),
            "kssl": np.ascontiguousarray(ssl[:, order]),
            "qcch": np.ascontiguousarray(qcch[:, order[:1024]]),
            "qccl": np.ascontiguousarray(qccl[:, order[:1024]]),
            "qssh": np.ascontiguousarray(qssh[:, order[:1024]]),
            "qssl": np.ascontiguousarray(qssl[:, order[:1024]]),
        })
    res = run_bass_kernel_spmd(nc, in_maps, core_ids=list(range(8)))
    out = np.empty((4, 2048, 2048), np.float32)
    for c in range(8):
        b, half = c // 2, c % 2
        out[b, half * 1024:(half + 1) * 1024, :] = res.results[c]["y"].T
    return out


# version 14
# speedup vs baseline: 1.0937x; 1.0937x over previous
import numpy as np
from contextlib import ExitStack

import concourse.bass as bass
import concourse.mybir as mybir
from concourse.bass_utils import run_bass_kernel_spmd

F32 = mybir.dt.float32
F32R = mybir.dt.float32r
BF16 = mybir.dt.bfloat16
AF = mybir.ActivationFunctionType
ALU = mybir.AluOpType
AXX = mybir.AxisListType.X

D = 2048
S = 2048
T = 1024
NDT = 16
HD = 128
INV_D = 1.0 / 2048.0
EPS = 1e-5
Q_EPS = 1e-5


STOP_AFTER = 99


class Sem:
    __slots__ = ("h", "n")

    def __init__(self, h):
        self.h = h
        self.n = 0


def build_nc():
    nc = bass.Bass("TRN2", target_bir_lowering=False, debug=False,
                   detect_race_conditions=False)

    din = lambda name, shape: nc.dram_tensor(name, shape, F32, kind="ExternalInput").ap()
    xT = din("xT", [D, S])
    wq = din("wq", [D, D])
    wk = din("wk", [D, 512])
    wv = din("wv", [D, 512])
    wo = din("wo", [D, D])
    wfT = din("wfT", [D, D])
    gbb = din("gbb", [128, 48])
    kcch = din("kcch", [128, S])
    kccl = din("kccl", [128, S])
    kssh = din("kssh", [128, S])
    kssl = din("kssl", [128, S])
    qcch = din("qcch", [128, T])
    qccl = din("qccl", [128, T])
    qssh = din("qssh", [128, T])
    qssl = din("qssl", [128, T])
    eye = din("eye", [128, 128])
    y = nc.dram_tensor("y", [D, T], F32, kind="ExternalOutput").ap()
    x1s = nc.dram_tensor("x1s", [D, T], F32).ap()
    thrs = nc.dram_tensor("thrs", [1, 2], F32)

    with ExitStack() as top:
        _uid = [0]

        def SB(st_, name, shape, dt=F32):
            _uid[0] += 1
            return st_.enter_context(nc.sbuf_tensor(f"{name}_{_uid[0]}", shape, dt))

        sb = lambda name, shape, dt=F32: SB(top, name, shape, dt)

        P = [top.enter_context(nc.psum_tensor(f"ps{i}", [128, 512], F32))
             for i in range(8)]

        ds = Sem(top.enter_context(nc.semaphore("ds")))
        gd = Sem(top.enter_context(nc.semaphore("gd")))
        ts = Sem(top.enter_context(nc.semaphore("ts")))
        vs = Sem(top.enter_context(nc.semaphore("vs")))
        ss = Sem(top.enter_context(nc.semaphore("ss")))
        gs = Sem(top.enter_context(nc.semaphore("gs")))
        dw = [Sem(top.enter_context(nc.semaphore("dw0"))),
              Sem(top.enter_context(nc.semaphore("dw1")))]
        dx = [Sem(top.enter_context(nc.semaphore("dx0"))),
              Sem(top.enter_context(nc.semaphore("dx1")))]
        bar = Sem(top.enter_context(nc.semaphore("bar")))

        eye_f = sb("eye_f", [128, 128])
        ones_c_bf = sb("ones_c_bf", [128, 1], BF16)
        ones128 = sb("ones128", [128, 128], BF16)
        gb = sb("gb", [128, 48])
        smb = sb("smb", [128, 1024], BF16)
        smb2 = sb("smb2", [128, 1024], BF16)
        pub1 = sb("pub1", [1, 8])
        pub2 = sb("pub2", [1, 8])
        pub3 = sb("pub3", [1, 8])
        pub4 = sb("pub4", [1, 8])
        xbuf = [sb("xbuf0", [128, S]), sb("xbuf1", [128, S])]
        tmp1 = sb("tmp1", [128, S])
        onh = [sb(f"onh{i}", [128, T], BF16) for i in range(16)]
        onl = [sb(f"onl{i}", [128, T], BF16) for i in range(16)]

        top.enter_context(nc.Block())

        ENG = (nc.tensor, nc.vector, nc.scalar, nc.sync, nc.gpsimd)

        def dma(q, out, in_, sem=ds):
            q.dma_start(out=out, in_=in_).then_inc(sem.h, 16)
            sem.n += 16

        def tinc(inst, sem):
            inst.then_inc(sem.h, 1)
            sem.n += 1

        def wait(eng, *sems):
            for s_ in sems:
                eng.wait_ge(s_.h, s_.n)

        def pair(dst_h, dst_l, src):
            nc.vector.tensor_copy(dst_h, src)
            nc.vector.tensor_tensor(dst_l, src, dst_h, ALU.subtract)

        def vpub():
            tinc(nc.vector.tensor_copy(pub1[:], pub2[:]), vs)

        def spub():
            tinc(nc.scalar.copy(pub3[:], pub2[:]), ss)

        def gpub():
            tinc(nc.gpsimd.tensor_copy(pub4[:], pub2[:]), gs)

        def barrier():
            nc.sync.wait_ge(ds.h, ds.n)
            nc.gpsimd.wait_ge(gd.h, gd.n)
            for eng in ENG:
                for s_ in (ds, gd, ts, vs, ss, gs, dw[0], dw[1], dx[0], dx[1]):
                    eng.wait_ge(s_.h, s_.n)

        def finish_early(extra=()):
            barrier()
            dma(nc.gpsimd, y[0:128, :], tmp1[:, 0:T], gd)
            for i, ap_ in enumerate(extra):
                dma(nc.gpsimd, y[(i + 1) * 128:(i + 1) * 128 + ap_.shape[0],
                                 0:ap_.shape[1]], ap_, gd)
            nc.gpsimd.wait_ge(gd.h, gd.n)

        dma(nc.sync, eye_f[:], eye)
        dma(nc.sync, gb[:], gbb)
        wait(nc.vector, ds)
        nc.vector.tensor_scalar_mul(pub1[:], eye_f[0:1, 0:8], 0.0)
        nc.vector.tensor_scalar_mul(pub2[:], eye_f[0:1, 0:8], 0.0)
        nc.vector.tensor_scalar(ones_c_bf[:], eye_f[:, 0:1], 0.0, 1.0,
                                op0=ALU.mult, op1=ALU.add)
        nc.vector.tensor_scalar(ones128[:], eye_f[:], 0.0, 1.0,
                                op0=ALU.mult, op1=ALU.add)
        for c in range(8):
            nc.vector.tensor_scalar_mul(smb[:, c * 128:(c + 1) * 128], eye_f[:], 0.0)
            nc.vector.tensor_scalar_mul(smb2[:, c * 128:(c + 1) * 128], eye_f[:], 0.0)
        vpub()
        if STOP_AFTER <= 0:
            finish_early()
            return nc

        with ExitStack() as scA:
            sbA = lambda name, shape, dt=F32: SB(scA, name, shape, dt)
            bc_s1 = sbA("bc_s1", [128, S])
            bc_negm = sbA("bc_negm", [128, S])
            ck_cc = sbA("ck_cc", [128, S])
            ck_ss = sbA("ck_ss", [128, S])
            cq_cc = sbA("cq_cc", [128, T])
            cq_ss = sbA("cq_ss", [128, T])
            tmp2 = sbA("tmp2", [128, S])
            tmpr = sbA("tmpr", [128, T], BF16)

            with ExitStack() as sc1:
                sb1 = lambda name, shape, dt=F32: SB(sc1, name, shape, dt)
                xh1 = sb1("xh1", [128, S], BF16)
                xl1 = sb1("xl1", [128, S], BF16)
                x2h1 = sb1("x2h1", [128, S], BF16)
                x2l1 = sb1("x2l1", [128, S], BF16)
                sm = sb1("sm", [1, 2048])
                sm2 = sb1("sm2", [1, 2048])
                st_s1 = sb1("st_s1", [1, 2048])
                sumx_tgt = [P[0][0:1, :], P[1][0:1, :], P[2][0:1, :], P[3][0:1, :]]
                sumx2_tgt = [P[4][0:1, :], P[5][0:1, :], P[6][0:1, :], P[7][0:1, :]]
                for dt in range(NDT):
                    sl = dt % 2
                    nc.sync.wait_ge(vs.h, vs.n)
                    nc.sync.wait_ge(ss.h, ss.n)
                    dma(nc.sync, xbuf[sl][:], xT[dt * 128:(dt + 1) * 128, :])
                    wait(nc.scalar, ds)
                    if dt >= 1:
                        nc.scalar.wait_ge(vs.h, vs.n)
                    nc.scalar.activation(tmp2[:], xbuf[sl][:], AF.Square)
                    spub()
                    wait(nc.vector, ds)
                    if dt >= 1:
                        nc.vector.wait_ge(ts.h, ts.n)
                    pair(xh1[:], xl1[:], xbuf[sl][:])
                    wait(nc.vector, ss)
                    pair(x2h1[:], x2l1[:], tmp2[:])
                    vpub()
                    wait(nc.tensor, vs)
                    st = dt == 0
                    sp = dt == NDT - 1
                    for c in range(4):
                        cs = slice(c * 512, (c + 1) * 512)
                        nc.tensor.matmul(P[c][:], ones128[:], xh1[:, cs], start=st, stop=False)
                        nc.tensor.matmul(P[c][:], ones128[:], xl1[:, cs], start=False, stop=sp)
                        nc.tensor.matmul(P[4 + c][:], ones128[:], x2h1[:, cs], start=st, stop=False)
                        inst = nc.tensor.matmul(P[4 + c][:], ones128[:], x2l1[:, cs], start=False, stop=sp)
                    tinc(inst, ts)
                if STOP_AFTER <= 0.3:
                    finish_early()
                    return nc
                wait(nc.vector, ts)
                for c in range(4):
                    nc.vector.tensor_copy(sm[0:1, c * 512:(c + 1) * 512], sumx_tgt[c])
                    nc.vector.tensor_copy(sm2[0:1, c * 512:(c + 1) * 512], sumx2_tgt[c])
                nc.vector.tensor_scalar_mul(sm[:], sm[:], INV_D)
                nc.vector.tensor_scalar_mul(sm2[:], sm2[:], INV_D)
                nc.vector.tensor_mul(st_s1[:], sm[:], sm[:])
                nc.vector.tensor_sub(sm2[:], sm2[:], st_s1[:])
                nc.vector.tensor_scalar_add(sm2[:], sm2[:], EPS)
                vpub()
                wait(nc.scalar, vs)
                nc.scalar.activation(st_s1[:], sm2[:], AF.Ln)
                nc.scalar.activation(st_s1[:], st_s1[:], AF.Exp, scale=-0.5)
                spub()
                wait(nc.vector, ss)
                nc.vector.tensor_scalar_mul(sm[:], sm[:], -1.0)
                vpub()
                if STOP_AFTER <= 0.6:
                    finish_early()
                    return nc

                for (srcv, dstt) in ((st_s1, bc_s1), (sm, bc_negm)):
                    for half in range(2):
                        hs = slice(half * 1024, (half + 1) * 1024)
                        pair(smb[0:1, :], smb2[0:1, :], srcv[0:1, hs])
                        vpub()
                        wait(nc.tensor, vs)
                        for c in range(2):
                            s0 = slice(c * 512, (c + 1) * 512)
                            nc.tensor.matmul(P[c][:], ones128[:], smb[:, s0], start=True, stop=False)
                            inst = nc.tensor.matmul(P[c][:], ones128[:], smb2[:, s0], start=False, stop=True)
                        tinc(inst, ts)
                        wait(nc.vector, ts)
                        for c in range(2):
                            nc.vector.tensor_copy(
                                dstt[:, half * 1024 + c * 512:half * 1024 + (c + 1) * 512],
                                P[c][:])
                        vpub()

                if STOP_AFTER <= 0.8:
                    finish_early([bc_s1[:, 0:1024], bc_negm[:, 0:1024],
                                  smb[:, 0:512].bitcast(F32), st_s1[0:1, 0:1024]])
                    return nc
                for (hi_t, lo_t, dst, w, scl) in (
                        (kcch, kccl, ck_cc, S, bc_s1), (kssh, kssl, ck_ss, S, bc_s1),
                        (qcch, qccl, cq_cc, T, bc_s1), (qssh, qssl, cq_ss, T, bc_s1)):
                    nc.sync.wait_ge(vs.h, vs.n)
                    dma(nc.sync, tmp1[:, 0:w], hi_t)
                    dma(nc.sync, tmp2[:, 0:w], lo_t)
                    wait(nc.vector, ds)
                    nc.vector.tensor_add(tmp1[:, 0:w], tmp1[:, 0:w], tmp2[:, 0:w])
                    nc.vector.tensor_mul(dst[:], tmp1[:, 0:w], scl[:, 0:w])
                    vpub()
            barrier()
            if STOP_AFTER <= 1:
                finish_early([ck_cc[:, 0:1024], ck_ss[:, 0:1024],
                              cq_cc[:, 0:1024], cq_ss[:, 0:1024]])
                return nc

            with ExitStack() as sc2:
                sb2 = lambda name, shape, dt=F32: SB(sc2, name, shape, dt)
                kresh = sb2("kresh", [128, S], BF16)
                kresl = sb2("kresl", [128, S], BF16)
                vth = sb2("vth", [128, S], BF16)
                vtl = sb2("vtl", [128, S], BF16)
                phi = [sb2("phi0", [128, T], BF16), sb2("phi1", [128, T], BF16)]
                plo = [sb2("plo0", [128, T], BF16), sb2("plo1", [128, T], BF16)]
                pfull = [sb2("pf0", [128, T]), sb2("pf1", [128, T])]
                invv = sb2("invv", [1, 1024])
                csKV = sb2("csKV", [128, 2])
                csQ = sb2("csQ", [128, 4])
                negmh = sb2("negmh", [128, 4])
                k2ln = sb2("k2ln", [1, 32])

                for g in range(4):
                    with ExitStack() as sckv:
                        sbk = lambda name, shape, dt=F32: SB(sckv, name, shape, dt)
                        wkh = sbk("wkh", [128, 2048], BF16)
                        wkl = sbk("wkl", [128, 2048], BF16)
                        wvh = sbk("wvh", [128, 2048], BF16)
                        wvl = sbk("wvl", [128, 2048], BF16)
                        xph = sbk("xph", [128, 2048], BF16)
                        xpl = sbk("xpl", [128, 2048], BF16)
                        nc.sync.wait_ge(ts.h, ts.n)
                        for dt in range(NDT):
                            dma(nc.sync, xbuf[0][:, dt * 128:(dt + 1) * 128],
                                wk[dt * 128:(dt + 1) * 128, g * 128:(g + 1) * 128])
                            dma(nc.sync, xbuf[1][:, dt * 128:(dt + 1) * 128],
                                wv[dt * 128:(dt + 1) * 128, g * 128:(g + 1) * 128])
                        wait(nc.vector, ds)
                        pair(wkh[:], wkl[:], xbuf[0][:])
                        pair(wvh[:], wvl[:], xbuf[1][:])
                        vpub()
                        wait(nc.tensor, vs)
                        for dt in range(NDT):
                            st = dt == 0
                            sp = dt == NDT - 1
                            w0 = slice(dt * 128, (dt + 1) * 128)
                            nc.tensor.matmul(P[6][:, 0:1], wkh[:, w0], ones_c_bf[:], start=st, stop=False)
                            nc.tensor.matmul(P[6][:, 0:1], wkl[:, w0], ones_c_bf[:], start=False, stop=sp)
                            nc.tensor.matmul(P[7][:, 0:1], wvh[:, w0], ones_c_bf[:], start=st, stop=False)
                            inst = nc.tensor.matmul(P[7][:, 0:1], wvl[:, w0], ones_c_bf[:], start=False, stop=sp)
                        tinc(inst, ts)
                        wait(nc.vector, ts)
                        nc.vector.tensor_copy(csKV[:, 0:1], P[6][:, 0:1])
                        nc.vector.tensor_copy(csKV[:, 1:2], P[7][:, 0:1])
                        vpub()

                        kt_tgt = [P[0][:], P[1][:], P[2][:], P[3][:]]
                        vt_tgt = [P[4][:], P[5][:], P[6][:], P[7][:]]
                        slot_pair = [vs.n, vs.n]
                        slot_mm = [ts.n, ts.n]
                        for dt in range(NDT):
                            sl = dt % 2
                            nc.sync.wait_ge(vs.h, slot_pair[sl])
                            dma(nc.sync, xbuf[sl][:], xT[dt * 128:(dt + 1) * 128, :])
                            wait(nc.vector, ds)
                            nc.vector.wait_ge(ts.h, ts.n)
                            pair(xph[:], xpl[:], xbuf[sl][:])
                            vpub()
                            slot_pair[sl] = vs.n
                            wait(nc.tensor, vs)
                            st = dt == 0
                            sp = dt == NDT - 1
                            w0 = slice(dt * 128, (dt + 1) * 128)
                            for c in range(4):
                                cs = slice(c * 512, (c + 1) * 512)
                                nc.tensor.matmul(kt_tgt[c], wkh[:, w0], xph[:, cs], start=st, stop=False)
                                nc.tensor.matmul(kt_tgt[c], wkh[:, w0], xpl[:, cs], start=False, stop=False)
                                nc.tensor.matmul(kt_tgt[c], wkl[:, w0], xph[:, cs], start=False, stop=sp)
                                nc.tensor.matmul(vt_tgt[c], wvh[:, w0], xph[:, cs], start=st, stop=False)
                                nc.tensor.matmul(vt_tgt[c], wvh[:, w0], xpl[:, cs], start=False, stop=False)
                                inst = nc.tensor.matmul(vt_tgt[c], wvl[:, w0], xph[:, cs], start=False, stop=sp)
                            tinc(inst, ts)
                            slot_mm[sl] = ts.n

                        wait(nc.vector, ts)
                        for c in range(4):
                            cs = slice(c * 512, (c + 1) * 512)
                            nc.vector.scalar_tensor_tensor(tmp1[:, cs], bc_negm[:, cs], csKV[:, 0:1],
                                                           kt_tgt[c], op0=ALU.mult, op1=ALU.add)
                        vpub()
                        wait(nc.gpsimd, vs)
                        dma(nc.gpsimd, tmp2[0:64, :], tmp1[64:128, :], gd)
                        dma(nc.gpsimd, tmp2[64:128, :], tmp1[0:64, :], gd)
                        wait(nc.vector, gd)
                        nc.vector.tensor_mul(tmp1[:], tmp1[:], ck_cc[:])
                        nc.vector.tensor_mul(tmp2[:], tmp2[:], ck_ss[:])
                        nc.vector.tensor_add(tmp1[:], tmp1[:], tmp2[:])
                        pair(kresh[:], kresl[:], tmp1[:])
                        vpub()
                        wait(nc.scalar, vs)
                        for half in range(2):
                            hs = slice(half * 1024, (half + 1) * 1024)
                            if half == 1:
                                nc.scalar.wait_ge(ts.h, ts.n)
                            nc.scalar.activation(tmpr[:], tmp1[:, hs], AF.Square)
                            spub()
                            wait(nc.tensor, ss)
                            tg = [kt_tgt[2 * half], kt_tgt[2 * half + 1]]
                            nc.tensor.matmul(tg[0], ones128[:], tmpr[:, 0:512], start=True, stop=True)
                            inst = nc.tensor.matmul(tg[1], ones128[:], tmpr[:, 512:1024],
                                                    start=True, stop=True)
                            tinc(inst, ts)
                        wait(nc.vector, ts)
                        for c in range(4):
                            nc.vector.tensor_copy(xbuf[1][0:1, c * 512:(c + 1) * 512],
                                                  kt_tgt[c][0:1, :])
                        nc.vector.reduce_max(k2ln[0:1, 0:1], xbuf[1][0:1, 0:2048], axis=AXX)
                        vpub()
                        wait(nc.scalar, vs)
                        nc.scalar.activation(k2ln[0:1, 1:2], k2ln[0:1, 0:1], AF.Ln)
                        spub()

                        wait(nc.vector, ts)
                        vn = xbuf[0]
                        for c in range(4):
                            cs = slice(c * 512, (c + 1) * 512)
                            nc.vector.scalar_tensor_tensor(vn[:, cs], bc_negm[:, cs], csKV[:, 1:2],
                                                           vt_tgt[c], op0=ALU.mult, op1=ALU.add)
                        nc.vector.tensor_mul(vn[:], vn[:], bc_s1[:])
                        vpub()
                        wait(nc.tensor, vs)
                        for k in range(16):
                            tg = P[6] if k % 2 == 0 else P[7]
                            if k >= 2:
                                nc.tensor.wait_ge(vs.h, vs.n)
                            inst = nc.tensor.transpose(tg[:, 0:128], vn[:, k * 128:(k + 1) * 128], eye_f[:])
                            tinc(inst, ts)
                            wait(nc.vector, ts)
                            ksl = slice(k * 128, (k + 1) * 128)
                            nc.vector.tensor_copy(vth[:, ksl], tg[:, 0:128])
                            nc.vector.tensor_tensor(vtl[:, ksl], tg[:, 0:128], vth[:, ksl], ALU.subtract)
                            vpub()

                    if STOP_AFTER <= 2 and g == 0:
                        finish_early([csKV[:, 0:2], k2ln[0:1, 0:32],
                                      xbuf[0][:, 0:1024]])
                        return nc
                    with ExitStack() as scq:
                        sbq = lambda name, shape, dt=F32: SB(scq, name, shape, dt)
                        wqs = [sbq("wqs0", [128, 512]), sbq("wqs1", [128, 512])]
                        wqph = [sbq("wqph0", [128, 512], BF16), sbq("wqph1", [128, 512], BF16)]
                        wqpl = [sbq("wqpl0", [128, 512], BF16), sbq("wqpl1", [128, 512], BF16)]
                        xoph = sbq("xoph", [128, T], BF16)
                        xopl = sbq("xopl", [128, T], BF16)
                        qhh = [sbq(f"qhh{i}", [128, T], BF16) for i in range(4)]
                        qll = [sbq(f"qll{i}", [128, T], BF16) for i in range(4)]

                        cs_tgt = [P[0][:, 0:1], P[2][:, 0:1], P[4][:, 0:1], P[6][:, 0:1]]
                        slot_pair = [vs.n, vs.n]
                        slot_mm = [ts.n, ts.n]
                        for dt in range(NDT):
                            sl = dt % 2
                            nc.sync.wait_ge(vs.h, slot_pair[sl])
                            dma(nc.sync, wqs[sl][:], wq[dt * 128:(dt + 1) * 128, g * 512:(g + 1) * 512])
                            wait(nc.vector, ds)
                            nc.vector.wait_ge(ts.h, slot_mm[sl])
                            pair(wqph[sl][:], wqpl[sl][:], wqs[sl][:])
                            vpub()
                            slot_pair[sl] = vs.n
                            wait(nc.tensor, vs)
                            st = dt == 0
                            sp = dt == NDT - 1
                            for h in range(4):
                                hsl = slice(h * 128, (h + 1) * 128)
                                nc.tensor.matmul(cs_tgt[h], wqph[sl][:, hsl], ones_c_bf[:], start=st, stop=False)
                                inst = nc.tensor.matmul(cs_tgt[h], wqpl[sl][:, hsl], ones_c_bf[:],
                                                        start=False, stop=sp)
                            tinc(inst, ts)
                            slot_mm[sl] = ts.n
                        wait(nc.vector, ts)
                        for h in range(4):
                            nc.vector.tensor_copy(csQ[:, h:h + 1], cs_tgt[h])
                        vpub()

                        slot_pair = [vs.n, vs.n]
                        slot_mm = [ts.n, ts.n]
                        for dt in range(NDT):
                            sl = dt % 2
                            nc.sync.wait_ge(vs.h, slot_pair[sl])
                            dma(nc.sync, wqs[sl][:], wq[dt * 128:(dt + 1) * 128, g * 512:(g + 1) * 512])
                            dma(nc.sync, xbuf[sl][:, 0:T], xT[dt * 128:(dt + 1) * 128, 0:T])
                            wait(nc.vector, ds)
                            nc.vector.wait_ge(ts.h, ts.n)
                            pair(wqph[sl][:], wqpl[sl][:], wqs[sl][:])
                            pair(xoph[:], xopl[:], xbuf[sl][:, 0:T])
                            vpub()
                            slot_pair[sl] = vs.n
                            wait(nc.tensor, vs)
                            st = dt == 0
                            sp = dt == NDT - 1
                            for h in range(4):
                                hsl = slice(h * 128, (h + 1) * 128)
                                for c0 in (0, 512):
                                    rs = slice(c0, c0 + 512)
                                    tgt = P[2 * h + c0 // 512][:]
                                    nc.tensor.matmul(tgt, wqph[sl][:, hsl], xoph[:, rs], start=st, stop=False)
                                    nc.tensor.matmul(tgt, wqph[sl][:, hsl], xopl[:, rs], start=False, stop=False)
                                    inst = nc.tensor.matmul(tgt, wqpl[sl][:, hsl], xoph[:, rs],
                                                            start=False, stop=sp)
                            tinc(inst, ts)
                            slot_mm[sl] = ts.n

                        wait(nc.vector, ts)
                        for h in range(4):
                            segs = [(P[2 * h][:], 0, 512), (P[2 * h + 1][:], 512, 1024)]
                            own = [P[2 * h][0:1, :], P[2 * h + 1][0:1, :]]
                            for (ap_, a, b) in segs:
                                nc.vector.scalar_tensor_tensor(tmp1[:, a:b], bc_negm[:, a:b],
                                                               csQ[:, h:h + 1], ap_,
                                                               op0=ALU.mult, op1=ALU.add)
                            vpub()
                            wait(nc.gpsimd, vs)
                            dma(nc.gpsimd, tmp2[0:64, 0:T], tmp1[64:128, 0:T], gd)
                            dma(nc.gpsimd, tmp2[64:128, 0:T], tmp1[0:64, 0:T], gd)
                            wait(nc.vector, gd)
                            nc.vector.tensor_mul(tmp1[:, 0:T], tmp1[:, 0:T], cq_cc[:])
                            nc.vector.tensor_mul(tmp2[:, 0:T], tmp2[:, 0:T], cq_ss[:])
                            nc.vector.tensor_add(tmp1[:, 0:T], tmp1[:, 0:T], tmp2[:, 0:T])
                            pair(qhh[h][:], qll[h][:], tmp1[:, 0:T])
                            vpub()
                            wait(nc.scalar, vs)
                            nc.scalar.activation(tmpr[:], tmp1[:, 0:T], AF.Square)
                            spub()
                            wait(nc.tensor, ss, vs)
                            nc.tensor.matmul(P[2 * h][:], ones128[:], tmpr[:, 0:512], start=True, stop=True)
                            inst = nc.tensor.matmul(P[2 * h + 1][:], ones128[:], tmpr[:, 512:1024],
                                                    start=True, stop=True)
                            tinc(inst, ts)
                            wait(nc.vector, ts)
                            nc.vector.tensor_copy(invv[0:1, 0:512], own[0])
                            nc.vector.tensor_copy(invv[0:1, 512:1024], own[1])
                            nc.vector.reduce_max(k2ln[0:1, 4 + h:5 + h], invv[:], axis=AXX)
                            vpub()
                            if STOP_AFTER <= 2.5 and g == 0 and h == 0:
                                nc.vector.tensor_copy(xbuf[1][:, 0:1024], tmpr[:])
                                nc.vector.tensor_copy(xbuf[1][:, 1024:2048], qhh[0][:])
                                nc.vector.tensor_copy(xbuf[0][0:1, 0:512], own[0])
                                vpub()
                                finish_early([xbuf[1][:, 0:1024],
                                              xbuf[1][:, 1024:2048],
                                              k2ln[0:1, 0:32],
                                              xbuf[0][0:1, 0:512]])
                                return nc

                        wait(nc.scalar, vs)
                        nc.scalar.activation(k2ln[0:1, 8:12], k2ln[0:1, 4:8], AF.Ln)
                        spub()
                        wait(nc.vector, ss)
                        nc.vector.tensor_single_scalar(k2ln[0:1, 8:12], k2ln[0:1, 8:12],
                                                       k2ln[0:1, 1:2], ALU.add)
                        vpub()
                        wait(nc.scalar, vs)
                        nc.scalar.activation(k2ln[0:1, 8:12], k2ln[0:1, 8:12], AF.Exp, scale=0.5)
                        spub()
                        wait(nc.vector, ss)
                        nc.vector.tensor_scalar_mul(smb[0:1, 0:4], k2ln[0:1, 8:12], -1.0)
                        vpub()
                        wait(nc.tensor, vs)
                        tinc(nc.tensor.matmul(P[6][:, 0:4], ones128[:], smb[:, 0:4],
                                              start=True, stop=True), ts)
                        wait(nc.vector, ts)
                        nc.vector.tensor_copy(negmh[:], P[6][:, 0:4])
                        vpub()
                        nc.tensor.wait_ge(vs.h, vs.n)

                        for h in range(4):
                            exp_mark = [ss.n, ss.n]
                            lo_mark = [vs.n, vs.n]
                            lo_used = [ts.n, ts.n]

                            def emit_lo(jj):
                                kwj = slice(jj * 128, (jj + 1) * 128)
                                pjj = jj % 2
                                nc.tensor.wait_ge(vs.h, lo_mark[pjj])
                                nc.tensor.matmul(P[6][:], ones128[:], plo[pjj][:, 0:512],
                                                 start=False, stop=(jj == 15))
                                nc.tensor.matmul(P[7][:], ones128[:], plo[pjj][:, 512:1024],
                                                 start=False, stop=(jj == 15))
                                for c0 in (0, 512):
                                    rs = slice(c0, c0 + 512)
                                    nc.tensor.matmul(P[4 + c0 // 512][:], vth[:, kwj],
                                                     plo[pjj][:, rs], start=False, stop=False)
                                    inst = nc.tensor.matmul(P[4 + c0 // 512][:], vtl[:, kwj],
                                                            phi[pjj][:, rs], start=False,
                                                            stop=(jj == 15))
                                tinc(inst, ts)
                                lo_used[pjj] = ts.n

                            sc_mark = [ts.n, ts.n]

                            def emit_scores(jj):
                                pjj = jj % 2
                                kwj = slice(jj * 128, (jj + 1) * 128)
                                nc.tensor.wait_ge(ss.h, exp_mark[pjj])
                                for c0 in (0, 512):
                                    rs = slice(c0, c0 + 512)
                                    tg = P[2 * pjj + c0 // 512][:]
                                    nc.tensor.matmul(tg, kresh[:, kwj], qhh[h][:, rs],
                                                     start=True, stop=False)
                                    nc.tensor.matmul(tg, kresh[:, kwj], qll[h][:, rs],
                                                     start=False, stop=False)
                                    inst = nc.tensor.matmul(tg, kresl[:, kwj], qhh[h][:, rs],
                                                            start=False, stop=True)
                                tinc(inst, ts)
                                sc_mark[pjj] = ts.n

                            emit_scores(0)
                            for j in range(16):
                                pj = j % 2
                                kw = slice(j * 128, (j + 1) * 128)
                                if j + 1 < 16:
                                    emit_scores(j + 1)
                                nc.scalar.wait_ge(ts.h, sc_mark[pj])
                                nc.scalar.wait_ge(ts.h, lo_used[pj])
                                for c0 in (0, 512):
                                    rs = slice(c0, c0 + 512)
                                    nc.scalar.activation(phi[pj][:, rs], P[2 * pj + c0 // 512][:],
                                                         AF.Exp, bias=negmh[:, h:h + 1])
                                    nc.scalar.activation(pfull[pj][:, rs], P[2 * pj + c0 // 512][:],
                                                         AF.Exp, bias=negmh[:, h:h + 1])
                                spub()
                                exp_mark[pj] = ss.n
                                nc.vector.wait_ge(ss.h, ss.n)
                                nc.vector.wait_ge(ts.h, lo_used[pj])
                                nc.vector.tensor_tensor(plo[pj][:], pfull[pj][:], phi[pj][:],
                                                        ALU.subtract)
                                vpub()
                                lo_mark[pj] = vs.n
                                nc.tensor.wait_ge(ss.h, exp_mark[pj])
                                stj = j == 0
                                nc.tensor.matmul(P[6][:], ones128[:], phi[pj][:, 0:512],
                                                 start=stj, stop=False)
                                nc.tensor.matmul(P[7][:], ones128[:], phi[pj][:, 512:1024],
                                                 start=stj, stop=False)
                                for c0 in (0, 512):
                                    rs = slice(c0, c0 + 512)
                                    inst = nc.tensor.matmul(P[4 + c0 // 512][:], vth[:, kw],
                                                            phi[pj][:, rs], start=stj, stop=False)
                                tinc(inst, ts)
                                if j >= 1:
                                    emit_lo(j - 1)
                            emit_lo(15)
                            wait(nc.vector, ts)
                            nc.vector.tensor_copy(invv[0:1, 0:512], P[6][0:1, :])
                            nc.vector.tensor_copy(invv[0:1, 512:1024], P[7][0:1, :])
                            nc.vector.reciprocal(invv[:], invv[:])
                            pair(smb[0:1, :], smb2[0:1, :], invv[:])
                            vpub()
                            nc.tensor.wait_ge(vs.h, vs.n)
                            nc.tensor.wait_ge(ss.h, ss.n)
                            for c0 in (0, 512):
                                rs = slice(c0, c0 + 512)
                                nc.tensor.matmul(P[c0 // 512][:], ones128[:], smb[:, rs],
                                                 start=True, stop=False)
                                inst = nc.tensor.matmul(P[c0 // 512][:], ones128[:], smb2[:, rs],
                                                        start=False, stop=True)
                            tinc(inst, ts)
                            wait(nc.vector, ts)
                            nc.vector.tensor_copy(tmp1[:, 1024:1536], P[0][:])
                            nc.vector.tensor_copy(tmp1[:, 1536:2048], P[1][:])
                            nc.vector.tensor_mul(tmp1[:, 0:512], P[4][:], tmp1[:, 1024:1536])
                            nc.vector.tensor_mul(tmp1[:, 512:1024], P[5][:], tmp1[:, 1536:2048])
                            pair(onh[g * 4 + h][:], onl[g * 4 + h][:], tmp1[:, 0:1024])
                            vpub()
                            nc.tensor.wait_ge(vs.h, vs.n)
                    if STOP_AFTER <= 3 and g == 0:
                        finish_early([negmh[:, 0:4], k2ln[0:1, 0:32], invv[:],
                                      csQ[:, 0:4]])
                        return nc
        barrier()
        if STOP_AFTER <= 4:
            finish_early()
            return nc

        with ExitStack() as sc4:
            sb4 = lambda name, shape, dt=F32: SB(sc4, name, shape, dt)
            wos = [sb4("wos0", [128, 256]), sb4("wos1", [128, 256])]
            woh = [sb4("woh0", [128, 4096], BF16), sb4("woh1", [128, 4096], BF16)]
            wol = [sb4("wol0", [128, 4096], BF16), sb4("wol1", [128, 4096], BF16)]
            used_mk = [ts.n, ts.n]
            prep_mk = [vs.n, vs.n]
            pdone_mk = [vs.n, vs.n]
            xb_mk = [ds.n, ds.n]
            xbfree_mk = [vs.n, vs.n]
            evac_mk = [vs.n, vs.n]
            st_mk = gd.n

            def prep4(hc):
                b = hc % 2
                cw = hc * 256
                nc.vector.wait_ge(ts.h, used_mk[b])
                for kt in range(NDT):
                    s2 = kt % 2
                    nc.sync.wait_ge(vs.h, pdone_mk[s2])
                    dma(nc.sync, wos[s2][:], wo[kt * 128:(kt + 1) * 128, cw:cw + 256],
                        dw[s2])
                    nc.vector.wait_ge(dw[s2].h, dw[s2].n)
                    pair(woh[b][:, kt * 256:(kt + 1) * 256],
                         wol[b][:, kt * 256:(kt + 1) * 256], wos[s2][:])
                    vpub()
                    pdone_mk[s2] = vs.n
                prep_mk[b] = vs.n

            prep4(0)
            for hc in range(8):
                b = hc % 2
                if hc < 7:
                    prep4(hc + 1)
                for ot in range(2):
                    gidx = hc * 2 + ot
                    sl = gidx % 2
                    nc.sync.wait_ge(vs.h, xbfree_mk[sl])
                    dma(nc.sync, xbuf[sl][:, 0:T], xT[gidx * 128:(gidx + 1) * 128, 0:T],
                        dx[sl])
                    xb_mk[sl] = dx[sl].n
                for ot in range(2):
                    gidx = hc * 2 + ot
                    sl = gidx % 2
                    pb = 2 * (gidx % 2)
                    nc.tensor.wait_ge(vs.h, prep_mk[b])
                    nc.tensor.wait_ge(vs.h, evac_mk[gidx % 2])
                    for kt in range(NDT):
                        wsl = slice(kt * 256 + ot * 128, kt * 256 + (ot + 1) * 128)
                        st = kt == 0
                        sp = kt == NDT - 1
                        for c0 in (0, 512):
                            rs = slice(c0, c0 + 512)
                            tg = P[pb + c0 // 512][:]
                            nc.tensor.matmul(tg, woh[b][:, wsl], onh[kt][:, rs], start=st, stop=False)
                            nc.tensor.matmul(tg, woh[b][:, wsl], onl[kt][:, rs], start=False, stop=False)
                            inst = nc.tensor.matmul(tg, wol[b][:, wsl], onh[kt][:, rs],
                                                    start=False, stop=sp)
                    tinc(inst, ts)
                    used_mk[b] = ts.n
                    nc.vector.wait_ge(ts.h, ts.n)
                    nc.vector.wait_ge(dx[sl].h, xb_mk[sl])
                    nc.vector.wait_ge(gd.h, st_mk)
                    nc.vector.tensor_add(tmp1[:, 0:512], P[pb][:], xbuf[sl][:, 0:512])
                    nc.vector.tensor_add(tmp1[:, 512:1024], P[pb + 1][:], xbuf[sl][:, 512:1024])
                    vpub()
                    evac_mk[gidx % 2] = vs.n
                    xbfree_mk[sl] = vs.n
                    nc.gpsimd.wait_ge(vs.h, vs.n)
                    dma(nc.gpsimd, x1s[gidx * 128:(gidx + 1) * 128, :], tmp1[:, 0:T], gd)
                    st_mk = gd.n
            if STOP_AFTER <= 4.7:
                nc.vector.wait_ge(ts.h, ts.n)
                nc.vector.wait_ge(gd.h, gd.n)
                nc.vector.tensor_copy(tmp1[:, 0:1024], wol[1][:, 2048:3072])
                nc.vector.tensor_copy(tmp1[:, 1024:2048], wol[1][:, 3072:4096])
                nc.vector.tensor_copy(xbuf[0][:, 0:1024], woh[1][:, 2048:3072])
                nc.vector.tensor_copy(xbuf[0][:, 1024:2048], woh[1][:, 3072:4096])
                nc.vector.tensor_copy(xbuf[1][:, 0:1024], wol[1][:, 0:1024])
                nc.vector.tensor_copy(xbuf[1][:, 1024:2048], wol[1][:, 1024:2048])
                vpub()
                finish_early([tmp1[:, 1024:2048], xbuf[0][:, 0:1024],
                              xbuf[0][:, 1024:2048], xbuf[1][:, 0:1024],
                              xbuf[1][:, 1024:2048]])
                return nc
        barrier()
        if STOP_AFTER <= 5:
            finish_early()
            return nc
        if STOP_AFTER <= 5.5:
            dma(nc.gpsimd, y[:, :], x1s, gd)
            nc.gpsimd.wait_ge(gd.h, gd.n)
            return nc

        with ExitStack() as sc56:
            sb56 = lambda name, shape, dt=F32: SB(sc56, name, shape, dt)
            xnh = [sb56(f"xnh{i}", [128, T], BF16) for i in range(16)]
            xnl = [sb56(f"xnl{i}", [128, T], BF16) for i in range(16)]

            with ExitStack() as sc5:
                sb5 = lambda name, shape, dt=F32: SB(sc5, name, shape, dt)
                xh5 = sb5("xh5", [128, T], BF16)
                xl5 = sb5("xl5", [128, T], BF16)
                x2h5 = sb5("x2h5", [128, T], BF16)
                x2l5 = sb5("x2l5", [128, T], BF16)
                sm5 = sb5("sm5", [1, 2048])
                bcS = sb5("bcS", [128, T])
                bcB = sb5("bcB", [128, T])
                for dt in range(NDT):
                    sl = dt % 2
                    nc.sync.wait_ge(vs.h, vs.n)
                    nc.sync.wait_ge(ss.h, ss.n)
                    dma(nc.sync, xbuf[sl][:, 0:T], x1s[dt * 128:(dt + 1) * 128, :])
                    wait(nc.scalar, ds)
                    if dt >= 1:
                        nc.scalar.wait_ge(vs.h, vs.n)
                    nc.scalar.activation(tmp1[:, 0:T], xbuf[sl][:, 0:T], AF.Square)
                    spub()
                    wait(nc.vector, ds)
                    if dt >= 1:
                        nc.vector.wait_ge(ts.h, ts.n)
                    pair(xh5[:], xl5[:], xbuf[sl][:, 0:T])
                    wait(nc.vector, ss)
                    pair(x2h5[:], x2l5[:], tmp1[:, 0:T])
                    vpub()
                    wait(nc.tensor, vs)
                    st = dt == 0
                    sp = dt == NDT - 1
                    for (tgt, hh, ll) in ((P[0][:], xh5[:, 0:512], xl5[:, 0:512]),
                                          (P[1][:], xh5[:, 512:1024], xl5[:, 512:1024]),
                                          (P[2][:], x2h5[:, 0:512], x2l5[:, 0:512]),
                                          (P[3][:], x2h5[:, 512:1024], x2l5[:, 512:1024])):
                        nc.tensor.matmul(tgt, ones128[:], hh, start=st, stop=False)
                        inst = nc.tensor.matmul(tgt, ones128[:], ll, start=False, stop=sp)
                    tinc(inst, ts)
                wait(nc.vector, ts)
                m1 = sm5[0:1, 0:1024]
                vv = sm5[0:1, 1024:2048]
                nc.vector.tensor_copy(m1[0:1, 0:512], P[0][0:1, :])
                nc.vector.tensor_copy(m1[0:1, 512:1024], P[1][0:1, :])
                nc.vector.tensor_copy(vv[0:1, 0:512], P[2][0:1, :])
                nc.vector.tensor_copy(vv[0:1, 512:1024], P[3][0:1, :])
                nc.vector.tensor_scalar_mul(m1, m1, INV_D)
                nc.vector.tensor_scalar_mul(vv, vv, INV_D)
                tS = bcS[0:1, 0:1024]
                tB = bcB[0:1, 0:1024]
                nc.vector.tensor_mul(tS, m1, m1)
                nc.vector.tensor_sub(vv, vv, tS)
                nc.vector.tensor_scalar_add(tS, vv, EPS)
                nc.vector.reciprocal(tB, tS)
                nc.vector.tensor_mul(vv, vv, tB)
                nc.vector.tensor_scalar_add(vv, vv, EPS)
                vpub()
                wait(nc.scalar, vs)
                nc.scalar.activation(tS, tS, AF.Ln)
                nc.scalar.activation(tS, tS, AF.Exp, scale=-0.5)
                nc.scalar.activation(vv, vv, AF.Ln)
                nc.scalar.activation(vv, vv, AF.Exp, scale=-0.5)
                spub()
                wait(nc.vector, ss)
                nc.vector.tensor_mul(tS, tS, vv)
                nc.vector.tensor_mul(m1, m1, tS)
                nc.vector.tensor_scalar_mul(m1, m1, -1.0)
                vpub()
                for (srcv, pb0) in ((tS, 0), (m1, 2)):
                    nc.vector.wait_ge(ts.h, ts.n)
                    pair(smb[0:1, :], smb2[0:1, :], srcv)
                    vpub()
                    wait(nc.tensor, vs)
                    for c0 in (0, 512):
                        rs = slice(c0, c0 + 512)
                        nc.tensor.matmul(P[pb0 + c0 // 512][:], ones128[:], smb[:, rs],
                                         start=True, stop=False)
                        inst = nc.tensor.matmul(P[pb0 + c0 // 512][:], ones128[:], smb2[:, rs],
                                                start=False, stop=True)
                    tinc(inst, ts)
                wait(nc.vector, ts)
                nc.vector.tensor_copy(bcS[:, 0:512], P[0][:])
                nc.vector.tensor_copy(bcS[:, 512:1024], P[1][:])
                nc.vector.tensor_copy(bcB[:, 0:512], P[2][:])
                nc.vector.tensor_copy(bcB[:, 512:1024], P[3][:])
                vpub()
                for dt in range(NDT):
                    sl = dt % 2
                    nc.sync.wait_ge(vs.h, vs.n)
                    dma(nc.sync, xbuf[sl][:, 0:T], x1s[dt * 128:(dt + 1) * 128, :])
                    wait(nc.vector, ds)
                    nc.vector.tensor_mul(tmp1[:, 0:T], xbuf[sl][:, 0:T], bcS[:])
                    nc.vector.tensor_add(tmp1[:, 0:T], tmp1[:, 0:T], bcB[:])
                    nc.vector.tensor_scalar(tmp1[:, 0:T], tmp1[:, 0:T], gb[:, dt:dt + 1],
                                            gb[:, 16 + dt:17 + dt], op0=ALU.mult, op1=ALU.add)
                    pair(xnh[dt][:], xnl[dt][:], tmp1[:, 0:T])
                    vpub()
                barrier()
                if STOP_AFTER <= 6:
                    finish_early()
                    return nc

                with ExitStack() as sc6:
                    sb6 = lambda name, shape, dt=F32: SB(sc6, name, shape, dt)
                    wfb = [sb6("wfb0", [128, 512]), sb6("wfb1", [128, 512])]
                    wabs = sb6("wabs", [128, 1])
                    thrb = sb6("thrb", [128, 2])
                    tq = sb6("tq", [128, 512])
                    tq2 = sb6("tq2", [128, 512])
                    sc6w = top.enter_context(ExitStack())
                    wabh = SB(sc6w, "wabh", [128, 2048], BF16)
                    wabl = SB(sc6w, "wabl", [128, 2048], BF16)
                    for dt in range(NDT):
                        sl = dt % 2
                        nc.sync.wait_ge(vs.h, vs.n)
                        nc.sync.wait_ge(ss.h, ss.n)
                        dma(nc.sync, xbuf[sl][:], wfT[dt * 128:(dt + 1) * 128, :])
                        wait(nc.scalar, ds)
                        if dt >= 1:
                            nc.scalar.wait_ge(vs.h, vs.n)
                        nc.scalar.activation(tmp1[:], xbuf[sl][:], AF.Abs)
                        spub()
                        wait(nc.vector, ss)
                        if dt >= 1:
                            nc.vector.wait_ge(ts.h, ts.n)
                        pair(wabh[:], wabl[:], tmp1[:])
                        vpub()
                        wait(nc.tensor, vs)
                        for c in range(4):
                            cs = slice(c * 512, (c + 1) * 512)
                            nc.tensor.matmul(P[c][:], ones128[:], wabh[:, cs],
                                             start=(dt == 0), stop=False)
                            inst = nc.tensor.matmul(P[c][:], ones128[:], wabl[:, cs],
                                                    start=False, stop=(dt == NDT - 1))
                        tinc(inst, ts)
                    wait(nc.vector, ts)
                    for c in range(4):
                        nc.vector.tensor_copy(xbuf[1][0:1, c * 512:(c + 1) * 512], P[c][0:1, :])
                    nc.vector.reduce_sum(wabs[0:1, 0:1], xbuf[1][0:1, 0:2048], axis=AXX)
                    vpub()
                    nc.vector.wait_ge(vs.h, vs.n)
                    nc.vector.tensor_scalar(tq2[0:1, 0:1], wabs[0:1, 0:1],
                                            0.5 / (2048.0 * 2048.0), 0.5 * Q_EPS,
                                            op0=ALU.mult, op1=ALU.add)
                    vpub()
                    nc.vector.wait_ge(vs.h, vs.n)
                    nc.vector.tensor_scalar_mul(tq2[0:1, 1:2], tq2[0:1, 0:1], -1.0)
                    vpub()
                    nc.vector.wait_ge(vs.h, vs.n)
                    wait(nc.gpsimd, vs)
                    dma(nc.gpsimd, thrs.ap(), tq2[0:1, 0:2], gd)
                    nc.gpsimd.wait_ge(gd.h, gd.n)
                    dma(nc.gpsimd, thrb[:], thrs.ap().broadcast_to([128, 2]), gd)
                    wait(nc.vector, gd)
                    nc.vector.tensor_copy(tq2[0:1, 4:6], thrb[0:1, 0:2])
                    vpub()
                    nc.vector.wait_ge(vs.h, vs.n)
                    sc6w.close()
                    wqq = [sb6("wqq0", [128, 4096], BF16), sb6("wqq1", [128, 4096], BF16)]
                    if STOP_AFTER <= 6.5:
                        nc.vector.tensor_copy(tmp1[:, 0:2], thrb[:])
                        nc.vector.tensor_copy(tmp1[0:1, 2:3], wabs[0:1, 0:1])
                        vpub()
                        finish_early()
                        return nc

                    used6_mk = [ts.n, ts.n]
                    quant_mk = [vs.n, vs.n]
                    qdone_mk = [vs.n, vs.n]
                    xb6_mk = [dx[0].n, dx[1].n]
                    xb6free_mk = [vs.n, vs.n]
                    evac6_mk = [vs.n, vs.n]
                    st6_mk = gd.n

                    def prep6(hq):
                        b = hq % 2
                        cw = hq * 256
                        nc.vector.wait_ge(ts.h, used6_mk[b])
                        for dt in range(NDT):
                            s2 = dt % 2
                            nc.sync.wait_ge(vs.h, qdone_mk[s2])
                            dma(nc.sync, wfb[s2][:, 0:256],
                                wfT[dt * 128:(dt + 1) * 128, cw:cw + 256], dw[s2])
                            nc.vector.wait_ge(dw[s2].h, dw[s2].n)
                            nc.vector.tensor_single_scalar(tq[:, 0:256], wfb[s2][:, 0:256],
                                                           thrb[:, 0:1], ALU.is_ge)
                            nc.vector.tensor_single_scalar(tq2[:, 0:256], wfb[s2][:, 0:256],
                                                           thrb[:, 1:2], ALU.is_le)
                            nc.vector.tensor_tensor(wqq[b][:, dt * 256:(dt + 1) * 256],
                                                    tq[:, 0:256], tq2[:, 0:256], ALU.subtract)
                            vpub()
                            qdone_mk[s2] = vs.n
                        quant_mk[b] = vs.n

                    prep6(0)
                    for hq in range(8):
                        b = hq % 2
                        if hq < 7:
                            prep6(hq + 1)
                        for ot in range(2):
                            gidx = hq * 2 + ot
                            sl = gidx % 2
                            nc.sync.wait_ge(vs.h, xb6free_mk[sl])
                            dma(nc.sync, xbuf[sl][:, 0:T], x1s[gidx * 128:(gidx + 1) * 128, :],
                                dx[sl])
                            xb6_mk[sl] = dx[sl].n
                        for ot in range(2):
                            gidx = hq * 2 + ot
                            sl = gidx % 2
                            pb = 2 * (gidx % 2)
                            nc.tensor.wait_ge(vs.h, quant_mk[b])
                            nc.tensor.wait_ge(vs.h, evac6_mk[gidx % 2])
                            for dt in range(NDT):
                                wsl = slice(dt * 256 + ot * 128, dt * 256 + (ot + 1) * 128)
                                st = dt == 0
                                sp = dt == NDT - 1
                                for c0 in (0, 512):
                                    rs = slice(c0, c0 + 512)
                                    tg = P[pb + c0 // 512][:]
                                    nc.tensor.matmul(tg, wqq[b][:, wsl], xnh[dt][:, rs],
                                                     start=st, stop=False)
                                    inst = nc.tensor.matmul(tg, wqq[b][:, wsl], xnl[dt][:, rs],
                                                            start=False, stop=sp)
                            tinc(inst, ts)
                            used6_mk[b] = ts.n
                            nc.vector.wait_ge(ts.h, ts.n)
                            nc.vector.wait_ge(dx[sl].h, xb6_mk[sl])
                            nc.vector.wait_ge(gd.h, st6_mk)
                            for c0 in (0, 512):
                                nc.vector.scalar_tensor_tensor(tmp1[:, c0:c0 + 512],
                                                               xbuf[sl][:, c0:c0 + 512],
                                                               gb[:, 32 + gidx:33 + gidx],
                                                               P[pb + c0 // 512][:],
                                                               op0=ALU.add, op1=ALU.add)
                            vpub()
                            evac6_mk[gidx % 2] = vs.n
                            xb6free_mk[sl] = vs.n
                            nc.gpsimd.wait_ge(vs.h, vs.n)
                            dma(nc.gpsimd, y[gidx * 128:(gidx + 1) * 128, :], tmp1[:, 0:T], gd)
                            st6_mk = gd.n
                    nc.gpsimd.wait_ge(gd.h, gd.n)
                    nc.gpsimd.wait_ge(gd.h, gd.n)
    return nc


_CACHE = {}


def _tables():
    pos = np.arange(S, dtype=np.float64)
    inv_freq = 1.0 / (10000.0 ** (np.arange(0, HD, 2, dtype=np.float64) / HD))
    fr = np.outer(pos, inv_freq)
    cos64 = np.cos(fr).T
    sin64 = np.sin(fr).T
    cc = np.vstack([cos64, cos64])
    ssg = np.vstack([-sin64, sin64])
    qs = 1.0 / np.sqrt(128.0)

    def hl(a):
        h = a.astype(np.float32)
        l = (a - h.astype(np.float64)).astype(np.float32)
        return h, l

    return hl(cc), hl(ssg), hl(cc * qs), hl(ssg * qs)


def kernel(L0_attn_hidden_states, L0_attn_Wq, L0_attn_Wk, L0_attn_Wv, L0_attn_Wo,
           L0_ffn_weight, L0_ffn_bias, L0_ffn_ln_gamma, L0_ffn_ln_beta):
    x = np.asarray(L0_attn_hidden_states, dtype=np.float32)
    perm = np.empty(128, dtype=np.int64)
    perm[:64] = np.arange(64) * 2
    perm[64:] = np.arange(64) * 2 + 1
    permq = np.concatenate([h * 128 + perm for h in range(16)])
    permk = np.concatenate([h * 128 + perm for h in range(4)])
    wq_p = np.ascontiguousarray(np.asarray(L0_attn_Wq, np.float32)[:, permq])
    wk_p = np.ascontiguousarray(np.asarray(L0_attn_Wk, np.float32)[:, permk])
    wv = np.ascontiguousarray(np.asarray(L0_attn_Wv, np.float32))
    wo = np.ascontiguousarray(np.asarray(L0_attn_Wo, np.float32))
    wfT = np.ascontiguousarray(np.asarray(L0_ffn_weight, np.float32).T)
    gbb = np.zeros((128, 48), np.float32)
    gbb[:, 0:16] = np.asarray(L0_ffn_ln_gamma, np.float32).reshape(16, 128).T
    gbb[:, 16:32] = np.asarray(L0_ffn_ln_beta, np.float32).reshape(16, 128).T
    gbb[:, 32:48] = np.asarray(L0_ffn_bias, np.float32).reshape(16, 128).T
    eye = np.eye(128, dtype=np.float32)

    (cch, ccl), (ssh, ssl), (qcch, qccl), (qssh, qssl) = _tables()

    if "nc" not in _CACHE:
        _CACHE["nc"] = build_nc()
    nc = _CACHE["nc"]

    in_maps = []
    for c in range(8):
        b, half = c // 2, c % 2
        order = np.r_[half * 1024:half * 1024 + 1024,
                      (1 - half) * 1024:(1 - half) * 1024 + 1024]
        in_maps.append({
            "xT": np.ascontiguousarray(x[b].T[:, order]),
            "wq": wq_p, "wk": wk_p, "wv": wv, "wo": wo, "wfT": wfT,
            "gbb": gbb, "eye": eye,
            "kcch": np.ascontiguousarray(cch[:, order]),
            "kccl": np.ascontiguousarray(ccl[:, order]),
            "kssh": np.ascontiguousarray(ssh[:, order]),
            "kssl": np.ascontiguousarray(ssl[:, order]),
            "qcch": np.ascontiguousarray(qcch[:, order[:1024]]),
            "qccl": np.ascontiguousarray(qccl[:, order[:1024]]),
            "qssh": np.ascontiguousarray(qssh[:, order[:1024]]),
            "qssl": np.ascontiguousarray(qssl[:, order[:1024]]),
        })
    res = run_bass_kernel_spmd(nc, in_maps, core_ids=list(range(8)))
    out = np.empty((4, 2048, 2048), np.float32)
    for c in range(8):
        b, half = c // 2, c % 2
        out[b, half * 1024:(half + 1) * 1024, :] = res.results[c]["y"].T
    return out


# version 15
# speedup vs baseline: 1.1779x; 1.0770x over previous
import numpy as np
from contextlib import ExitStack

import concourse.bass as bass
import concourse.mybir as mybir
from concourse.bass_utils import run_bass_kernel_spmd

F32 = mybir.dt.float32
F32R = mybir.dt.float32r
BF16 = mybir.dt.bfloat16
AF = mybir.ActivationFunctionType
ALU = mybir.AluOpType
AXX = mybir.AxisListType.X

D = 2048
S = 2048
T = 1024
NDT = 16
HD = 128
INV_D = 1.0 / 2048.0
EPS = 1e-5
Q_EPS = 1e-5


STOP_AFTER = 99


class Sem:
    __slots__ = ("h", "n")

    def __init__(self, h):
        self.h = h
        self.n = 0


def build_nc():
    nc = bass.Bass("TRN2", target_bir_lowering=False, debug=False,
                   detect_race_conditions=False)

    din = lambda name, shape: nc.dram_tensor(name, shape, F32, kind="ExternalInput").ap()
    xT = din("xT", [D, S])
    wq = din("wq", [D, D])
    wk = din("wk", [D, 512])
    wv = din("wv", [D, 512])
    wo = din("wo", [D, D])
    wfT = din("wfT", [D, D])
    gbb = din("gbb", [128, 48])
    kcch = din("kcch", [128, S])
    kccl = din("kccl", [128, S])
    kssh = din("kssh", [128, S])
    kssl = din("kssl", [128, S])
    qcch = din("qcch", [128, T])
    qccl = din("qccl", [128, T])
    qssh = din("qssh", [128, T])
    qssl = din("qssl", [128, T])
    eye = din("eye", [128, 128])
    y = nc.dram_tensor("y", [D, T], F32, kind="ExternalOutput").ap()
    x1s = nc.dram_tensor("x1s", [D, T], F32).ap()
    thrs = nc.dram_tensor("thrs", [1, 2], F32)

    with ExitStack() as top:
        _uid = [0]

        def SB(st_, name, shape, dt=F32):
            _uid[0] += 1
            return st_.enter_context(nc.sbuf_tensor(f"{name}_{_uid[0]}", shape, dt))

        sb = lambda name, shape, dt=F32: SB(top, name, shape, dt)

        P = [top.enter_context(nc.psum_tensor(f"ps{i}", [128, 512], F32))
             for i in range(8)]

        ds = Sem(top.enter_context(nc.semaphore("ds")))
        gd = Sem(top.enter_context(nc.semaphore("gd")))
        ts = Sem(top.enter_context(nc.semaphore("ts")))
        vs = Sem(top.enter_context(nc.semaphore("vs")))
        ss = Sem(top.enter_context(nc.semaphore("ss")))
        gs = Sem(top.enter_context(nc.semaphore("gs")))
        dw = [Sem(top.enter_context(nc.semaphore("dw0"))),
              Sem(top.enter_context(nc.semaphore("dw1")))]
        dx = [Sem(top.enter_context(nc.semaphore("dx0"))),
              Sem(top.enter_context(nc.semaphore("dx1")))]
        bar = Sem(top.enter_context(nc.semaphore("bar")))

        eye_f = sb("eye_f", [128, 128])
        ones_c_bf = sb("ones_c_bf", [128, 1], BF16)
        ones128 = sb("ones128", [128, 128], BF16)
        gb = sb("gb", [128, 48])
        smb = sb("smb", [128, 1024], BF16)
        smb2 = sb("smb2", [128, 1024], BF16)
        pub1 = sb("pub1", [1, 8])
        pub2 = sb("pub2", [1, 8])
        pub3 = sb("pub3", [1, 8])
        pub4 = sb("pub4", [1, 8])
        xbuf = [sb("xbuf0", [128, S]), sb("xbuf1", [128, S])]
        tmp1 = sb("tmp1", [128, S])
        onh = [sb(f"onh{i}", [128, T], BF16) for i in range(16)]
        onl = [sb(f"onl{i}", [128, T], BF16) for i in range(16)]

        top.enter_context(nc.Block())

        ENG = (nc.tensor, nc.vector, nc.scalar, nc.sync, nc.gpsimd)

        def dma(q, out, in_, sem=ds):
            q.dma_start(out=out, in_=in_).then_inc(sem.h, 16)
            sem.n += 16

        def tinc(inst, sem):
            inst.then_inc(sem.h, 1)
            sem.n += 1

        def wait(eng, *sems):
            for s_ in sems:
                eng.wait_ge(s_.h, s_.n)

        def pair(dst_h, dst_l, src):
            nc.vector.tensor_copy(dst_h, src)
            nc.vector.tensor_tensor(dst_l, src, dst_h, ALU.subtract)

        def vpub():
            tinc(nc.vector.tensor_copy(pub1[:], pub2[:]), vs)

        def spub():
            tinc(nc.scalar.copy(pub3[:], pub2[:]), ss)

        def gpub():
            tinc(nc.gpsimd.tensor_copy(pub4[:], pub2[:]), gs)

        def barrier():
            nc.sync.wait_ge(ds.h, ds.n)
            nc.gpsimd.wait_ge(gd.h, gd.n)
            for eng in ENG:
                for s_ in (ds, gd, ts, vs, ss, gs, dw[0], dw[1], dx[0], dx[1]):
                    eng.wait_ge(s_.h, s_.n)

        def finish_early(extra=()):
            barrier()
            dma(nc.gpsimd, y[0:128, :], tmp1[:, 0:T], gd)
            for i, ap_ in enumerate(extra):
                dma(nc.gpsimd, y[(i + 1) * 128:(i + 1) * 128 + ap_.shape[0],
                                 0:ap_.shape[1]], ap_, gd)
            nc.gpsimd.wait_ge(gd.h, gd.n)

        dma(nc.sync, eye_f[:], eye)
        dma(nc.sync, gb[:], gbb)
        wait(nc.vector, ds)
        nc.vector.tensor_scalar_mul(pub1[:], eye_f[0:1, 0:8], 0.0)
        nc.vector.tensor_scalar_mul(pub2[:], eye_f[0:1, 0:8], 0.0)
        nc.vector.tensor_scalar(ones_c_bf[:], eye_f[:, 0:1], 0.0, 1.0,
                                op0=ALU.mult, op1=ALU.add)
        nc.vector.tensor_scalar(ones128[:], eye_f[:], 0.0, 1.0,
                                op0=ALU.mult, op1=ALU.add)
        for c in range(8):
            nc.vector.tensor_scalar_mul(smb[:, c * 128:(c + 1) * 128], eye_f[:], 0.0)
            nc.vector.tensor_scalar_mul(smb2[:, c * 128:(c + 1) * 128], eye_f[:], 0.0)
        vpub()
        if STOP_AFTER <= 0:
            finish_early()
            return nc

        with ExitStack() as scA:
            sbA = lambda name, shape, dt=F32: SB(scA, name, shape, dt)
            bc_s1 = sbA("bc_s1", [128, S])
            bc_negm = sbA("bc_negm", [128, S])
            ck_cc = sbA("ck_cc", [128, S])
            ck_ss = sbA("ck_ss", [128, S])
            cq_cc = sbA("cq_cc", [128, T])
            cq_ss = sbA("cq_ss", [128, T])
            tmp2 = sbA("tmp2", [128, S])
            tmpr = sbA("tmpr", [128, T], BF16)

            with ExitStack() as sc1:
                sb1 = lambda name, shape, dt=F32: SB(sc1, name, shape, dt)
                xh1 = sb1("xh1", [128, S], BF16)
                xl1 = sb1("xl1", [128, S], BF16)
                x2h1 = sb1("x2h1", [128, S], BF16)
                x2l1 = sb1("x2l1", [128, S], BF16)
                sm = sb1("sm", [1, 2048])
                sm2 = sb1("sm2", [1, 2048])
                st_s1 = sb1("st_s1", [1, 2048])
                sumx_tgt = [P[0][0:1, :], P[1][0:1, :], P[2][0:1, :], P[3][0:1, :]]
                sumx2_tgt = [P[4][0:1, :], P[5][0:1, :], P[6][0:1, :], P[7][0:1, :]]
                for dt in range(NDT):
                    sl = dt % 2
                    nc.sync.wait_ge(vs.h, vs.n)
                    nc.sync.wait_ge(ss.h, ss.n)
                    dma(nc.sync, xbuf[sl][:], xT[dt * 128:(dt + 1) * 128, :])
                    wait(nc.scalar, ds)
                    if dt >= 1:
                        nc.scalar.wait_ge(vs.h, vs.n)
                    nc.scalar.activation(tmp2[:], xbuf[sl][:], AF.Square)
                    spub()
                    wait(nc.vector, ds)
                    if dt >= 1:
                        nc.vector.wait_ge(ts.h, ts.n)
                    pair(xh1[:], xl1[:], xbuf[sl][:])
                    wait(nc.vector, ss)
                    pair(x2h1[:], x2l1[:], tmp2[:])
                    vpub()
                    wait(nc.tensor, vs)
                    st = dt == 0
                    sp = dt == NDT - 1
                    for c in range(4):
                        cs = slice(c * 512, (c + 1) * 512)
                        nc.tensor.matmul(P[c][:], ones128[:], xh1[:, cs], start=st, stop=False)
                        nc.tensor.matmul(P[c][:], ones128[:], xl1[:, cs], start=False, stop=sp)
                        nc.tensor.matmul(P[4 + c][:], ones128[:], x2h1[:, cs], start=st, stop=False)
                        inst = nc.tensor.matmul(P[4 + c][:], ones128[:], x2l1[:, cs], start=False, stop=sp)
                    tinc(inst, ts)
                if STOP_AFTER <= 0.3:
                    finish_early()
                    return nc
                wait(nc.vector, ts)
                for c in range(4):
                    nc.vector.tensor_copy(sm[0:1, c * 512:(c + 1) * 512], sumx_tgt[c])
                    nc.vector.tensor_copy(sm2[0:1, c * 512:(c + 1) * 512], sumx2_tgt[c])
                nc.vector.tensor_scalar_mul(sm[:], sm[:], INV_D)
                nc.vector.tensor_scalar_mul(sm2[:], sm2[:], INV_D)
                nc.vector.tensor_mul(st_s1[:], sm[:], sm[:])
                nc.vector.tensor_sub(sm2[:], sm2[:], st_s1[:])
                nc.vector.tensor_scalar_add(sm2[:], sm2[:], EPS)
                vpub()
                wait(nc.scalar, vs)
                nc.scalar.activation(st_s1[:], sm2[:], AF.Ln)
                nc.scalar.activation(st_s1[:], st_s1[:], AF.Exp, scale=-0.5)
                spub()
                wait(nc.vector, ss)
                nc.vector.tensor_scalar_mul(sm[:], sm[:], -1.0)
                vpub()
                if STOP_AFTER <= 0.6:
                    finish_early()
                    return nc

                for (srcv, dstt) in ((st_s1, bc_s1), (sm, bc_negm)):
                    for half in range(2):
                        hs = slice(half * 1024, (half + 1) * 1024)
                        pair(smb[0:1, :], smb2[0:1, :], srcv[0:1, hs])
                        vpub()
                        wait(nc.tensor, vs)
                        for c in range(2):
                            s0 = slice(c * 512, (c + 1) * 512)
                            nc.tensor.matmul(P[c][:], ones128[:], smb[:, s0], start=True, stop=False)
                            inst = nc.tensor.matmul(P[c][:], ones128[:], smb2[:, s0], start=False, stop=True)
                        tinc(inst, ts)
                        wait(nc.vector, ts)
                        for c in range(2):
                            nc.vector.tensor_copy(
                                dstt[:, half * 1024 + c * 512:half * 1024 + (c + 1) * 512],
                                P[c][:])
                        vpub()

                if STOP_AFTER <= 0.8:
                    finish_early([bc_s1[:, 0:1024], bc_negm[:, 0:1024],
                                  smb[:, 0:512].bitcast(F32), st_s1[0:1, 0:1024]])
                    return nc
                for (hi_t, lo_t, dst, w, scl) in (
                        (kcch, kccl, ck_cc, S, bc_s1), (kssh, kssl, ck_ss, S, bc_s1),
                        (qcch, qccl, cq_cc, T, bc_s1), (qssh, qssl, cq_ss, T, bc_s1)):
                    nc.sync.wait_ge(vs.h, vs.n)
                    dma(nc.sync, tmp1[:, 0:w], hi_t)
                    dma(nc.sync, tmp2[:, 0:w], lo_t)
                    wait(nc.vector, ds)
                    nc.vector.tensor_add(tmp1[:, 0:w], tmp1[:, 0:w], tmp2[:, 0:w])
                    nc.vector.tensor_mul(dst[:], tmp1[:, 0:w], scl[:, 0:w])
                    vpub()
            barrier()
            if STOP_AFTER <= 1:
                finish_early([ck_cc[:, 0:1024], ck_ss[:, 0:1024],
                              cq_cc[:, 0:1024], cq_ss[:, 0:1024]])
                return nc

            with ExitStack() as sc2:
                sb2 = lambda name, shape, dt=F32: SB(sc2, name, shape, dt)
                kresh = sb2("kresh", [128, S], BF16)
                kresl = sb2("kresl", [128, S], BF16)
                vth = sb2("vth", [128, S], BF16)
                vtl = sb2("vtl", [128, S], BF16)
                phi = [sb2("phi0", [128, T], BF16), sb2("phi1", [128, T], BF16)]
                plo = [sb2("plo0", [128, T], BF16), sb2("plo1", [128, T], BF16)]
                pfull = [sb2("pf0", [128, T]), sb2("pf1", [128, T])]
                invv = sb2("invv", [1, 1024])
                csKV = sb2("csKV", [128, 2])
                csQ = sb2("csQ", [128, 4])
                negmh = sb2("negmh", [128, 4])
                k2ln = sb2("k2ln", [1, 32])

                for g in range(4):
                    with ExitStack() as sckv:
                        sbk = lambda name, shape, dt=F32: SB(sckv, name, shape, dt)
                        wkh = sbk("wkh", [128, 2048], BF16)
                        wkl = sbk("wkl", [128, 2048], BF16)
                        wvh = sbk("wvh", [128, 2048], BF16)
                        wvl = sbk("wvl", [128, 2048], BF16)
                        xph = sbk("xph", [128, 2048], BF16)
                        xpl = sbk("xpl", [128, 2048], BF16)
                        nc.sync.wait_ge(ts.h, ts.n)
                        for dt in range(NDT):
                            dma(nc.sync, xbuf[0][:, dt * 128:(dt + 1) * 128],
                                wk[dt * 128:(dt + 1) * 128, g * 128:(g + 1) * 128], dw[0])
                            dma(nc.sync, xbuf[1][:, dt * 128:(dt + 1) * 128],
                                wv[dt * 128:(dt + 1) * 128, g * 128:(g + 1) * 128], dw[0])
                        wait(nc.vector, dw[0])
                        pair(wkh[:], wkl[:], xbuf[0][:])
                        pair(wvh[:], wvl[:], xbuf[1][:])
                        vpub()
                        wait(nc.tensor, vs)
                        for dt in range(NDT):
                            st = dt == 0
                            sp = dt == NDT - 1
                            w0 = slice(dt * 128, (dt + 1) * 128)
                            nc.tensor.matmul(P[6][:, 0:1], wkh[:, w0], ones_c_bf[:], start=st, stop=False)
                            nc.tensor.matmul(P[6][:, 0:1], wkl[:, w0], ones_c_bf[:], start=False, stop=sp)
                            nc.tensor.matmul(P[7][:, 0:1], wvh[:, w0], ones_c_bf[:], start=st, stop=False)
                            inst = nc.tensor.matmul(P[7][:, 0:1], wvl[:, w0], ones_c_bf[:], start=False, stop=sp)
                        tinc(inst, ts)
                        wait(nc.vector, ts)
                        nc.vector.tensor_copy(csKV[:, 0:1], P[6][:, 0:1])
                        nc.vector.tensor_copy(csKV[:, 1:2], P[7][:, 0:1])
                        vpub()

                        kt_tgt = [P[0][:], P[1][:], P[2][:], P[3][:]]
                        vt_tgt = [P[4][:], P[5][:], P[6][:], P[7][:]]
                        slot_pair = [vs.n, vs.n]
                        slot_mm = [ts.n, ts.n]
                        for dt in range(NDT):
                            sl = dt % 2
                            nc.sync.wait_ge(vs.h, slot_pair[sl])
                            dma(nc.sync, xbuf[sl][:], xT[dt * 128:(dt + 1) * 128, :], dx[sl])
                            nc.vector.wait_ge(dx[sl].h, dx[sl].n)
                            nc.vector.wait_ge(ts.h, ts.n)
                            pair(xph[:], xpl[:], xbuf[sl][:])
                            vpub()
                            slot_pair[sl] = vs.n
                            wait(nc.tensor, vs)
                            st = dt == 0
                            sp = dt == NDT - 1
                            w0 = slice(dt * 128, (dt + 1) * 128)
                            for c in range(4):
                                cs = slice(c * 512, (c + 1) * 512)
                                nc.tensor.matmul(kt_tgt[c], wkh[:, w0], xph[:, cs], start=st, stop=False)
                                nc.tensor.matmul(kt_tgt[c], wkh[:, w0], xpl[:, cs], start=False, stop=False)
                                nc.tensor.matmul(kt_tgt[c], wkl[:, w0], xph[:, cs], start=False, stop=sp)
                                nc.tensor.matmul(vt_tgt[c], wvh[:, w0], xph[:, cs], start=st, stop=False)
                                nc.tensor.matmul(vt_tgt[c], wvh[:, w0], xpl[:, cs], start=False, stop=False)
                                inst = nc.tensor.matmul(vt_tgt[c], wvl[:, w0], xph[:, cs], start=False, stop=sp)
                            tinc(inst, ts)
                            slot_mm[sl] = ts.n

                        wait(nc.vector, ts)
                        for c in range(4):
                            cs = slice(c * 512, (c + 1) * 512)
                            nc.vector.scalar_tensor_tensor(tmp1[:, cs], bc_negm[:, cs], csKV[:, 0:1],
                                                           kt_tgt[c], op0=ALU.mult, op1=ALU.add)
                        vpub()
                        wait(nc.gpsimd, vs)
                        dma(nc.gpsimd, tmp2[0:64, :], tmp1[64:128, :], gd)
                        dma(nc.gpsimd, tmp2[64:128, :], tmp1[0:64, :], gd)
                        wait(nc.vector, gd)
                        nc.vector.tensor_mul(tmp1[:], tmp1[:], ck_cc[:])
                        nc.vector.tensor_mul(tmp2[:], tmp2[:], ck_ss[:])
                        nc.vector.tensor_add(tmp1[:], tmp1[:], tmp2[:])
                        pair(kresh[:], kresl[:], tmp1[:])
                        vpub()
                        wait(nc.scalar, vs)
                        for half in range(2):
                            hs = slice(half * 1024, (half + 1) * 1024)
                            if half == 1:
                                nc.scalar.wait_ge(ts.h, ts.n)
                            nc.scalar.activation(tmpr[:], tmp1[:, hs], AF.Square)
                            spub()
                            wait(nc.tensor, ss)
                            tg = [kt_tgt[2 * half], kt_tgt[2 * half + 1]]
                            nc.tensor.matmul(tg[0], ones128[:], tmpr[:, 0:512], start=True, stop=True)
                            inst = nc.tensor.matmul(tg[1], ones128[:], tmpr[:, 512:1024],
                                                    start=True, stop=True)
                            tinc(inst, ts)
                        wait(nc.vector, ts)
                        for c in range(4):
                            nc.vector.tensor_copy(xbuf[1][0:1, c * 512:(c + 1) * 512],
                                                  kt_tgt[c][0:1, :])
                        nc.vector.reduce_max(k2ln[0:1, 0:1], xbuf[1][0:1, 0:2048], axis=AXX)
                        vpub()
                        wait(nc.scalar, vs)
                        nc.scalar.activation(k2ln[0:1, 1:2], k2ln[0:1, 0:1], AF.Ln)
                        spub()

                        wait(nc.vector, ts)
                        vn = xbuf[0]
                        for c in range(4):
                            cs = slice(c * 512, (c + 1) * 512)
                            nc.vector.scalar_tensor_tensor(vn[:, cs], bc_negm[:, cs], csKV[:, 1:2],
                                                           vt_tgt[c], op0=ALU.mult, op1=ALU.add)
                        nc.vector.tensor_mul(vn[:], vn[:], bc_s1[:])
                        vpub()
                        wait(nc.tensor, vs)
                        for k in range(16):
                            tg = P[6] if k % 2 == 0 else P[7]
                            if k >= 2:
                                nc.tensor.wait_ge(vs.h, vs.n)
                            inst = nc.tensor.transpose(tg[:, 0:128], vn[:, k * 128:(k + 1) * 128], eye_f[:])
                            tinc(inst, ts)
                            wait(nc.vector, ts)
                            ksl = slice(k * 128, (k + 1) * 128)
                            nc.vector.tensor_copy(vth[:, ksl], tg[:, 0:128])
                            nc.vector.tensor_tensor(vtl[:, ksl], tg[:, 0:128], vth[:, ksl], ALU.subtract)
                            vpub()

                    if STOP_AFTER <= 2 and g == 0:
                        finish_early([csKV[:, 0:2], k2ln[0:1, 0:32],
                                      xbuf[0][:, 0:1024]])
                        return nc
                    with ExitStack() as scq:
                        sbq = lambda name, shape, dt=F32: SB(scq, name, shape, dt)
                        wqs = [sbq("wqs0", [128, 512]), sbq("wqs1", [128, 512])]
                        wqph = [sbq("wqph0", [128, 512], BF16), sbq("wqph1", [128, 512], BF16)]
                        wqpl = [sbq("wqpl0", [128, 512], BF16), sbq("wqpl1", [128, 512], BF16)]
                        xoph = sbq("xoph", [128, T], BF16)
                        xopl = sbq("xopl", [128, T], BF16)
                        qhh = [sbq(f"qhh{i}", [128, T], BF16) for i in range(4)]
                        qll = [sbq(f"qll{i}", [128, T], BF16) for i in range(4)]

                        cs_tgt = [P[0][:, 0:1], P[2][:, 0:1], P[4][:, 0:1], P[6][:, 0:1]]
                        slot_pair = [vs.n, vs.n]
                        slot_mm = [ts.n, ts.n]
                        for dt in range(NDT):
                            sl = dt % 2
                            nc.sync.wait_ge(vs.h, slot_pair[sl])
                            dma(nc.sync, wqs[sl][:], wq[dt * 128:(dt + 1) * 128, g * 512:(g + 1) * 512],
                                dw[sl])
                            nc.vector.wait_ge(dw[sl].h, dw[sl].n)
                            nc.vector.wait_ge(ts.h, slot_mm[sl])
                            pair(wqph[sl][:], wqpl[sl][:], wqs[sl][:])
                            vpub()
                            slot_pair[sl] = vs.n
                            wait(nc.tensor, vs)
                            st = dt == 0
                            sp = dt == NDT - 1
                            for h in range(4):
                                hsl = slice(h * 128, (h + 1) * 128)
                                nc.tensor.matmul(cs_tgt[h], wqph[sl][:, hsl], ones_c_bf[:], start=st, stop=False)
                                inst = nc.tensor.matmul(cs_tgt[h], wqpl[sl][:, hsl], ones_c_bf[:],
                                                        start=False, stop=sp)
                            tinc(inst, ts)
                            slot_mm[sl] = ts.n
                        wait(nc.vector, ts)
                        for h in range(4):
                            nc.vector.tensor_copy(csQ[:, h:h + 1], cs_tgt[h])
                        vpub()

                        slot_pair = [vs.n, vs.n]
                        slot_mm = [ts.n, ts.n]
                        for dt in range(NDT):
                            sl = dt % 2
                            nc.sync.wait_ge(vs.h, slot_pair[sl])
                            dma(nc.sync, wqs[sl][:], wq[dt * 128:(dt + 1) * 128, g * 512:(g + 1) * 512],
                                dw[sl])
                            dma(nc.sync, xbuf[sl][:, 0:T], xT[dt * 128:(dt + 1) * 128, 0:T],
                                dx[sl])
                            nc.vector.wait_ge(dw[sl].h, dw[sl].n)
                            nc.vector.wait_ge(dx[sl].h, dx[sl].n)
                            nc.vector.wait_ge(ts.h, ts.n)
                            pair(wqph[sl][:], wqpl[sl][:], wqs[sl][:])
                            pair(xoph[:], xopl[:], xbuf[sl][:, 0:T])
                            vpub()
                            slot_pair[sl] = vs.n
                            wait(nc.tensor, vs)
                            st = dt == 0
                            sp = dt == NDT - 1
                            for h in range(4):
                                hsl = slice(h * 128, (h + 1) * 128)
                                for c0 in (0, 512):
                                    rs = slice(c0, c0 + 512)
                                    tgt = P[2 * h + c0 // 512][:]
                                    nc.tensor.matmul(tgt, wqph[sl][:, hsl], xoph[:, rs], start=st, stop=False)
                                    nc.tensor.matmul(tgt, wqph[sl][:, hsl], xopl[:, rs], start=False, stop=False)
                                    inst = nc.tensor.matmul(tgt, wqpl[sl][:, hsl], xoph[:, rs],
                                                            start=False, stop=sp)
                            tinc(inst, ts)
                            slot_mm[sl] = ts.n

                        wait(nc.vector, ts)
                        for h in range(4):
                            segs = [(P[2 * h][:], 0, 512), (P[2 * h + 1][:], 512, 1024)]
                            own = [P[2 * h][0:1, :], P[2 * h + 1][0:1, :]]
                            for (ap_, a, b) in segs:
                                nc.vector.scalar_tensor_tensor(tmp1[:, a:b], bc_negm[:, a:b],
                                                               csQ[:, h:h + 1], ap_,
                                                               op0=ALU.mult, op1=ALU.add)
                            vpub()
                            wait(nc.gpsimd, vs)
                            dma(nc.gpsimd, tmp2[0:64, 0:T], tmp1[64:128, 0:T], gd)
                            dma(nc.gpsimd, tmp2[64:128, 0:T], tmp1[0:64, 0:T], gd)
                            wait(nc.vector, gd)
                            nc.vector.tensor_mul(tmp1[:, 0:T], tmp1[:, 0:T], cq_cc[:])
                            nc.vector.tensor_mul(tmp2[:, 0:T], tmp2[:, 0:T], cq_ss[:])
                            nc.vector.tensor_add(tmp1[:, 0:T], tmp1[:, 0:T], tmp2[:, 0:T])
                            pair(qhh[h][:], qll[h][:], tmp1[:, 0:T])
                            vpub()
                            wait(nc.scalar, vs)
                            nc.scalar.activation(tmpr[:], tmp1[:, 0:T], AF.Square)
                            spub()
                            wait(nc.tensor, ss, vs)
                            nc.tensor.matmul(P[2 * h][:], ones128[:], tmpr[:, 0:512], start=True, stop=True)
                            inst = nc.tensor.matmul(P[2 * h + 1][:], ones128[:], tmpr[:, 512:1024],
                                                    start=True, stop=True)
                            tinc(inst, ts)
                            wait(nc.vector, ts)
                            nc.vector.tensor_copy(invv[0:1, 0:512], own[0])
                            nc.vector.tensor_copy(invv[0:1, 512:1024], own[1])
                            nc.vector.reduce_max(k2ln[0:1, 4 + h:5 + h], invv[:], axis=AXX)
                            vpub()
                            if STOP_AFTER <= 2.5 and g == 0 and h == 0:
                                nc.vector.tensor_copy(xbuf[1][:, 0:1024], tmpr[:])
                                nc.vector.tensor_copy(xbuf[1][:, 1024:2048], qhh[0][:])
                                nc.vector.tensor_copy(xbuf[0][0:1, 0:512], own[0])
                                vpub()
                                finish_early([xbuf[1][:, 0:1024],
                                              xbuf[1][:, 1024:2048],
                                              k2ln[0:1, 0:32],
                                              xbuf[0][0:1, 0:512]])
                                return nc

                        wait(nc.scalar, vs)
                        nc.scalar.activation(k2ln[0:1, 8:12], k2ln[0:1, 4:8], AF.Ln)
                        spub()
                        wait(nc.vector, ss)
                        nc.vector.tensor_single_scalar(k2ln[0:1, 8:12], k2ln[0:1, 8:12],
                                                       k2ln[0:1, 1:2], ALU.add)
                        vpub()
                        wait(nc.scalar, vs)
                        nc.scalar.activation(k2ln[0:1, 8:12], k2ln[0:1, 8:12], AF.Exp, scale=0.5)
                        spub()
                        wait(nc.vector, ss)
                        nc.vector.tensor_scalar_mul(smb[0:1, 0:4], k2ln[0:1, 8:12], -1.0)
                        vpub()
                        wait(nc.tensor, vs)
                        tinc(nc.tensor.matmul(P[6][:, 0:4], ones128[:], smb[:, 0:4],
                                              start=True, stop=True), ts)
                        wait(nc.vector, ts)
                        nc.vector.tensor_copy(negmh[:], P[6][:, 0:4])
                        vpub()
                        nc.tensor.wait_ge(vs.h, vs.n)

                        for h in range(4):
                            exp_mark = [ss.n, ss.n]
                            lo_mark = [vs.n, vs.n]
                            lo_used = [ts.n, ts.n]

                            def emit_lo(jj):
                                kwj = slice(jj * 128, (jj + 1) * 128)
                                pjj = jj % 2
                                nc.tensor.wait_ge(vs.h, lo_mark[pjj])
                                nc.tensor.matmul(P[6][:], ones128[:], plo[pjj][:, 0:512],
                                                 start=False, stop=(jj == 15))
                                nc.tensor.matmul(P[7][:], ones128[:], plo[pjj][:, 512:1024],
                                                 start=False, stop=(jj == 15))
                                for c0 in (0, 512):
                                    rs = slice(c0, c0 + 512)
                                    nc.tensor.matmul(P[4 + c0 // 512][:], vth[:, kwj],
                                                     plo[pjj][:, rs], start=False, stop=False)
                                    inst = nc.tensor.matmul(P[4 + c0 // 512][:], vtl[:, kwj],
                                                            phi[pjj][:, rs], start=False,
                                                            stop=(jj == 15))
                                tinc(inst, ts)
                                lo_used[pjj] = ts.n

                            sc_mark = [ts.n, ts.n]

                            def emit_scores(jj):
                                pjj = jj % 2
                                kwj = slice(jj * 128, (jj + 1) * 128)
                                nc.tensor.wait_ge(ss.h, exp_mark[pjj])
                                for c0 in (0, 512):
                                    rs = slice(c0, c0 + 512)
                                    tg = P[2 * pjj + c0 // 512][:]
                                    nc.tensor.matmul(tg, kresh[:, kwj], qhh[h][:, rs],
                                                     start=True, stop=False)
                                    nc.tensor.matmul(tg, kresh[:, kwj], qll[h][:, rs],
                                                     start=False, stop=False)
                                    inst = nc.tensor.matmul(tg, kresl[:, kwj], qhh[h][:, rs],
                                                            start=False, stop=True)
                                tinc(inst, ts)
                                sc_mark[pjj] = ts.n

                            emit_scores(0)
                            for j in range(16):
                                pj = j % 2
                                kw = slice(j * 128, (j + 1) * 128)
                                if j + 1 < 16:
                                    emit_scores(j + 1)
                                nc.scalar.wait_ge(ts.h, sc_mark[pj])
                                nc.scalar.wait_ge(ts.h, lo_used[pj])
                                for c0 in (0, 512):
                                    rs = slice(c0, c0 + 512)
                                    nc.scalar.activation(phi[pj][:, rs], P[2 * pj + c0 // 512][:],
                                                         AF.Exp, bias=negmh[:, h:h + 1])
                                    nc.scalar.activation(pfull[pj][:, rs], P[2 * pj + c0 // 512][:],
                                                         AF.Exp, bias=negmh[:, h:h + 1])
                                spub()
                                exp_mark[pj] = ss.n
                                nc.vector.wait_ge(ss.h, ss.n)
                                nc.vector.wait_ge(ts.h, lo_used[pj])
                                nc.vector.tensor_tensor(plo[pj][:], pfull[pj][:], phi[pj][:],
                                                        ALU.subtract)
                                vpub()
                                lo_mark[pj] = vs.n
                                if j >= 2:
                                    emit_lo(j - 1)
                                nc.tensor.wait_ge(ss.h, exp_mark[pj])
                                stj = j == 0
                                nc.tensor.matmul(P[6][:], ones128[:], phi[pj][:, 0:512],
                                                 start=stj, stop=False)
                                nc.tensor.matmul(P[7][:], ones128[:], phi[pj][:, 512:1024],
                                                 start=stj, stop=False)
                                for c0 in (0, 512):
                                    rs = slice(c0, c0 + 512)
                                    inst = nc.tensor.matmul(P[4 + c0 // 512][:], vth[:, kw],
                                                            phi[pj][:, rs], start=stj, stop=False)
                                tinc(inst, ts)
                                if j == 1:
                                    emit_lo(0)
                            emit_lo(15)
                            wait(nc.vector, ts)
                            nc.vector.tensor_copy(invv[0:1, 0:512], P[6][0:1, :])
                            nc.vector.tensor_copy(invv[0:1, 512:1024], P[7][0:1, :])
                            nc.vector.reciprocal(invv[:], invv[:])
                            pair(smb[0:1, :], smb2[0:1, :], invv[:])
                            vpub()
                            nc.tensor.wait_ge(vs.h, vs.n)
                            nc.tensor.wait_ge(ss.h, ss.n)
                            for c0 in (0, 512):
                                rs = slice(c0, c0 + 512)
                                nc.tensor.matmul(P[c0 // 512][:], ones128[:], smb[:, rs],
                                                 start=True, stop=False)
                                inst = nc.tensor.matmul(P[c0 // 512][:], ones128[:], smb2[:, rs],
                                                        start=False, stop=True)
                            tinc(inst, ts)
                            wait(nc.vector, ts)
                            nc.vector.tensor_copy(tmp1[:, 1024:1536], P[0][:])
                            nc.vector.tensor_copy(tmp1[:, 1536:2048], P[1][:])
                            nc.vector.tensor_mul(tmp1[:, 0:512], P[4][:], tmp1[:, 1024:1536])
                            nc.vector.tensor_mul(tmp1[:, 512:1024], P[5][:], tmp1[:, 1536:2048])
                            pair(onh[g * 4 + h][:], onl[g * 4 + h][:], tmp1[:, 0:1024])
                            vpub()
                            nc.tensor.wait_ge(vs.h, vs.n)
                    if STOP_AFTER <= 3 and g == 0:
                        finish_early([negmh[:, 0:4], k2ln[0:1, 0:32], invv[:],
                                      csQ[:, 0:4]])
                        return nc
        barrier()
        if STOP_AFTER <= 4:
            finish_early()
            return nc

        with ExitStack() as sc4:
            sb4 = lambda name, shape, dt=F32: SB(sc4, name, shape, dt)
            wos = [sb4("wos0", [128, 256]), sb4("wos1", [128, 256])]
            woh = [sb4("woh0", [128, 4096], BF16), sb4("woh1", [128, 4096], BF16)]
            wol = [sb4("wol0", [128, 4096], BF16), sb4("wol1", [128, 4096], BF16)]
            used_mk = [ts.n, ts.n]
            prep_mk = [vs.n, vs.n]
            pdone_mk = [vs.n, vs.n]
            xb_mk = [ds.n, ds.n]
            xbfree_mk = [vs.n, vs.n]
            evac_mk = [vs.n, vs.n]
            st_mk = gd.n

            def prep4(hc):
                b = hc % 2
                cw = hc * 256
                nc.vector.wait_ge(ts.h, used_mk[b])
                for kt in range(NDT):
                    s2 = kt % 2
                    nc.sync.wait_ge(vs.h, pdone_mk[s2])
                    dma(nc.sync, wos[s2][:], wo[kt * 128:(kt + 1) * 128, cw:cw + 256],
                        dw[s2])
                    nc.vector.wait_ge(dw[s2].h, dw[s2].n)
                    pair(woh[b][:, kt * 256:(kt + 1) * 256],
                         wol[b][:, kt * 256:(kt + 1) * 256], wos[s2][:])
                    vpub()
                    pdone_mk[s2] = vs.n
                prep_mk[b] = vs.n

            prep4(0)
            for hc in range(8):
                b = hc % 2
                if hc < 7:
                    prep4(hc + 1)
                for ot in range(2):
                    gidx = hc * 2 + ot
                    sl = gidx % 2
                    nc.sync.wait_ge(vs.h, xbfree_mk[sl])
                    dma(nc.sync, xbuf[sl][:, 0:T], xT[gidx * 128:(gidx + 1) * 128, 0:T],
                        dx[sl])
                    xb_mk[sl] = dx[sl].n
                for ot in range(2):
                    gidx = hc * 2 + ot
                    sl = gidx % 2
                    pb = 2 * (gidx % 2)
                    nc.tensor.wait_ge(vs.h, prep_mk[b])
                    nc.tensor.wait_ge(vs.h, evac_mk[gidx % 2])
                    for kt in range(NDT):
                        wsl = slice(kt * 256 + ot * 128, kt * 256 + (ot + 1) * 128)
                        st = kt == 0
                        sp = kt == NDT - 1
                        for c0 in (0, 512):
                            rs = slice(c0, c0 + 512)
                            tg = P[pb + c0 // 512][:]
                            nc.tensor.matmul(tg, woh[b][:, wsl], onh[kt][:, rs], start=st, stop=False)
                            nc.tensor.matmul(tg, woh[b][:, wsl], onl[kt][:, rs], start=False, stop=False)
                            inst = nc.tensor.matmul(tg, wol[b][:, wsl], onh[kt][:, rs],
                                                    start=False, stop=sp)
                    tinc(inst, ts)
                    used_mk[b] = ts.n
                    nc.vector.wait_ge(ts.h, ts.n)
                    nc.vector.wait_ge(dx[sl].h, xb_mk[sl])
                    nc.vector.wait_ge(gd.h, st_mk)
                    nc.vector.tensor_add(tmp1[:, 0:512], P[pb][:], xbuf[sl][:, 0:512])
                    nc.vector.tensor_add(tmp1[:, 512:1024], P[pb + 1][:], xbuf[sl][:, 512:1024])
                    vpub()
                    evac_mk[gidx % 2] = vs.n
                    xbfree_mk[sl] = vs.n
                    nc.gpsimd.wait_ge(vs.h, vs.n)
                    dma(nc.gpsimd, x1s[gidx * 128:(gidx + 1) * 128, :], tmp1[:, 0:T], gd)
                    st_mk = gd.n
            if STOP_AFTER <= 4.7:
                nc.vector.wait_ge(ts.h, ts.n)
                nc.vector.wait_ge(gd.h, gd.n)
                nc.vector.tensor_copy(tmp1[:, 0:1024], wol[1][:, 2048:3072])
                nc.vector.tensor_copy(tmp1[:, 1024:2048], wol[1][:, 3072:4096])
                nc.vector.tensor_copy(xbuf[0][:, 0:1024], woh[1][:, 2048:3072])
                nc.vector.tensor_copy(xbuf[0][:, 1024:2048], woh[1][:, 3072:4096])
                nc.vector.tensor_copy(xbuf[1][:, 0:1024], wol[1][:, 0:1024])
                nc.vector.tensor_copy(xbuf[1][:, 1024:2048], wol[1][:, 1024:2048])
                vpub()
                finish_early([tmp1[:, 1024:2048], xbuf[0][:, 0:1024],
                              xbuf[0][:, 1024:2048], xbuf[1][:, 0:1024],
                              xbuf[1][:, 1024:2048]])
                return nc
        barrier()
        if STOP_AFTER <= 5:
            finish_early()
            return nc
        if STOP_AFTER <= 5.5:
            dma(nc.gpsimd, y[:, :], x1s, gd)
            nc.gpsimd.wait_ge(gd.h, gd.n)
            return nc

        with ExitStack() as sc56:
            sb56 = lambda name, shape, dt=F32: SB(sc56, name, shape, dt)
            xnh = [sb56(f"xnh{i}", [128, T], BF16) for i in range(16)]
            xnl = [sb56(f"xnl{i}", [128, T], BF16) for i in range(16)]

            with ExitStack() as sc5:
                sb5 = lambda name, shape, dt=F32: SB(sc5, name, shape, dt)
                xh5 = sb5("xh5", [128, T], BF16)
                xl5 = sb5("xl5", [128, T], BF16)
                x2h5 = sb5("x2h5", [128, T], BF16)
                x2l5 = sb5("x2l5", [128, T], BF16)
                sm5 = sb5("sm5", [1, 2048])
                bcS = sb5("bcS", [128, T])
                bcB = sb5("bcB", [128, T])
                for dt in range(NDT):
                    sl = dt % 2
                    nc.sync.wait_ge(vs.h, vs.n)
                    nc.sync.wait_ge(ss.h, ss.n)
                    dma(nc.sync, xbuf[sl][:, 0:T], x1s[dt * 128:(dt + 1) * 128, :])
                    wait(nc.scalar, ds)
                    if dt >= 1:
                        nc.scalar.wait_ge(vs.h, vs.n)
                    nc.scalar.activation(tmp1[:, 0:T], xbuf[sl][:, 0:T], AF.Square)
                    spub()
                    wait(nc.vector, ds)
                    if dt >= 1:
                        nc.vector.wait_ge(ts.h, ts.n)
                    pair(xh5[:], xl5[:], xbuf[sl][:, 0:T])
                    wait(nc.vector, ss)
                    pair(x2h5[:], x2l5[:], tmp1[:, 0:T])
                    vpub()
                    wait(nc.tensor, vs)
                    st = dt == 0
                    sp = dt == NDT - 1
                    for (tgt, hh, ll) in ((P[0][:], xh5[:, 0:512], xl5[:, 0:512]),
                                          (P[1][:], xh5[:, 512:1024], xl5[:, 512:1024]),
                                          (P[2][:], x2h5[:, 0:512], x2l5[:, 0:512]),
                                          (P[3][:], x2h5[:, 512:1024], x2l5[:, 512:1024])):
                        nc.tensor.matmul(tgt, ones128[:], hh, start=st, stop=False)
                        inst = nc.tensor.matmul(tgt, ones128[:], ll, start=False, stop=sp)
                    tinc(inst, ts)
                wait(nc.vector, ts)
                m1 = sm5[0:1, 0:1024]
                vv = sm5[0:1, 1024:2048]
                nc.vector.tensor_copy(m1[0:1, 0:512], P[0][0:1, :])
                nc.vector.tensor_copy(m1[0:1, 512:1024], P[1][0:1, :])
                nc.vector.tensor_copy(vv[0:1, 0:512], P[2][0:1, :])
                nc.vector.tensor_copy(vv[0:1, 512:1024], P[3][0:1, :])
                nc.vector.tensor_scalar_mul(m1, m1, INV_D)
                nc.vector.tensor_scalar_mul(vv, vv, INV_D)
                tS = bcS[0:1, 0:1024]
                tB = bcB[0:1, 0:1024]
                nc.vector.tensor_mul(tS, m1, m1)
                nc.vector.tensor_sub(vv, vv, tS)
                nc.vector.tensor_scalar_add(tS, vv, EPS)
                nc.vector.reciprocal(tB, tS)
                nc.vector.tensor_mul(vv, vv, tB)
                nc.vector.tensor_scalar_add(vv, vv, EPS)
                vpub()
                wait(nc.scalar, vs)
                nc.scalar.activation(tS, tS, AF.Ln)
                nc.scalar.activation(tS, tS, AF.Exp, scale=-0.5)
                nc.scalar.activation(vv, vv, AF.Ln)
                nc.scalar.activation(vv, vv, AF.Exp, scale=-0.5)
                spub()
                wait(nc.vector, ss)
                nc.vector.tensor_mul(tS, tS, vv)
                nc.vector.tensor_mul(m1, m1, tS)
                nc.vector.tensor_scalar_mul(m1, m1, -1.0)
                vpub()
                for (srcv, pb0) in ((tS, 0), (m1, 2)):
                    nc.vector.wait_ge(ts.h, ts.n)
                    pair(smb[0:1, :], smb2[0:1, :], srcv)
                    vpub()
                    wait(nc.tensor, vs)
                    for c0 in (0, 512):
                        rs = slice(c0, c0 + 512)
                        nc.tensor.matmul(P[pb0 + c0 // 512][:], ones128[:], smb[:, rs],
                                         start=True, stop=False)
                        inst = nc.tensor.matmul(P[pb0 + c0 // 512][:], ones128[:], smb2[:, rs],
                                                start=False, stop=True)
                    tinc(inst, ts)
                wait(nc.vector, ts)
                nc.vector.tensor_copy(bcS[:, 0:512], P[0][:])
                nc.vector.tensor_copy(bcS[:, 512:1024], P[1][:])
                nc.vector.tensor_copy(bcB[:, 0:512], P[2][:])
                nc.vector.tensor_copy(bcB[:, 512:1024], P[3][:])
                vpub()
                for dt in range(NDT):
                    sl = dt % 2
                    nc.sync.wait_ge(vs.h, vs.n)
                    dma(nc.sync, xbuf[sl][:, 0:T], x1s[dt * 128:(dt + 1) * 128, :])
                    wait(nc.vector, ds)
                    nc.vector.tensor_mul(tmp1[:, 0:T], xbuf[sl][:, 0:T], bcS[:])
                    nc.vector.tensor_add(tmp1[:, 0:T], tmp1[:, 0:T], bcB[:])
                    nc.vector.tensor_scalar(tmp1[:, 0:T], tmp1[:, 0:T], gb[:, dt:dt + 1],
                                            gb[:, 16 + dt:17 + dt], op0=ALU.mult, op1=ALU.add)
                    pair(xnh[dt][:], xnl[dt][:], tmp1[:, 0:T])
                    vpub()
                barrier()
                if STOP_AFTER <= 6:
                    finish_early()
                    return nc

                with ExitStack() as sc6:
                    sb6 = lambda name, shape, dt=F32: SB(sc6, name, shape, dt)
                    wfb = [sb6("wfb0", [128, 512]), sb6("wfb1", [128, 512])]
                    wabs = sb6("wabs", [128, 1])
                    thrb = sb6("thrb", [128, 2])
                    tq = sb6("tq", [128, 512])
                    tq2 = sb6("tq2", [128, 512])
                    sc6w = top.enter_context(ExitStack())
                    wabh = SB(sc6w, "wabh", [128, 2048], BF16)
                    wabl = SB(sc6w, "wabl", [128, 2048], BF16)
                    for dt in range(NDT):
                        sl = dt % 2
                        nc.sync.wait_ge(vs.h, vs.n)
                        nc.sync.wait_ge(ss.h, ss.n)
                        dma(nc.sync, xbuf[sl][:], wfT[dt * 128:(dt + 1) * 128, :])
                        wait(nc.scalar, ds)
                        if dt >= 1:
                            nc.scalar.wait_ge(vs.h, vs.n)
                        nc.scalar.activation(tmp1[:], xbuf[sl][:], AF.Abs)
                        spub()
                        wait(nc.vector, ss)
                        if dt >= 1:
                            nc.vector.wait_ge(ts.h, ts.n)
                        pair(wabh[:], wabl[:], tmp1[:])
                        vpub()
                        wait(nc.tensor, vs)
                        for c in range(4):
                            cs = slice(c * 512, (c + 1) * 512)
                            nc.tensor.matmul(P[c][:], ones128[:], wabh[:, cs],
                                             start=(dt == 0), stop=False)
                            inst = nc.tensor.matmul(P[c][:], ones128[:], wabl[:, cs],
                                                    start=False, stop=(dt == NDT - 1))
                        tinc(inst, ts)
                    wait(nc.vector, ts)
                    for c in range(4):
                        nc.vector.tensor_copy(xbuf[1][0:1, c * 512:(c + 1) * 512], P[c][0:1, :])
                    nc.vector.reduce_sum(wabs[0:1, 0:1], xbuf[1][0:1, 0:2048], axis=AXX)
                    vpub()
                    nc.vector.wait_ge(vs.h, vs.n)
                    nc.vector.tensor_scalar(tq2[0:1, 0:1], wabs[0:1, 0:1],
                                            0.5 / (2048.0 * 2048.0), 0.5 * Q_EPS,
                                            op0=ALU.mult, op1=ALU.add)
                    vpub()
                    nc.vector.wait_ge(vs.h, vs.n)
                    nc.vector.tensor_scalar_mul(tq2[0:1, 1:2], tq2[0:1, 0:1], -1.0)
                    vpub()
                    nc.vector.wait_ge(vs.h, vs.n)
                    wait(nc.gpsimd, vs)
                    dma(nc.gpsimd, thrs.ap(), tq2[0:1, 0:2], gd)
                    nc.gpsimd.wait_ge(gd.h, gd.n)
                    dma(nc.gpsimd, thrb[:], thrs.ap().broadcast_to([128, 2]), gd)
                    wait(nc.vector, gd)
                    nc.vector.tensor_copy(tq2[0:1, 4:6], thrb[0:1, 0:2])
                    vpub()
                    nc.vector.wait_ge(vs.h, vs.n)
                    sc6w.close()
                    wqq = [sb6("wqq0", [128, 4096], BF16), sb6("wqq1", [128, 4096], BF16)]
                    if STOP_AFTER <= 6.5:
                        nc.vector.tensor_copy(tmp1[:, 0:2], thrb[:])
                        nc.vector.tensor_copy(tmp1[0:1, 2:3], wabs[0:1, 0:1])
                        vpub()
                        finish_early()
                        return nc

                    used6_mk = [ts.n, ts.n]
                    quant_mk = [vs.n, vs.n]
                    qdone_mk = [vs.n, vs.n]
                    xb6_mk = [dx[0].n, dx[1].n]
                    xb6free_mk = [vs.n, vs.n]
                    evac6_mk = [vs.n, vs.n]
                    st6_mk = gd.n

                    def prep6(hq):
                        b = hq % 2
                        cw = hq * 256
                        nc.vector.wait_ge(ts.h, used6_mk[b])
                        for dt in range(NDT):
                            s2 = dt % 2
                            nc.sync.wait_ge(vs.h, qdone_mk[s2])
                            dma(nc.sync, wfb[s2][:, 0:256],
                                wfT[dt * 128:(dt + 1) * 128, cw:cw + 256], dw[s2])
                            nc.vector.wait_ge(dw[s2].h, dw[s2].n)
                            nc.vector.tensor_single_scalar(tq[:, 0:256], wfb[s2][:, 0:256],
                                                           thrb[:, 0:1], ALU.is_ge)
                            nc.vector.tensor_single_scalar(tq2[:, 0:256], wfb[s2][:, 0:256],
                                                           thrb[:, 1:2], ALU.is_le)
                            nc.vector.tensor_tensor(wqq[b][:, dt * 256:(dt + 1) * 256],
                                                    tq[:, 0:256], tq2[:, 0:256], ALU.subtract)
                            vpub()
                            qdone_mk[s2] = vs.n
                        quant_mk[b] = vs.n

                    prep6(0)
                    for hq in range(8):
                        b = hq % 2
                        if hq < 7:
                            prep6(hq + 1)
                        for ot in range(2):
                            gidx = hq * 2 + ot
                            sl = gidx % 2
                            nc.sync.wait_ge(vs.h, xb6free_mk[sl])
                            dma(nc.sync, xbuf[sl][:, 0:T], x1s[gidx * 128:(gidx + 1) * 128, :],
                                dx[sl])
                            xb6_mk[sl] = dx[sl].n
                        for ot in range(2):
                            gidx = hq * 2 + ot
                            sl = gidx % 2
                            pb = 2 * (gidx % 2)
                            nc.tensor.wait_ge(vs.h, quant_mk[b])
                            nc.tensor.wait_ge(vs.h, evac6_mk[gidx % 2])
                            for dt in range(NDT):
                                wsl = slice(dt * 256 + ot * 128, dt * 256 + (ot + 1) * 128)
                                st = dt == 0
                                sp = dt == NDT - 1
                                for c0 in (0, 512):
                                    rs = slice(c0, c0 + 512)
                                    tg = P[pb + c0 // 512][:]
                                    nc.tensor.matmul(tg, wqq[b][:, wsl], xnh[dt][:, rs],
                                                     start=st, stop=False)
                                    inst = nc.tensor.matmul(tg, wqq[b][:, wsl], xnl[dt][:, rs],
                                                            start=False, stop=sp)
                            tinc(inst, ts)
                            used6_mk[b] = ts.n
                            nc.vector.wait_ge(ts.h, ts.n)
                            nc.vector.wait_ge(dx[sl].h, xb6_mk[sl])
                            nc.vector.wait_ge(gd.h, st6_mk)
                            for c0 in (0, 512):
                                nc.vector.scalar_tensor_tensor(tmp1[:, c0:c0 + 512],
                                                               xbuf[sl][:, c0:c0 + 512],
                                                               gb[:, 32 + gidx:33 + gidx],
                                                               P[pb + c0 // 512][:],
                                                               op0=ALU.add, op1=ALU.add)
                            vpub()
                            evac6_mk[gidx % 2] = vs.n
                            xb6free_mk[sl] = vs.n
                            nc.gpsimd.wait_ge(vs.h, vs.n)
                            dma(nc.gpsimd, y[gidx * 128:(gidx + 1) * 128, :], tmp1[:, 0:T], gd)
                            st6_mk = gd.n
                    nc.gpsimd.wait_ge(gd.h, gd.n)
                    nc.gpsimd.wait_ge(gd.h, gd.n)
    return nc


_CACHE = {}


def _tables():
    pos = np.arange(S, dtype=np.float64)
    inv_freq = 1.0 / (10000.0 ** (np.arange(0, HD, 2, dtype=np.float64) / HD))
    fr = np.outer(pos, inv_freq)
    cos64 = np.cos(fr).T
    sin64 = np.sin(fr).T
    cc = np.vstack([cos64, cos64])
    ssg = np.vstack([-sin64, sin64])
    qs = 1.0 / np.sqrt(128.0)

    def hl(a):
        h = a.astype(np.float32)
        l = (a - h.astype(np.float64)).astype(np.float32)
        return h, l

    return hl(cc), hl(ssg), hl(cc * qs), hl(ssg * qs)


def kernel(L0_attn_hidden_states, L0_attn_Wq, L0_attn_Wk, L0_attn_Wv, L0_attn_Wo,
           L0_ffn_weight, L0_ffn_bias, L0_ffn_ln_gamma, L0_ffn_ln_beta):
    x = np.asarray(L0_attn_hidden_states, dtype=np.float32)
    perm = np.empty(128, dtype=np.int64)
    perm[:64] = np.arange(64) * 2
    perm[64:] = np.arange(64) * 2 + 1
    permq = np.concatenate([h * 128 + perm for h in range(16)])
    permk = np.concatenate([h * 128 + perm for h in range(4)])
    wq_p = np.ascontiguousarray(np.asarray(L0_attn_Wq, np.float32)[:, permq])
    wk_p = np.ascontiguousarray(np.asarray(L0_attn_Wk, np.float32)[:, permk])
    wv = np.ascontiguousarray(np.asarray(L0_attn_Wv, np.float32))
    wo = np.ascontiguousarray(np.asarray(L0_attn_Wo, np.float32))
    wfT = np.ascontiguousarray(np.asarray(L0_ffn_weight, np.float32).T)
    gbb = np.zeros((128, 48), np.float32)
    gbb[:, 0:16] = np.asarray(L0_ffn_ln_gamma, np.float32).reshape(16, 128).T
    gbb[:, 16:32] = np.asarray(L0_ffn_ln_beta, np.float32).reshape(16, 128).T
    gbb[:, 32:48] = np.asarray(L0_ffn_bias, np.float32).reshape(16, 128).T
    eye = np.eye(128, dtype=np.float32)

    (cch, ccl), (ssh, ssl), (qcch, qccl), (qssh, qssl) = _tables()

    if "nc" not in _CACHE:
        _CACHE["nc"] = build_nc()
    nc = _CACHE["nc"]

    in_maps = []
    for c in range(8):
        b, half = c // 2, c % 2
        order = np.r_[half * 1024:half * 1024 + 1024,
                      (1 - half) * 1024:(1 - half) * 1024 + 1024]
        in_maps.append({
            "xT": np.ascontiguousarray(x[b].T[:, order]),
            "wq": wq_p, "wk": wk_p, "wv": wv, "wo": wo, "wfT": wfT,
            "gbb": gbb, "eye": eye,
            "kcch": np.ascontiguousarray(cch[:, order]),
            "kccl": np.ascontiguousarray(ccl[:, order]),
            "kssh": np.ascontiguousarray(ssh[:, order]),
            "kssl": np.ascontiguousarray(ssl[:, order]),
            "qcch": np.ascontiguousarray(qcch[:, order[:1024]]),
            "qccl": np.ascontiguousarray(qccl[:, order[:1024]]),
            "qssh": np.ascontiguousarray(qssh[:, order[:1024]]),
            "qssl": np.ascontiguousarray(qssl[:, order[:1024]]),
        })
    res = run_bass_kernel_spmd(nc, in_maps, core_ids=list(range(8)))
    out = np.empty((4, 2048, 2048), np.float32)
    for c in range(8):
        b, half = c // 2, c % 2
        out[b, half * 1024:(half + 1) * 1024, :] = res.results[c]["y"].T
    return out


# version 16 (final)
# speedup vs baseline: 1.1832x; 1.0044x over previous
import numpy as np
from contextlib import ExitStack

import concourse.bass as bass
import concourse.mybir as mybir
from concourse.bass_utils import run_bass_kernel_spmd

F32 = mybir.dt.float32
F32R = mybir.dt.float32r
BF16 = mybir.dt.bfloat16
AF = mybir.ActivationFunctionType
ALU = mybir.AluOpType
AXX = mybir.AxisListType.X

D = 2048
S = 2048
T = 1024
NDT = 16
HD = 128
INV_D = 1.0 / 2048.0
EPS = 1e-5
Q_EPS = 1e-5


STOP_AFTER = 99


class Sem:
    __slots__ = ("h", "n")

    def __init__(self, h):
        self.h = h
        self.n = 0


def build_nc():
    nc = bass.Bass("TRN2", target_bir_lowering=False, debug=False,
                   detect_race_conditions=False)

    din = lambda name, shape: nc.dram_tensor(name, shape, F32, kind="ExternalInput").ap()
    xT = din("xT", [D, S])
    wq = din("wq", [D, D])
    wk = din("wk", [D, 512])
    wv = din("wv", [D, 512])
    wo = din("wo", [D, D])
    wfT = din("wfT", [D, D])
    gbb = din("gbb", [128, 48])
    kcch = din("kcch", [128, S])
    kccl = din("kccl", [128, S])
    kssh = din("kssh", [128, S])
    kssl = din("kssl", [128, S])
    qcch = din("qcch", [128, T])
    qccl = din("qccl", [128, T])
    qssh = din("qssh", [128, T])
    qssl = din("qssl", [128, T])
    eye = din("eye", [128, 128])
    y = nc.dram_tensor("y", [D, T], F32, kind="ExternalOutput").ap()
    x1s = nc.dram_tensor("x1s", [D, T], F32).ap()
    thrs = nc.dram_tensor("thrs", [1, 2], F32)

    with ExitStack() as top:
        _uid = [0]

        def SB(st_, name, shape, dt=F32):
            _uid[0] += 1
            return st_.enter_context(nc.sbuf_tensor(f"{name}_{_uid[0]}", shape, dt))

        sb = lambda name, shape, dt=F32: SB(top, name, shape, dt)

        P = [top.enter_context(nc.psum_tensor(f"ps{i}", [128, 512], F32))
             for i in range(8)]

        ds = Sem(top.enter_context(nc.semaphore("ds")))
        gd = Sem(top.enter_context(nc.semaphore("gd")))
        ts = Sem(top.enter_context(nc.semaphore("ts")))
        vs = Sem(top.enter_context(nc.semaphore("vs")))
        ss = Sem(top.enter_context(nc.semaphore("ss")))
        gs = Sem(top.enter_context(nc.semaphore("gs")))
        dw = [Sem(top.enter_context(nc.semaphore("dw0"))),
              Sem(top.enter_context(nc.semaphore("dw1")))]
        dx = [Sem(top.enter_context(nc.semaphore("dx0"))),
              Sem(top.enter_context(nc.semaphore("dx1")))]
        bar = Sem(top.enter_context(nc.semaphore("bar")))

        eye_f = sb("eye_f", [128, 128])
        ones_c_bf = sb("ones_c_bf", [128, 1], BF16)
        ones128 = sb("ones128", [128, 128], BF16)
        gb = sb("gb", [128, 48])
        smb = sb("smb", [128, 1024], BF16)
        smb2 = sb("smb2", [128, 1024], BF16)
        pub1 = sb("pub1", [1, 8])
        pub2 = sb("pub2", [1, 8])
        pub3 = sb("pub3", [1, 8])
        pub4 = sb("pub4", [1, 8])
        xbuf = [sb("xbuf0", [128, S]), sb("xbuf1", [128, S])]
        tmp1 = sb("tmp1", [128, S])
        onh = [sb(f"onh{i}", [128, T], BF16) for i in range(16)]
        onl = [sb(f"onl{i}", [128, T], BF16) for i in range(16)]

        top.enter_context(nc.Block())

        ENG = (nc.tensor, nc.vector, nc.scalar, nc.sync, nc.gpsimd)

        def dma(q, out, in_, sem=ds):
            q.dma_start(out=out, in_=in_).then_inc(sem.h, 16)
            sem.n += 16

        def tinc(inst, sem):
            inst.then_inc(sem.h, 1)
            sem.n += 1

        def wait(eng, *sems):
            for s_ in sems:
                eng.wait_ge(s_.h, s_.n)

        def pair(dst_h, dst_l, src):
            nc.vector.tensor_copy(dst_h, src)
            nc.vector.tensor_tensor(dst_l, src, dst_h, ALU.subtract)

        def vpub():
            tinc(nc.vector.tensor_copy(pub1[:], pub2[:]), vs)

        def spub():
            tinc(nc.scalar.copy(pub3[:], pub2[:]), ss)

        def gpub():
            tinc(nc.gpsimd.tensor_copy(pub4[:], pub2[:]), gs)

        def barrier():
            nc.sync.wait_ge(ds.h, ds.n)
            nc.gpsimd.wait_ge(gd.h, gd.n)
            for eng in ENG:
                for s_ in (ds, gd, ts, vs, ss, gs, dw[0], dw[1], dx[0], dx[1]):
                    eng.wait_ge(s_.h, s_.n)

        def finish_early(extra=()):
            barrier()
            dma(nc.gpsimd, y[0:128, :], tmp1[:, 0:T], gd)
            for i, ap_ in enumerate(extra):
                dma(nc.gpsimd, y[(i + 1) * 128:(i + 1) * 128 + ap_.shape[0],
                                 0:ap_.shape[1]], ap_, gd)
            nc.gpsimd.wait_ge(gd.h, gd.n)

        dma(nc.sync, eye_f[:], eye)
        dma(nc.sync, gb[:], gbb)
        wait(nc.vector, ds)
        nc.vector.tensor_scalar_mul(pub1[:], eye_f[0:1, 0:8], 0.0)
        nc.vector.tensor_scalar_mul(pub2[:], eye_f[0:1, 0:8], 0.0)
        nc.vector.tensor_scalar(ones_c_bf[:], eye_f[:, 0:1], 0.0, 1.0,
                                op0=ALU.mult, op1=ALU.add)
        nc.vector.tensor_scalar(ones128[:], eye_f[:], 0.0, 1.0,
                                op0=ALU.mult, op1=ALU.add)
        for c in range(8):
            nc.vector.tensor_scalar_mul(smb[:, c * 128:(c + 1) * 128], eye_f[:], 0.0)
            nc.vector.tensor_scalar_mul(smb2[:, c * 128:(c + 1) * 128], eye_f[:], 0.0)
        vpub()
        if STOP_AFTER <= 0:
            finish_early()
            return nc

        with ExitStack() as scA:
            sbA = lambda name, shape, dt=F32: SB(scA, name, shape, dt)
            bc_s1 = sbA("bc_s1", [128, S])
            bc_negm = sbA("bc_negm", [128, S])
            ck_cc = sbA("ck_cc", [128, S])
            ck_ss = sbA("ck_ss", [128, S])
            cq_cc = sbA("cq_cc", [128, T])
            cq_ss = sbA("cq_ss", [128, T])
            tmp2 = sbA("tmp2", [128, S])
            tmpr = sbA("tmpr", [128, T], BF16)

            with ExitStack() as sc1:
                sb1 = lambda name, shape, dt=F32: SB(sc1, name, shape, dt)
                xh1 = sb1("xh1", [128, S], BF16)
                xl1 = sb1("xl1", [128, S], BF16)
                x2h1 = sb1("x2h1", [128, S], BF16)
                x2l1 = sb1("x2l1", [128, S], BF16)
                sm = sb1("sm", [1, 2048])
                sm2 = sb1("sm2", [1, 2048])
                st_s1 = sb1("st_s1", [1, 2048])
                sumx_tgt = [P[0][0:1, :], P[1][0:1, :], P[2][0:1, :], P[3][0:1, :]]
                sumx2_tgt = [P[4][0:1, :], P[5][0:1, :], P[6][0:1, :], P[7][0:1, :]]
                for dt in range(NDT):
                    sl = dt % 2
                    nc.sync.wait_ge(vs.h, vs.n)
                    nc.sync.wait_ge(ss.h, ss.n)
                    dma(nc.sync, xbuf[sl][:], xT[dt * 128:(dt + 1) * 128, :])
                    wait(nc.scalar, ds)
                    if dt >= 1:
                        nc.scalar.wait_ge(vs.h, vs.n)
                    nc.scalar.activation(tmp2[:], xbuf[sl][:], AF.Square)
                    spub()
                    wait(nc.vector, ds)
                    if dt >= 1:
                        nc.vector.wait_ge(ts.h, ts.n)
                    pair(xh1[:], xl1[:], xbuf[sl][:])
                    wait(nc.vector, ss)
                    pair(x2h1[:], x2l1[:], tmp2[:])
                    vpub()
                    wait(nc.tensor, vs)
                    st = dt == 0
                    sp = dt == NDT - 1
                    for c in range(4):
                        cs = slice(c * 512, (c + 1) * 512)
                        nc.tensor.matmul(P[c][:], ones128[:], xh1[:, cs], start=st, stop=False)
                        nc.tensor.matmul(P[c][:], ones128[:], xl1[:, cs], start=False, stop=sp)
                        nc.tensor.matmul(P[4 + c][:], ones128[:], x2h1[:, cs], start=st, stop=False)
                        inst = nc.tensor.matmul(P[4 + c][:], ones128[:], x2l1[:, cs], start=False, stop=sp)
                    tinc(inst, ts)
                if STOP_AFTER <= 0.3:
                    finish_early()
                    return nc
                wait(nc.vector, ts)
                for c in range(4):
                    nc.vector.tensor_copy(sm[0:1, c * 512:(c + 1) * 512], sumx_tgt[c])
                    nc.vector.tensor_copy(sm2[0:1, c * 512:(c + 1) * 512], sumx2_tgt[c])
                nc.vector.tensor_scalar_mul(sm[:], sm[:], INV_D)
                nc.vector.tensor_scalar_mul(sm2[:], sm2[:], INV_D)
                nc.vector.tensor_mul(st_s1[:], sm[:], sm[:])
                nc.vector.tensor_sub(sm2[:], sm2[:], st_s1[:])
                nc.vector.tensor_scalar_add(sm2[:], sm2[:], EPS)
                vpub()
                wait(nc.scalar, vs)
                nc.scalar.activation(st_s1[:], sm2[:], AF.Ln)
                nc.scalar.activation(st_s1[:], st_s1[:], AF.Exp, scale=-0.5)
                spub()
                wait(nc.vector, ss)
                nc.vector.tensor_scalar_mul(sm[:], sm[:], -1.0)
                vpub()
                if STOP_AFTER <= 0.6:
                    finish_early()
                    return nc

                for (srcv, dstt) in ((st_s1, bc_s1), (sm, bc_negm)):
                    for half in range(2):
                        hs = slice(half * 1024, (half + 1) * 1024)
                        pair(smb[0:1, :], smb2[0:1, :], srcv[0:1, hs])
                        vpub()
                        wait(nc.tensor, vs)
                        for c in range(2):
                            s0 = slice(c * 512, (c + 1) * 512)
                            nc.tensor.matmul(P[c][:], ones128[:], smb[:, s0], start=True, stop=False)
                            inst = nc.tensor.matmul(P[c][:], ones128[:], smb2[:, s0], start=False, stop=True)
                        tinc(inst, ts)
                        wait(nc.vector, ts)
                        for c in range(2):
                            nc.vector.tensor_copy(
                                dstt[:, half * 1024 + c * 512:half * 1024 + (c + 1) * 512],
                                P[c][:])
                        vpub()

                if STOP_AFTER <= 0.8:
                    finish_early([bc_s1[:, 0:1024], bc_negm[:, 0:1024],
                                  smb[:, 0:512].bitcast(F32), st_s1[0:1, 0:1024]])
                    return nc
                for (hi_t, lo_t, dst, w, scl) in (
                        (kcch, kccl, ck_cc, S, bc_s1), (kssh, kssl, ck_ss, S, bc_s1),
                        (qcch, qccl, cq_cc, T, bc_s1), (qssh, qssl, cq_ss, T, bc_s1)):
                    nc.sync.wait_ge(vs.h, vs.n)
                    dma(nc.sync, tmp1[:, 0:w], hi_t)
                    dma(nc.sync, tmp2[:, 0:w], lo_t)
                    wait(nc.vector, ds)
                    nc.vector.tensor_add(tmp1[:, 0:w], tmp1[:, 0:w], tmp2[:, 0:w])
                    nc.vector.tensor_mul(dst[:], tmp1[:, 0:w], scl[:, 0:w])
                    vpub()
            barrier()
            if STOP_AFTER <= 1:
                finish_early([ck_cc[:, 0:1024], ck_ss[:, 0:1024],
                              cq_cc[:, 0:1024], cq_ss[:, 0:1024]])
                return nc

            with ExitStack() as sc2:
                sb2 = lambda name, shape, dt=F32: SB(sc2, name, shape, dt)
                kresh = sb2("kresh", [128, S], BF16)
                kresl = sb2("kresl", [128, S], BF16)
                vth = sb2("vth", [128, S], BF16)
                vtl = sb2("vtl", [128, S], BF16)
                phi = [sb2("phi0", [128, T], BF16), sb2("phi1", [128, T], BF16)]
                plo = [sb2("plo0", [128, T], BF16), sb2("plo1", [128, T], BF16)]
                pfull = [sb2("pf0", [128, T]), sb2("pf1", [128, T])]
                invv = sb2("invv", [1, 1024])
                csKV = sb2("csKV", [128, 2])
                csQ = sb2("csQ", [128, 4])
                negmh = sb2("negmh", [128, 4])
                k2ln = sb2("k2ln", [1, 32])

                for g in range(4):
                    with ExitStack() as sckv:
                        sbk = lambda name, shape, dt=F32: SB(sckv, name, shape, dt)
                        wkh = sbk("wkh", [128, 2048], BF16)
                        wkl = sbk("wkl", [128, 2048], BF16)
                        wvh = sbk("wvh", [128, 2048], BF16)
                        wvl = sbk("wvl", [128, 2048], BF16)
                        xph = sbk("xph", [128, 2048], BF16)
                        xpl = sbk("xpl", [128, 2048], BF16)
                        nc.sync.wait_ge(ts.h, ts.n)
                        for dt in range(NDT):
                            dma(nc.sync, xbuf[0][:, dt * 128:(dt + 1) * 128],
                                wk[dt * 128:(dt + 1) * 128, g * 128:(g + 1) * 128], dw[0])
                            dma(nc.sync, xbuf[1][:, dt * 128:(dt + 1) * 128],
                                wv[dt * 128:(dt + 1) * 128, g * 128:(g + 1) * 128], dw[0])
                        wait(nc.vector, dw[0])
                        pair(wkh[:], wkl[:], xbuf[0][:])
                        pair(wvh[:], wvl[:], xbuf[1][:])
                        vpub()
                        wait(nc.tensor, vs)
                        for dt in range(NDT):
                            st = dt == 0
                            sp = dt == NDT - 1
                            w0 = slice(dt * 128, (dt + 1) * 128)
                            nc.tensor.matmul(P[6][:, 0:1], wkh[:, w0], ones_c_bf[:], start=st, stop=False)
                            nc.tensor.matmul(P[6][:, 0:1], wkl[:, w0], ones_c_bf[:], start=False, stop=sp)
                            nc.tensor.matmul(P[7][:, 0:1], wvh[:, w0], ones_c_bf[:], start=st, stop=False)
                            inst = nc.tensor.matmul(P[7][:, 0:1], wvl[:, w0], ones_c_bf[:], start=False, stop=sp)
                        tinc(inst, ts)
                        wait(nc.vector, ts)
                        nc.vector.tensor_copy(csKV[:, 0:1], P[6][:, 0:1])
                        nc.vector.tensor_copy(csKV[:, 1:2], P[7][:, 0:1])
                        vpub()

                        kt_tgt = [P[0][:], P[1][:], P[2][:], P[3][:]]
                        vt_tgt = [P[4][:], P[5][:], P[6][:], P[7][:]]
                        slot_pair = [vs.n, vs.n]
                        slot_mm = [ts.n, ts.n]
                        for dt in range(NDT):
                            sl = dt % 2
                            nc.sync.wait_ge(vs.h, slot_pair[sl])
                            dma(nc.sync, xbuf[sl][:], xT[dt * 128:(dt + 1) * 128, :], dx[sl])
                            nc.vector.wait_ge(dx[sl].h, dx[sl].n)
                            nc.vector.wait_ge(ts.h, ts.n)
                            pair(xph[:], xpl[:], xbuf[sl][:])
                            vpub()
                            slot_pair[sl] = vs.n
                            wait(nc.tensor, vs)
                            st = dt == 0
                            sp = dt == NDT - 1
                            w0 = slice(dt * 128, (dt + 1) * 128)
                            for c in range(4):
                                cs = slice(c * 512, (c + 1) * 512)
                                nc.tensor.matmul(kt_tgt[c], wkh[:, w0], xph[:, cs], start=st, stop=False)
                                nc.tensor.matmul(kt_tgt[c], wkh[:, w0], xpl[:, cs], start=False, stop=False)
                                nc.tensor.matmul(kt_tgt[c], wkl[:, w0], xph[:, cs], start=False, stop=sp)
                                nc.tensor.matmul(vt_tgt[c], wvh[:, w0], xph[:, cs], start=st, stop=False)
                                nc.tensor.matmul(vt_tgt[c], wvh[:, w0], xpl[:, cs], start=False, stop=False)
                                inst = nc.tensor.matmul(vt_tgt[c], wvl[:, w0], xph[:, cs], start=False, stop=sp)
                            tinc(inst, ts)
                            slot_mm[sl] = ts.n

                        wait(nc.vector, ts)
                        for c in range(4):
                            cs = slice(c * 512, (c + 1) * 512)
                            nc.vector.scalar_tensor_tensor(tmp1[:, cs], bc_negm[:, cs], csKV[:, 0:1],
                                                           kt_tgt[c], op0=ALU.mult, op1=ALU.add)
                        vpub()
                        wait(nc.gpsimd, vs)
                        dma(nc.gpsimd, tmp2[0:64, :], tmp1[64:128, :], gd)
                        dma(nc.gpsimd, tmp2[64:128, :], tmp1[0:64, :], gd)
                        wait(nc.vector, gd)
                        nc.vector.tensor_mul(tmp1[:], tmp1[:], ck_cc[:])
                        nc.vector.tensor_mul(tmp2[:], tmp2[:], ck_ss[:])
                        nc.vector.tensor_add(tmp1[:], tmp1[:], tmp2[:])
                        pair(kresh[:], kresl[:], tmp1[:])
                        vpub()
                        wait(nc.scalar, vs)
                        for half in range(2):
                            hs = slice(half * 1024, (half + 1) * 1024)
                            if half == 1:
                                nc.scalar.wait_ge(ts.h, ts.n)
                            nc.scalar.activation(tmpr[:], tmp1[:, hs], AF.Square)
                            spub()
                            wait(nc.tensor, ss)
                            tg = [kt_tgt[2 * half], kt_tgt[2 * half + 1]]
                            nc.tensor.matmul(tg[0], ones128[:], tmpr[:, 0:512], start=True, stop=True)
                            inst = nc.tensor.matmul(tg[1], ones128[:], tmpr[:, 512:1024],
                                                    start=True, stop=True)
                            tinc(inst, ts)
                        wait(nc.vector, ts)
                        for c in range(4):
                            nc.vector.tensor_copy(xbuf[1][0:1, c * 512:(c + 1) * 512],
                                                  kt_tgt[c][0:1, :])
                        nc.vector.reduce_max(k2ln[0:1, 0:1], xbuf[1][0:1, 0:2048], axis=AXX)
                        vpub()
                        wait(nc.scalar, vs)
                        nc.scalar.activation(k2ln[0:1, 1:2], k2ln[0:1, 0:1], AF.Ln)
                        spub()

                        wait(nc.vector, ts)
                        vn = xbuf[0]
                        for c in range(4):
                            cs = slice(c * 512, (c + 1) * 512)
                            nc.vector.scalar_tensor_tensor(vn[:, cs], bc_negm[:, cs], csKV[:, 1:2],
                                                           vt_tgt[c], op0=ALU.mult, op1=ALU.add)
                        nc.vector.tensor_mul(vn[:], vn[:], bc_s1[:])
                        vpub()
                        wait(nc.tensor, vs)
                        for k in range(16):
                            tg = P[6] if k % 2 == 0 else P[7]
                            if k >= 2:
                                nc.tensor.wait_ge(vs.h, vs.n)
                            inst = nc.tensor.transpose(tg[:, 0:128], vn[:, k * 128:(k + 1) * 128], eye_f[:])
                            tinc(inst, ts)
                            wait(nc.vector, ts)
                            ksl = slice(k * 128, (k + 1) * 128)
                            nc.vector.tensor_copy(vth[:, ksl], tg[:, 0:128])
                            nc.vector.tensor_tensor(vtl[:, ksl], tg[:, 0:128], vth[:, ksl], ALU.subtract)
                            vpub()

                    if STOP_AFTER <= 2 and g == 0:
                        finish_early([csKV[:, 0:2], k2ln[0:1, 0:32],
                                      xbuf[0][:, 0:1024]])
                        return nc
                    with ExitStack() as scq:
                        sbq = lambda name, shape, dt=F32: SB(scq, name, shape, dt)
                        wqs = [sbq("wqs0", [128, 512]), sbq("wqs1", [128, 512])]
                        wqph = [sbq("wqph0", [128, 512], BF16), sbq("wqph1", [128, 512], BF16)]
                        wqpl = [sbq("wqpl0", [128, 512], BF16), sbq("wqpl1", [128, 512], BF16)]
                        xoph = sbq("xoph", [128, T], BF16)
                        xopl = sbq("xopl", [128, T], BF16)
                        qhh = [sbq(f"qhh{i}", [128, T], BF16) for i in range(4)]
                        qll = [sbq(f"qll{i}", [128, T], BF16) for i in range(4)]

                        cs_tgt = [P[0][:, 0:1], P[2][:, 0:1], P[4][:, 0:1], P[6][:, 0:1]]
                        slot_pair = [vs.n, vs.n]
                        slot_mm = [ts.n, ts.n]
                        for dt in range(NDT):
                            sl = dt % 2
                            nc.sync.wait_ge(vs.h, slot_pair[sl])
                            dma(nc.sync, wqs[sl][:], wq[dt * 128:(dt + 1) * 128, g * 512:(g + 1) * 512],
                                dw[sl])
                            nc.vector.wait_ge(dw[sl].h, dw[sl].n)
                            nc.vector.wait_ge(ts.h, slot_mm[sl])
                            pair(wqph[sl][:], wqpl[sl][:], wqs[sl][:])
                            vpub()
                            slot_pair[sl] = vs.n
                            wait(nc.tensor, vs)
                            st = dt == 0
                            sp = dt == NDT - 1
                            for h in range(4):
                                hsl = slice(h * 128, (h + 1) * 128)
                                nc.tensor.matmul(cs_tgt[h], wqph[sl][:, hsl], ones_c_bf[:], start=st, stop=False)
                                inst = nc.tensor.matmul(cs_tgt[h], wqpl[sl][:, hsl], ones_c_bf[:],
                                                        start=False, stop=sp)
                            tinc(inst, ts)
                            slot_mm[sl] = ts.n
                        wait(nc.vector, ts)
                        for h in range(4):
                            nc.vector.tensor_copy(csQ[:, h:h + 1], cs_tgt[h])
                        vpub()

                        slot_pair = [vs.n, vs.n]
                        slot_mm = [ts.n, ts.n]
                        for dt in range(NDT):
                            sl = dt % 2
                            nc.sync.wait_ge(vs.h, slot_pair[sl])
                            dma(nc.sync, wqs[sl][:], wq[dt * 128:(dt + 1) * 128, g * 512:(g + 1) * 512],
                                dw[sl])
                            dma(nc.sync, xbuf[sl][:, 0:T], xT[dt * 128:(dt + 1) * 128, 0:T],
                                dx[sl])
                            nc.vector.wait_ge(dw[sl].h, dw[sl].n)
                            nc.vector.wait_ge(dx[sl].h, dx[sl].n)
                            nc.vector.wait_ge(ts.h, ts.n)
                            pair(wqph[sl][:], wqpl[sl][:], wqs[sl][:])
                            pair(xoph[:], xopl[:], xbuf[sl][:, 0:T])
                            vpub()
                            slot_pair[sl] = vs.n
                            wait(nc.tensor, vs)
                            st = dt == 0
                            sp = dt == NDT - 1
                            for h in range(4):
                                hsl = slice(h * 128, (h + 1) * 128)
                                for c0 in (0, 512):
                                    rs = slice(c0, c0 + 512)
                                    tgt = P[2 * h + c0 // 512][:]
                                    nc.tensor.matmul(tgt, wqph[sl][:, hsl], xoph[:, rs], start=st, stop=False)
                                    nc.tensor.matmul(tgt, wqph[sl][:, hsl], xopl[:, rs], start=False, stop=False)
                                    inst = nc.tensor.matmul(tgt, wqpl[sl][:, hsl], xoph[:, rs],
                                                            start=False, stop=sp)
                            tinc(inst, ts)
                            slot_mm[sl] = ts.n

                        wait(nc.vector, ts)
                        for h in range(4):
                            segs = [(P[2 * h][:], 0, 512), (P[2 * h + 1][:], 512, 1024)]
                            own = [P[2 * h][0:1, :], P[2 * h + 1][0:1, :]]
                            for (ap_, a, b) in segs:
                                nc.vector.scalar_tensor_tensor(tmp1[:, a:b], bc_negm[:, a:b],
                                                               csQ[:, h:h + 1], ap_,
                                                               op0=ALU.mult, op1=ALU.add)
                            vpub()
                            wait(nc.gpsimd, vs)
                            dma(nc.gpsimd, tmp2[0:64, 0:T], tmp1[64:128, 0:T], gd)
                            dma(nc.gpsimd, tmp2[64:128, 0:T], tmp1[0:64, 0:T], gd)
                            wait(nc.vector, gd)
                            nc.vector.tensor_mul(tmp1[:, 0:T], tmp1[:, 0:T], cq_cc[:])
                            nc.vector.tensor_mul(tmp2[:, 0:T], tmp2[:, 0:T], cq_ss[:])
                            nc.vector.tensor_add(tmp1[:, 0:T], tmp1[:, 0:T], tmp2[:, 0:T])
                            pair(qhh[h][:], qll[h][:], tmp1[:, 0:T])
                            vpub()
                            wait(nc.scalar, vs)
                            nc.scalar.activation(tmpr[:], tmp1[:, 0:T], AF.Square)
                            spub()
                            wait(nc.tensor, ss, vs)
                            nc.tensor.matmul(P[2 * h][:], ones128[:], tmpr[:, 0:512], start=True, stop=True)
                            inst = nc.tensor.matmul(P[2 * h + 1][:], ones128[:], tmpr[:, 512:1024],
                                                    start=True, stop=True)
                            tinc(inst, ts)
                            wait(nc.vector, ts)
                            nc.vector.tensor_copy(invv[0:1, 0:512], own[0])
                            nc.vector.tensor_copy(invv[0:1, 512:1024], own[1])
                            nc.vector.reduce_max(k2ln[0:1, 4 + h:5 + h], invv[:], axis=AXX)
                            vpub()
                            if STOP_AFTER <= 2.5 and g == 0 and h == 0:
                                nc.vector.tensor_copy(xbuf[1][:, 0:1024], tmpr[:])
                                nc.vector.tensor_copy(xbuf[1][:, 1024:2048], qhh[0][:])
                                nc.vector.tensor_copy(xbuf[0][0:1, 0:512], own[0])
                                vpub()
                                finish_early([xbuf[1][:, 0:1024],
                                              xbuf[1][:, 1024:2048],
                                              k2ln[0:1, 0:32],
                                              xbuf[0][0:1, 0:512]])
                                return nc

                        wait(nc.scalar, vs)
                        nc.scalar.activation(k2ln[0:1, 8:12], k2ln[0:1, 4:8], AF.Ln)
                        spub()
                        wait(nc.vector, ss)
                        nc.vector.tensor_single_scalar(k2ln[0:1, 8:12], k2ln[0:1, 8:12],
                                                       k2ln[0:1, 1:2], ALU.add)
                        vpub()
                        wait(nc.scalar, vs)
                        nc.scalar.activation(k2ln[0:1, 8:12], k2ln[0:1, 8:12], AF.Exp, scale=0.5)
                        spub()
                        wait(nc.vector, ss)
                        nc.vector.tensor_scalar_mul(smb[0:1, 0:4], k2ln[0:1, 8:12], -1.0)
                        vpub()
                        wait(nc.tensor, vs)
                        tinc(nc.tensor.matmul(P[6][:, 0:4], ones128[:], smb[:, 0:4],
                                              start=True, stop=True), ts)
                        wait(nc.vector, ts)
                        nc.vector.tensor_copy(negmh[:], P[6][:, 0:4])
                        vpub()
                        nc.tensor.wait_ge(vs.h, vs.n)

                        for h in range(4):
                            exp_mark = [ss.n, ss.n]
                            lo_mark = [vs.n, vs.n]
                            lo_used = [ts.n, ts.n]

                            def emit_lo(jj):
                                kwj = slice(jj * 128, (jj + 1) * 128)
                                pjj = jj % 2
                                nc.tensor.wait_ge(vs.h, lo_mark[pjj])
                                nc.tensor.matmul(P[6][:], ones128[:], plo[pjj][:, 0:512],
                                                 start=False, stop=(jj == 15))
                                nc.tensor.matmul(P[7][:], ones128[:], plo[pjj][:, 512:1024],
                                                 start=False, stop=(jj == 15))
                                for c0 in (0, 512):
                                    rs = slice(c0, c0 + 512)
                                    nc.tensor.matmul(P[4 + c0 // 512][:], vth[:, kwj],
                                                     plo[pjj][:, rs], start=False, stop=False)
                                    inst = nc.tensor.matmul(P[4 + c0 // 512][:], vtl[:, kwj],
                                                            phi[pjj][:, rs], start=False,
                                                            stop=(jj == 15))
                                tinc(inst, ts)
                                lo_used[pjj] = ts.n

                            sc_mark = [ts.n, ts.n]

                            def emit_scores(jj):
                                pjj = jj % 2
                                kwj = slice(jj * 128, (jj + 1) * 128)
                                nc.tensor.wait_ge(ss.h, exp_mark[pjj])
                                for c0 in (0, 512):
                                    rs = slice(c0, c0 + 512)
                                    tg = P[2 * pjj + c0 // 512][:]
                                    nc.tensor.matmul(tg, kresh[:, kwj], qhh[h][:, rs],
                                                     start=True, stop=False)
                                    nc.tensor.matmul(tg, kresh[:, kwj], qll[h][:, rs],
                                                     start=False, stop=False)
                                    inst = nc.tensor.matmul(tg, kresl[:, kwj], qhh[h][:, rs],
                                                            start=False, stop=True)
                                tinc(inst, ts)
                                sc_mark[pjj] = ts.n

                            emit_scores(0)
                            for j in range(16):
                                pj = j % 2
                                kw = slice(j * 128, (j + 1) * 128)
                                if j + 1 < 16:
                                    emit_scores(j + 1)
                                nc.scalar.wait_ge(ts.h, sc_mark[pj])
                                nc.scalar.wait_ge(ts.h, lo_used[pj])
                                for c0 in (0, 512):
                                    rs = slice(c0, c0 + 512)
                                    nc.scalar.activation(phi[pj][:, rs], P[2 * pj + c0 // 512][:],
                                                         AF.Exp, bias=negmh[:, h:h + 1])
                                    nc.scalar.activation(pfull[pj][:, rs], P[2 * pj + c0 // 512][:],
                                                         AF.Exp, bias=negmh[:, h:h + 1])
                                spub()
                                exp_mark[pj] = ss.n
                                nc.vector.wait_ge(ss.h, ss.n)
                                nc.vector.wait_ge(ts.h, lo_used[pj])
                                nc.vector.tensor_tensor(plo[pj][:], pfull[pj][:], phi[pj][:],
                                                        ALU.subtract)
                                vpub()
                                lo_mark[pj] = vs.n
                                if j >= 2:
                                    emit_lo(j - 1)
                                nc.tensor.wait_ge(ss.h, exp_mark[pj])
                                stj = j == 0
                                nc.tensor.matmul(P[6][:], ones128[:], phi[pj][:, 0:512],
                                                 start=stj, stop=False)
                                nc.tensor.matmul(P[7][:], ones128[:], phi[pj][:, 512:1024],
                                                 start=stj, stop=False)
                                for c0 in (0, 512):
                                    rs = slice(c0, c0 + 512)
                                    inst = nc.tensor.matmul(P[4 + c0 // 512][:], vth[:, kw],
                                                            phi[pj][:, rs], start=stj, stop=False)
                                tinc(inst, ts)
                                if j == 1:
                                    emit_lo(0)
                            emit_lo(15)
                            wait(nc.vector, ts)
                            nc.vector.tensor_copy(invv[0:1, 0:512], P[6][0:1, :])
                            nc.vector.tensor_copy(invv[0:1, 512:1024], P[7][0:1, :])
                            nc.vector.reciprocal(invv[:], invv[:])
                            pair(smb[0:1, :], smb2[0:1, :], invv[:])
                            vpub()
                            nc.tensor.wait_ge(vs.h, vs.n)
                            nc.tensor.wait_ge(ss.h, ss.n)
                            for c0 in (0, 512):
                                rs = slice(c0, c0 + 512)
                                nc.tensor.matmul(P[c0 // 512][:], ones128[:], smb[:, rs],
                                                 start=True, stop=False)
                                inst = nc.tensor.matmul(P[c0 // 512][:], ones128[:], smb2[:, rs],
                                                        start=False, stop=True)
                            tinc(inst, ts)
                            wait(nc.vector, ts)
                            nc.vector.tensor_copy(tmp1[:, 1024:1536], P[0][:])
                            nc.vector.tensor_copy(tmp1[:, 1536:2048], P[1][:])
                            nc.vector.tensor_mul(tmp1[:, 0:512], P[4][:], tmp1[:, 1024:1536])
                            nc.vector.tensor_mul(tmp1[:, 512:1024], P[5][:], tmp1[:, 1536:2048])
                            pair(onh[g * 4 + h][:], onl[g * 4 + h][:], tmp1[:, 0:1024])
                            vpub()
                            nc.tensor.wait_ge(vs.h, vs.n)
                    if STOP_AFTER <= 3 and g == 0:
                        finish_early([negmh[:, 0:4], k2ln[0:1, 0:32], invv[:],
                                      csQ[:, 0:4]])
                        return nc
        barrier()
        if STOP_AFTER <= 4:
            finish_early()
            return nc

        with ExitStack() as sc4:
            sb4 = lambda name, shape, dt=F32: SB(sc4, name, shape, dt)
            wos = [sb4("wos0", [128, 256]), sb4("wos1", [128, 256])]
            woh = [sb4("woh0", [128, 4096], BF16), sb4("woh1", [128, 4096], BF16)]
            wol = [sb4("wol0", [128, 4096], BF16), sb4("wol1", [128, 4096], BF16)]
            used_mk = [ts.n, ts.n]
            prep_mk = [vs.n, vs.n]
            pdone_mk = [vs.n, vs.n]
            xb_mk = [ds.n, ds.n]
            xbfree_mk = [vs.n, vs.n]
            evac_mk = [vs.n, vs.n]
            st_mk = gd.n

            def prep4(hc):
                b = hc % 2
                cw = hc * 256
                nc.vector.wait_ge(ts.h, used_mk[b])
                for kt in range(NDT):
                    s2 = kt % 2
                    nc.sync.wait_ge(vs.h, pdone_mk[s2])
                    dma(nc.sync, wos[s2][:], wo[kt * 128:(kt + 1) * 128, cw:cw + 256],
                        dw[s2])
                    nc.vector.wait_ge(dw[s2].h, dw[s2].n)
                    pair(woh[b][:, kt * 256:(kt + 1) * 256],
                         wol[b][:, kt * 256:(kt + 1) * 256], wos[s2][:])
                    vpub()
                    pdone_mk[s2] = vs.n
                prep_mk[b] = vs.n

            prep4(0)
            for hc in range(8):
                b = hc % 2
                if hc < 7:
                    prep4(hc + 1)
                for ot in range(2):
                    gidx = hc * 2 + ot
                    sl = gidx % 2
                    nc.sync.wait_ge(vs.h, xbfree_mk[sl])
                    dma(nc.sync, xbuf[sl][:, 0:T], xT[gidx * 128:(gidx + 1) * 128, 0:T],
                        dx[sl])
                    xb_mk[sl] = dx[sl].n
                for ot in range(2):
                    gidx = hc * 2 + ot
                    sl = gidx % 2
                    pb = 2 * (gidx % 2)
                    nc.tensor.wait_ge(vs.h, prep_mk[b])
                    nc.tensor.wait_ge(vs.h, evac_mk[gidx % 2])
                    for kt in range(NDT):
                        wsl = slice(kt * 256 + ot * 128, kt * 256 + (ot + 1) * 128)
                        st = kt == 0
                        sp = kt == NDT - 1
                        for c0 in (0, 512):
                            rs = slice(c0, c0 + 512)
                            tg = P[pb + c0 // 512][:]
                            nc.tensor.matmul(tg, woh[b][:, wsl], onh[kt][:, rs], start=st, stop=False)
                            nc.tensor.matmul(tg, woh[b][:, wsl], onl[kt][:, rs], start=False, stop=False)
                            inst = nc.tensor.matmul(tg, wol[b][:, wsl], onh[kt][:, rs],
                                                    start=False, stop=sp)
                    tinc(inst, ts)
                    used_mk[b] = ts.n
                    nc.vector.wait_ge(ts.h, ts.n)
                    nc.vector.wait_ge(dx[sl].h, xb_mk[sl])
                    nc.vector.wait_ge(gd.h, st_mk)
                    nc.vector.tensor_add(tmp1[:, 0:512], P[pb][:], xbuf[sl][:, 0:512])
                    nc.vector.tensor_add(tmp1[:, 512:1024], P[pb + 1][:], xbuf[sl][:, 512:1024])
                    vpub()
                    evac_mk[gidx % 2] = vs.n
                    xbfree_mk[sl] = vs.n
                    nc.gpsimd.wait_ge(vs.h, vs.n)
                    dma(nc.gpsimd, x1s[gidx * 128:(gidx + 1) * 128, :], tmp1[:, 0:T], gd)
                    st_mk = gd.n
            if STOP_AFTER <= 4.7:
                nc.vector.wait_ge(ts.h, ts.n)
                nc.vector.wait_ge(gd.h, gd.n)
                nc.vector.tensor_copy(tmp1[:, 0:1024], wol[1][:, 2048:3072])
                nc.vector.tensor_copy(tmp1[:, 1024:2048], wol[1][:, 3072:4096])
                nc.vector.tensor_copy(xbuf[0][:, 0:1024], woh[1][:, 2048:3072])
                nc.vector.tensor_copy(xbuf[0][:, 1024:2048], woh[1][:, 3072:4096])
                nc.vector.tensor_copy(xbuf[1][:, 0:1024], wol[1][:, 0:1024])
                nc.vector.tensor_copy(xbuf[1][:, 1024:2048], wol[1][:, 1024:2048])
                vpub()
                finish_early([tmp1[:, 1024:2048], xbuf[0][:, 0:1024],
                              xbuf[0][:, 1024:2048], xbuf[1][:, 0:1024],
                              xbuf[1][:, 1024:2048]])
                return nc
        barrier()
        if STOP_AFTER <= 5:
            finish_early()
            return nc
        if STOP_AFTER <= 5.5:
            dma(nc.gpsimd, y[:, :], x1s, gd)
            nc.gpsimd.wait_ge(gd.h, gd.n)
            return nc

        with ExitStack() as sc56:
            sb56 = lambda name, shape, dt=F32: SB(sc56, name, shape, dt)
            xnh = [sb56(f"xnh{i}", [128, T], BF16) for i in range(16)]
            xnl = [sb56(f"xnl{i}", [128, T], BF16) for i in range(16)]

            with ExitStack() as sc5:
                sb5 = lambda name, shape, dt=F32: SB(sc5, name, shape, dt)
                xh5 = sb5("xh5", [128, T], BF16)
                xl5 = sb5("xl5", [128, T], BF16)
                x2h5 = sb5("x2h5", [128, T], BF16)
                x2l5 = sb5("x2l5", [128, T], BF16)
                sm5 = sb5("sm5", [1, 2048])
                bcS = sb5("bcS", [128, T])
                bcB = sb5("bcB", [128, T])
                for dt in range(NDT):
                    sl = dt % 2
                    nc.sync.wait_ge(vs.h, vs.n)
                    nc.sync.wait_ge(ss.h, ss.n)
                    dma(nc.sync, xbuf[sl][:, 0:T], x1s[dt * 128:(dt + 1) * 128, :])
                    wait(nc.scalar, ds)
                    if dt >= 1:
                        nc.scalar.wait_ge(vs.h, vs.n)
                    nc.scalar.activation(tmp1[:, 0:T], xbuf[sl][:, 0:T], AF.Square)
                    spub()
                    wait(nc.vector, ds)
                    if dt >= 1:
                        nc.vector.wait_ge(ts.h, ts.n)
                    pair(xh5[:], xl5[:], xbuf[sl][:, 0:T])
                    wait(nc.vector, ss)
                    pair(x2h5[:], x2l5[:], tmp1[:, 0:T])
                    vpub()
                    wait(nc.tensor, vs)
                    st = dt == 0
                    sp = dt == NDT - 1
                    for (tgt, hh, ll) in ((P[0][:], xh5[:, 0:512], xl5[:, 0:512]),
                                          (P[1][:], xh5[:, 512:1024], xl5[:, 512:1024]),
                                          (P[2][:], x2h5[:, 0:512], x2l5[:, 0:512]),
                                          (P[3][:], x2h5[:, 512:1024], x2l5[:, 512:1024])):
                        nc.tensor.matmul(tgt, ones128[:], hh, start=st, stop=False)
                        inst = nc.tensor.matmul(tgt, ones128[:], ll, start=False, stop=sp)
                    tinc(inst, ts)
                wait(nc.vector, ts)
                m1 = sm5[0:1, 0:1024]
                vv = sm5[0:1, 1024:2048]
                nc.vector.tensor_copy(m1[0:1, 0:512], P[0][0:1, :])
                nc.vector.tensor_copy(m1[0:1, 512:1024], P[1][0:1, :])
                nc.vector.tensor_copy(vv[0:1, 0:512], P[2][0:1, :])
                nc.vector.tensor_copy(vv[0:1, 512:1024], P[3][0:1, :])
                nc.vector.tensor_scalar_mul(m1, m1, INV_D)
                nc.vector.tensor_scalar_mul(vv, vv, INV_D)
                tS = bcS[0:1, 0:1024]
                tB = bcB[0:1, 0:1024]
                nc.vector.tensor_mul(tS, m1, m1)
                nc.vector.tensor_sub(vv, vv, tS)
                nc.vector.tensor_scalar_add(tS, vv, EPS)
                nc.vector.reciprocal(tB, tS)
                nc.vector.tensor_mul(vv, vv, tB)
                nc.vector.tensor_scalar_add(vv, vv, EPS)
                vpub()
                wait(nc.scalar, vs)
                nc.scalar.activation(tS, tS, AF.Ln)
                nc.scalar.activation(tS, tS, AF.Exp, scale=-0.5)
                nc.scalar.activation(vv, vv, AF.Ln)
                nc.scalar.activation(vv, vv, AF.Exp, scale=-0.5)
                spub()
                wait(nc.vector, ss)
                nc.vector.tensor_mul(tS, tS, vv)
                nc.vector.tensor_mul(m1, m1, tS)
                nc.vector.tensor_scalar_mul(m1, m1, -1.0)
                vpub()
                for (srcv, pb0) in ((tS, 0), (m1, 2)):
                    nc.vector.wait_ge(ts.h, ts.n)
                    pair(smb[0:1, :], smb2[0:1, :], srcv)
                    vpub()
                    wait(nc.tensor, vs)
                    for c0 in (0, 512):
                        rs = slice(c0, c0 + 512)
                        nc.tensor.matmul(P[pb0 + c0 // 512][:], ones128[:], smb[:, rs],
                                         start=True, stop=False)
                        inst = nc.tensor.matmul(P[pb0 + c0 // 512][:], ones128[:], smb2[:, rs],
                                                start=False, stop=True)
                    tinc(inst, ts)
                wait(nc.vector, ts)
                nc.vector.tensor_copy(bcS[:, 0:512], P[0][:])
                nc.vector.tensor_copy(bcS[:, 512:1024], P[1][:])
                nc.vector.tensor_copy(bcB[:, 0:512], P[2][:])
                nc.vector.tensor_copy(bcB[:, 512:1024], P[3][:])
                vpub()
                if STOP_AFTER <= 5.9:
                    nc.vector.tensor_copy(tmp1[:, 0:1024], bcS[:])
                    nc.vector.tensor_copy(xbuf[1][:, 0:1024], bcB[:])
                    nc.vector.tensor_copy(xbuf[1][0:1, 1024:2048], sm5[0:1, 1024:2048])
                    vpub()
                    finish_early([xbuf[1][:, 0:1024], xbuf[1][0:1, 1024:2048]])
                    return nc
                for dt in range(NDT):
                    sl = dt % 2
                    nc.sync.wait_ge(vs.h, vs.n)
                    dma(nc.sync, xbuf[sl][:, 0:T], x1s[dt * 128:(dt + 1) * 128, :])
                    wait(nc.vector, ds)
                    nc.vector.tensor_mul(tmp1[:, 0:T], xbuf[sl][:, 0:T], bcS[:])
                    nc.vector.tensor_add(tmp1[:, 0:T], tmp1[:, 0:T], bcB[:])
                    nc.vector.tensor_scalar(tmp1[:, 0:T], tmp1[:, 0:T], gb[:, dt:dt + 1],
                                            gb[:, 16 + dt:17 + dt], op0=ALU.mult, op1=ALU.add)
                    pair(xnh[dt][:], xnl[dt][:], tmp1[:, 0:T])
                    vpub()
                barrier()
                if STOP_AFTER <= 6:
                    finish_early()
                    return nc

                with ExitStack() as sc6:
                    sb6 = lambda name, shape, dt=F32: SB(sc6, name, shape, dt)
                    wfb = [sb6("wfb0", [128, 512]), sb6("wfb1", [128, 512])]
                    wabs = sb6("wabs", [128, 1])
                    thrb = sb6("thrb", [128, 2])
                    tq = sb6("tq", [128, 512])
                    tq2 = sb6("tq2", [128, 512])
                    sc6w = top.enter_context(ExitStack())
                    wabh = SB(sc6w, "wabh", [128, 2048], BF16)
                    wabl = SB(sc6w, "wabl", [128, 2048], BF16)
                    for dt in range(NDT):
                        sl = dt % 2
                        nc.sync.wait_ge(vs.h, vs.n)
                        nc.sync.wait_ge(ss.h, ss.n)
                        dma(nc.sync, xbuf[sl][:], wfT[dt * 128:(dt + 1) * 128, :])
                        wait(nc.scalar, ds)
                        if dt >= 1:
                            nc.scalar.wait_ge(vs.h, vs.n)
                        nc.scalar.activation(tmp1[:], xbuf[sl][:], AF.Abs)
                        spub()
                        wait(nc.vector, ss)
                        if dt >= 1:
                            nc.vector.wait_ge(ts.h, ts.n)
                        pair(wabh[:], wabl[:], tmp1[:])
                        vpub()
                        wait(nc.tensor, vs)
                        for c in range(4):
                            cs = slice(c * 512, (c + 1) * 512)
                            nc.tensor.matmul(P[c][:], ones128[:], wabh[:, cs],
                                             start=(dt == 0), stop=False)
                            inst = nc.tensor.matmul(P[c][:], ones128[:], wabl[:, cs],
                                                    start=False, stop=(dt == NDT - 1))
                        tinc(inst, ts)
                    wait(nc.vector, ts)
                    for c in range(4):
                        nc.vector.tensor_copy(xbuf[1][0:1, c * 512:(c + 1) * 512], P[c][0:1, :])
                    nc.vector.reduce_sum(wabs[0:1, 0:1], xbuf[1][0:1, 0:2048], axis=AXX)
                    vpub()
                    nc.vector.wait_ge(vs.h, vs.n)
                    nc.vector.tensor_scalar(tq2[0:1, 0:1], wabs[0:1, 0:1],
                                            0.5 / (2048.0 * 2048.0), 0.5 * Q_EPS,
                                            op0=ALU.mult, op1=ALU.add)
                    vpub()
                    nc.vector.wait_ge(vs.h, vs.n)
                    nc.vector.tensor_scalar_mul(tq2[0:1, 1:2], tq2[0:1, 0:1], -1.0)
                    vpub()
                    nc.vector.wait_ge(vs.h, vs.n)
                    wait(nc.gpsimd, vs)
                    dma(nc.gpsimd, thrs.ap(), tq2[0:1, 0:2], gd)
                    nc.gpsimd.wait_ge(gd.h, gd.n)
                    dma(nc.gpsimd, thrb[:], thrs.ap().broadcast_to([128, 2]), gd)
                    wait(nc.vector, gd)
                    nc.vector.tensor_copy(tq2[0:1, 4:6], thrb[0:1, 0:2])
                    vpub()
                    nc.vector.wait_ge(vs.h, vs.n)
                    sc6w.close()
                    wqq = [sb6("wqq0", [128, 4096], BF16), sb6("wqq1", [128, 4096], BF16)]
                    if STOP_AFTER <= 6.5:
                        nc.vector.tensor_copy(tmp1[:, 0:2], thrb[:])
                        nc.vector.tensor_copy(tmp1[0:1, 2:3], wabs[0:1, 0:1])
                        vpub()
                        finish_early()
                        return nc

                    used6_mk = [ts.n, ts.n]
                    quant_mk = [vs.n, vs.n]
                    qdone_mk = [vs.n, vs.n]
                    xb6_mk = [dx[0].n, dx[1].n]
                    xb6free_mk = [vs.n, vs.n]
                    evac6_mk = [vs.n, vs.n]
                    st6_mk = gd.n

                    def prep6(hq):
                        b = hq % 2
                        cw = hq * 256
                        nc.vector.wait_ge(ts.h, used6_mk[b])
                        for dt in range(NDT):
                            s2 = dt % 2
                            nc.sync.wait_ge(vs.h, qdone_mk[s2])
                            dma(nc.sync, wfb[s2][:, 0:256],
                                wfT[dt * 128:(dt + 1) * 128, cw:cw + 256], dw[s2])
                            nc.vector.wait_ge(dw[s2].h, dw[s2].n)
                            nc.vector.tensor_single_scalar(tq[:, 0:256], wfb[s2][:, 0:256],
                                                           thrb[:, 0:1], ALU.is_ge)
                            nc.vector.tensor_single_scalar(tq2[:, 0:256], wfb[s2][:, 0:256],
                                                           thrb[:, 1:2], ALU.is_le)
                            nc.vector.tensor_tensor(wqq[b][:, dt * 256:(dt + 1) * 256],
                                                    tq[:, 0:256], tq2[:, 0:256], ALU.subtract)
                            vpub()
                            qdone_mk[s2] = vs.n
                        quant_mk[b] = vs.n

                    prep6(0)
                    for hq in range(8):
                        b = hq % 2
                        if hq < 7:
                            prep6(hq + 1)
                        for ot in range(2):
                            gidx = hq * 2 + ot
                            sl = gidx % 2
                            nc.sync.wait_ge(vs.h, xb6free_mk[sl])
                            dma(nc.sync, xbuf[sl][:, 0:T], x1s[gidx * 128:(gidx + 1) * 128, :],
                                dx[sl])
                            xb6_mk[sl] = dx[sl].n
                        for ot in range(2):
                            gidx = hq * 2 + ot
                            sl = gidx % 2
                            pb = 2 * (gidx % 2)
                            nc.tensor.wait_ge(vs.h, quant_mk[b])
                            nc.tensor.wait_ge(vs.h, evac6_mk[gidx % 2])
                            for dt in range(NDT):
                                wsl = slice(dt * 256 + ot * 128, dt * 256 + (ot + 1) * 128)
                                st = dt == 0
                                sp = dt == NDT - 1
                                for c0 in (0, 512):
                                    rs = slice(c0, c0 + 512)
                                    tg = P[pb + c0 // 512][:]
                                    nc.tensor.matmul(tg, wqq[b][:, wsl], xnh[dt][:, rs],
                                                     start=st, stop=False)
                                    inst = nc.tensor.matmul(tg, wqq[b][:, wsl], xnl[dt][:, rs],
                                                            start=False, stop=sp)
                            tinc(inst, ts)
                            used6_mk[b] = ts.n
                            nc.vector.wait_ge(ts.h, ts.n)
                            nc.vector.wait_ge(dx[sl].h, xb6_mk[sl])
                            nc.vector.wait_ge(gd.h, st6_mk)
                            for c0 in (0, 512):
                                nc.vector.scalar_tensor_tensor(tmp1[:, c0:c0 + 512],
                                                               xbuf[sl][:, c0:c0 + 512],
                                                               gb[:, 32 + gidx:33 + gidx],
                                                               P[pb + c0 // 512][:],
                                                               op0=ALU.add, op1=ALU.add)
                            vpub()
                            evac6_mk[gidx % 2] = vs.n
                            xb6free_mk[sl] = vs.n
                            nc.gpsimd.wait_ge(vs.h, vs.n)
                            dma(nc.gpsimd, y[gidx * 128:(gidx + 1) * 128, :], tmp1[:, 0:T], gd)
                            st6_mk = gd.n
                    nc.gpsimd.wait_ge(gd.h, gd.n)
                    nc.gpsimd.wait_ge(gd.h, gd.n)
    return nc


_CACHE = {}


_INVF_B64 = "AACAP9evXT8R+T8/4D0mP5r1Dz/PU/k+m+jXPhv4uj6b6KE+BDWMPiPUcj7tR1I+hxg2PkCwHT53jQg+1n/sPc3MzD14WbE9DZSZPU3+hD3CVWY9P3ZHPRW6LD1IkxU944YBPdJU4DxPQ8I8izmoPDmtkTwzTXw88ntaPBEzPTwK1yM8LeENPLC59TsVytQ7nES4O8yRnzt3Loo7DlJvOzg+TzsPdzM7DWkbO2+UBjsoFek6XNfJOo7JrjoOXJc6bxKDOuIBYzqMlEQ6RDsqOhZqEzqsT/85JRfdOdd0vzlgy6U5cpKPORWoeDnkU1c5U3c6ORZ5ITly1As54izyOA=="
_DC_B64 = "hQAAAA4CAAARAwAAhwMAgMUDAAARBACAxAQAgBUFAABLBQCAVAUAgJEGAIASBwAAwwcAgAIJAIAPCQAAmAkAgFgKAIDECgAAWgsAANcLAIAHDACA3AwAgFcNAICNDQAAyw0AgAMPAIAbEACAmRAAgNwQAICLEQCA4xEAAE0SAIDKEgAAHhQAAEgUAABMFACAXhQAAFkVAIBfFQAARxYAABsXAICMFwAAABgAgBoYAIBeGQCAUBoAAIcaAADHGgCAIhsAgKQbAIDRGwCAYhwAgIwcAIDcHAAAXx0AgCEeAAChHgAAwx4AgOAeAIDcHwCAwCAAANEgAIDUIQCACCIAAAojAIBXIwAAxCMAgBgkAIBGJACAYiUAAJUlAACkJQCAyiUAgBkmAADNJgCAzyYAAN8mAIBLJwCATicAAFInAACDJwAAnicAABkoAIAhKACAiygAgKMoAABjKwAAjysAgBAsAICXLACAoiwAgNIsAICKLQCAzy0AgGQvAACJLwCAQDAAgN8wAICmMQCA5jEAgBYyAAAlMgAAVzIAAIEyAACRMgCApTIAAIozAIAUNACAiTQAAJs0AIDkNACAVzUAgKM1AIClNQAA2zUAgEo2AIDONwCA1TcAgAs4AAAnOACAhjgAAOQ4AIACOQCAFTkAgBs6AIDFOgCAyDoAAEk7AACMPAAAVj0AgJw+AIBkPwCAmD8AgMs/AADjPwCAIUEAgCdBAABHQQAAoUEAgKdBAADPQQAA0kEAgBRCAIAZQwAAUkMAgNNDAIAdRAAAQ0QAgEdEAIBQRAAAwkQAAMxEAIDSRACAHEUAgJlFAIDSRQAAokYAANZGAIDjRgCAikcAgKJHAAAFSAAAF0kAgKVJAIBXSgCAZ0oAAKdKAADwSgCAC0sAgFpLAICUSwCAB0wAAFRMAIBiTACAZE0AgJNNAICpTgCApk8AgCRQAIAHUQAAE1EAABZRAABoUQAASVIAAOhSAAAEUwCAC1MAAF5TAICFVACAiVQAgKhUAADiVACAAVUAgAlVAAAYVgCAI1YAgNtWAIDqVgCAqlcAgCdYAIAjWQCAkFkAgJhZAIDAWQCAx1kAgOlZAAAfWgAAVFoAgGdaAIDCWgAAxVoAgOlaAADUWwCAwVwAgCddAICnXQCA3l0AgCVeAICnXgCAw14AAChfAIAnYACAKWAAAAJhAIDnYQCACmIAgEtiAABWYgCAAmMAgJxjAAABZACAYGUAgGllAADTZQCAW2YAAKhmAIALaQCAhWkAgI1pAAAWagAAZWoAAGhqAICQagCASWsAgChsAIBObACAUm0AgKVtAADabQAAam4AAI5uAACqbgAAAG8AAGpvAAAgcACAqnAAAN9wAAAMcQCAIHEAgNxxAIDmcQCAGnIAgE1yAICDcgAAx3IAAOZyAIAKcwCAwHQAAMZ0AABBdQCA03UAgNx1AAAWdgAAHXYAAF92AIAcdwAAQ3cAgBd4AICpeACAYXkAAGl5AIAhegAAg3oAgMF6AADSegCA4XoAABB7AACFewCAwXsAgCl8AIBpfQCAnX0AgN19AIDqfQCAD38AAKx/AIDHfwCAzn8AANB/AIDbfwAABoAAgIWAAAADgQAATIEAgGGBAIDTgQCAFIIAAGuCAABCgwAALYUAgKuFAACihgCA6oYAgBWHAIAdhwAAbYcAgMiHAADZhwCAC4gAgBCIAACdiAAAoogAAMKIAICbiQCA3okAAOGJAIAWigCAF4oAgF6KAACRigCAlooAgKuKAAAMiwCALIsAAEWLAADtiwCAjYwAgJeMAICsjAAArYwAgMWMAID8jACAUY0AgImNAACXjQCAKI4AgJaOAAAojwCARI8AgEiPAABSjwAAlI8AgKePAACojwCAEJAAAA6RAIBckQCAYpEAgBCSAICskgAASJMAgAOUAIAIlACAQpQAgIGUAICrlACAYZUAAGKVAICilQCAE5YAgGGWAAAKlwCAYpcAgI6XAIDslwAAIpgAgE+YAIAjmQCAgpkAgKKZAIDKmQCAEJoAgGmaAICBmgAAmZoAgNqaAADZmwCAIZwAgK6cAIDCnAAAxpwAAF+dAAAZoACAK6AAgIqgAICioACABqEAgFahAAAuogCAh6IAgImiAADEogCA7qIAgOOjAADpowCA7qMAgBKkAAAopAAAK6QAgEmkAIAppQCAaKUAAG2lAADIpQCAuKYAgOumAIBjpwCArKcAgC6oAIDpqACA7KgAgJupAIDKqgCAAKsAgAKrAIBoqwAAh6sAgO+rAAAfrACAoqwAABStAIAlrQCARa0AgGmtAICdrQCArK0AgJ2uAIDBrgAAL7AAgKuwAICzsQAA1LEAgO+xAIBqsgCAbLIAgJayAICCswCA2rMAgF+0AABitACAjrQAAJ20AIDCtACABLUAgAu1AACrtQCAsbUAgOC1AADqtQCAHbYAgCi2AICLtgCAnbYAgKC3AABAuACAUrgAgFW4AIDfuACA77gAgBK5AABtuQAAg7kAgG+6AIDqugCATLsAgG67AACauwAAxLsAgNS7AIAavAAAILwAgCy8AIBZvACAbLwAgO+8AIAkvQAAM70AAKm9AADqvQCAxr4AgOq+AIBgvwCAB8AAgIbAAACBwQAAUMIAgITCAIAkwwCAXMMAgGnDAACkwwCAqsMAgOzDAIAexAAAbsQAAMrEAIBuxQAAjMUAgOnFAAAwxwAAU8cAgMDHAAAuyAAAk8gAgKPIAAAQyQCAi8kAAJTJAICcyQCAbsoAAJzKAICuygAAG8sAAPDLAIBlzACA0cwAgFLNAIDJzQCA7M0AgIPOAIDbzgCAcM8AgMnPAIAb0ACAa9AAgPDQAID40ACABNEAABrRAAA50QCAjtEAgNPRAABZ0wAAoNMAAKPTAADR0wAA6dMAgK/UAACw1ACAD9UAAI/VAICl1QAAz9UAgCDWAIBg1gCAm9YAgKTWAICp1gAAutcAgGXYAACS2AAAYdkAgGrZAACE2QCA29kAAB3aAICd2gCApNoAgK/aAADa2gCAKtwAAAbdAABl3QCADN4AgC3eAIDv3gAALd8AgFHfAIBt3wCAgd8AgEfhAIDN4QCA5OEAgE3iAAAt4wCAR+MAAMzjAIAq5QAAAeYAgALmAIAY6AAAK+kAgEXpAIBJ6QAAZekAgInpAICr6QCA0ukAAALqAIAx6gCAQeoAgNbqAIAc6wAA7usAgKHsAIBd7QAAce4AgJzuAIDx7gCAYu8AgKzvAIAD8ACAEPAAgAPxAIBd8QCApPIAgOLyAAAh8wAAx/MAANLzAIDq8wCAMfQAgID0AACx9ACAwPQAgFv1AIBf9QCAjPUAgMf1AIDx9QCAYvYAAI32AACf9gCA0/YAgBL4AAAc+ACA1/gAgBv6AIBA+gCAjPoAgAX7AIAO+wCALPsAgFv7AIBr/AAAbvwAgK78AIDW/ACAFv0AgBz9AIBw/QAAif0AgE/+AIBi/gCAFP8AgHH/AIDu/wCACQABgC8AAYDNAAEA2AABAG4BAYDTAQEARgIBAHECAYAeBAGArgQBgMoEAYBGBQGAzgUBgNMFAQAVBgEAGAYBgKAGAQAYBwGAawcBAHAHAQCVBwEA7AcBgFcIAYALCQGAFwkBAMIJAYDGCQEA1gkBgAcKAQAXCwGAIgsBgC8LAYBwCwEAjgsBgBcMAYAZDAEALAwBgFQNAYDADQGACw4BgDIOAYBZDgGAUA8BgA0QAQBIEAGAUBABAMEQAYAGEQGAFhEBAFoRAQDDEQEA3hEBAKcSAYCtEgGANBMBgGsTAYBtEwGA3hMBAAkVAYChFQGAbxYBgCIXAQCiFwEApRcBgMMXAQDtFwGAaxgBgPAYAQDyGAGACxoBgOMaAQDnGgEAqxsBgNUbAYANHAEATRwBgM4cAQAcHQGATB0BgJcdAYDvHQGALx4BgDEeAQBsHgEA5x4BADEfAQCsHwEABSABgCcgAQDUIAEAHSEBAIchAYCtIQGADCIBABgiAQAvIgGAbCIBAOMiAYDxIgEAziMBgE0kAQBtJAGAcSQBAI4kAQCvJAGAJiUBgK0lAYDGJQGA8iUBgCYmAYBjJgGAjyYBAJ8mAQCtJgGAzCYBgOMmAYAcJwEAyycBAAspAQAcKQGAQSkBgNUpAYByKgGApyoBgK8qAYDGKgGACCsBgFQrAYAxLAEAQiwBAGEsAQAfLQGAcS0BAIgtAYD8LQGACC4BgA4uAYBWLwGAXy8BgLAvAYAZMAGAizABgDExAQBUMQEApzEBgPExAQAZMgEAJjIBAHMyAQCGMgGAoTIBgJMzAYCwMwGAmDQBgNc0AQAfNQGAJjUBAGY1AQCINQGAjTUBACE3AYBHNwGA5zcBgBA4AYDXOAGA5jgBAEc5AQCvOQGAwzkBgNU5AYDmOQEA8zkBgGw6AYDHOgEADDsBAEA7AYBwOwGAcTsBAHM7AYCKOwGADjwBAIM8AQCsPQGAxz0BgJs+AQDyPgEASj8BgGw/AYCMPwEAnj8BgFpAAQCFQAGAMUEBAFxBAYCmQQGAsEEBgBNCAYBhQgGAcUIBAAZDAYAzQwGAQkMBAHNDAYDuQwGAo0UBAOFFAYAJRgEAbkYBgMtGAYBkRwEAgkcBAJBHAQChRwGAyUcBgO1HAQAbSAGA7EgBgI1JAYCySQEA3UkBgHBKAYDBSgEA2EoBALFLAQAVTQEATk0BgFtNAYDCTQEAyU4BgF1QAYBgUAGAcFABgKRQAYDtUAEAzlEBACdSAYBuUgGApFIBgN1SAQBYUwEAcFMBgA5UAYAYVAEAP1QBgK1UAQDHVAGARFUBAHBVAYCAVQGAblYBgI9WAQAnVwGAXVcBgMhXAYDVVwEAHFgBgDBYAYAgWQGA1lkBgPBZAYAaWgGAcVoBgOJaAQDuWgGAJ1sBgEJbAYAWXAEAs1wBgM9cAYASXQEAGl0BgJ9dAQCiXQEAwF0BAM5dAYDiXQEA+10BgBpeAYAnXgGAVV4BgBBfAQCJXwGAz18BgGhgAYDyYAEAFGEBgBlhAQAoYQGALWEBAFphAQCXYQGAmWEBAMhhAYDDYgEA72IBgCJjAYBDYwGA8mMBAAVkAYDQZAGA0WUBgAtmAYCiZgGAsWYBgOhmAYDpZgEAcWcBgKBnAYDwZwGAKGgBgFJoAYBVaAEAWWgBgKBoAYDIaAEAjmkBgANqAQAiagGAcGoBgJlqAQAYawGAM2sBAABsAYANbAGAkGwBgKhsAYBRbQGAtG0BgMltAYAybgEAn24BgLJuAQBgbwEAsG8BgPJvAQCTcAGA8nABAG5yAYBxcgGAg3IBALByAYC1cgEA3HIBgOdzAYAbdAGASnQBgFB0AYCxdAGAsnQBAN50AQAHdQGAi3UBAMx1AYDtdQGALXYBgDJ2AQBndgGAkHYBALx2AYAldwEATXcBgKJ4AYD0eQGAHHoBgK56AQAlewEAmHsBgJ97AYDKewEA03sBgNZ7AYDaewGA3nsBgCl8AQAvfAGAkHwBgIJ9AQCSfQGA5H0BgO99AYACfgGAaX4BAHF+AYD0fgGAGn8BgBx/AQBafwGAln8BgNh/AYAkgAEAKYABAKKAAQAEgQEAC4EBAHSBAYDMgQEAHYIBgISCAYCRggGAlIIBAMGCAYAigwEAnYMBgK+DAYBahAGAiYQBgNmEAYAZhQGABoYBgEOGAQCuhgEAx4YBAAKHAQBlhwGAb4cBgIiHAYDZhwEA6IcBgPSHAYAeiAEAKIgBgDOIAQCbiAGAnogBAMuIAQDXiAGAVYkBgGWKAYBvigGAh4oBgAeLAQAuiwEAUIsBAGOLAYCziwEA1IsBABWMAYAvjAGAV4wBgJeMAYBWjQEA1I0BgD6OAYBljgGAwo4BgAWPAYBbjwGAj48BgJSPAYBUkAEAwJABAJKRAQCJkgEAAZMBAFOTAQABlAGAD5QBgESUAYBzlAEAEZUBABuVAYAzlQEAk5UBgKmVAYDplQGAMpYBgK6WAQD4lgEALpcBAFOXAQBplwGAtZgBAOmYAYBVmQGA6ZkBgEGaAQClmgGAxpoBgBycAQBFnAGAmpwBgDOdAQBMnQGARp4BgFyeAYBwngGA1J4BABqgAYCMoAEAIaEBgGuhAYDwoQGARqIBgMqiAYDPogGA0KIBAHCjAYBCpAEAs6QBABelAYD1pQGAR6YBAK+mAYAwpwGARqcBAIWnAYD1pwGA7qgBgEOpAYCNqQGANKoBAMWqAYALqwEAIKwBgBStAYBgrQGAka0BAKWtAQDqrQEANa4BgKauAQDIrgGA6K4BgAavAYAJrwGAyq8BAO6vAYBesAEAs7ABAAexAQCDsQGApbEBAJayAQDgsgGAH7MBgF6zAYCgswGAB7QBgJC0AYDUtAGAJrUBADS1AQA1tgGAXbYBgMa2AYDftgGA4LYBgE23AQDWtwGA6bcBgPK3AYAguAGAQrgBAIi4AQCZuAEAsrgBgLS4AQA1uQGAHboBAOO6AQDwugGAA7sBgHW7AYAMvAGAD7wBgBK8AYAgvAEAWrwBgFu8AYBjvAEAarwBAJW8AYAyvQGA6r0BAO+9AQA1vgGAhr4BgKm+AYC0vgEAwb4BgPC+AYAZvwGAIL8BACO/AQAvvwEAjb8BgMS/AYAMwAGAJsABgFHAAYBewAGAsMABgObAAYDvwAEAJsEBgGnBAYDLwQGA38EBgPDBAYCwwgGA3sIBAEjDAQB1wwGAhMMBgKbDAYAixAEAUMQBgGnEAYCNxAGAi8UBgNbFAQBTxgGAtcYBgAnHAQAaxwGAR8cBAEnHAYCIyAGAksgBgNLIAYCwygGA6coBgAbLAYBvywEAL8wBAEvMAQDQzAEADc0BgCnNAYCOzQEA6s0BgM3OAYAizwGAcs8BgJfPAYDvzwEA8s8BgAfQAYBE0AGARtABgHLQAYCG0AGAstABgOrQAYAB0QEADNEBgDDSAYBp0gGAjNIBgNfSAQD00gEAANMBgNHTAYAv1AEASdQBgJnUAQDb1AGA8NQBgDPVAYCL1wGAy9cBAHPYAYC02AEAztgBgBrZAYBV2QEAotkBgALaAYAk2gGA5NoBgPPaAYAk2wGAmtsBAMzbAYBZ3AGAadwBAMXcAYAV3QGAQN0BAE3dAYAk3gGAYt4BgJneAQDM3gGAJt8BgGLfAYBk3wGAk98BAL3fAYDP3wGASOABALTgAQAA4QGAAeEBADPiAYCA4gEAkOIBgNfiAYBL4wEAc+MBgIvjAYDJ4wEA5OMBgPTjAQAL5AGANOQBAEDkAYCH5AGAouQBgPTkAQBh5QEAyOUBgMrlAQDP5QGA3eUBgOblAYAS5gGA4uYBAOfmAYAM5wGAMecBgK/nAYAF6AGAIugBAEDoAQBy6AGA8egBgPPpAYCz6gGA7+oBgDPrAYC06wEA4esBgADsAYAP7AEAiewBgNTtAQD17QEA1O4BgDbvAYBI7wGAle8BgMLvAQA08AEAoPABACnxAYC28QGAR/IBAGfyAQAK8wEAFPMBgFnzAYCm8wEA0vMBgOTzAYDm8wEA9vMBgAb0AYAm9AEAtvQBgOf0AQDn9QEA7/UBgAr2AYAm9gEADvcBgN/3AYAZ+AGARvgBgI74AQDD+AEA0/gBgCH5AYBe+QGAdvkBgLH5AYDn+QEAAPoBgAb6AQAe+wGARvsBAGf7AQCI+wEA3vsBgFL8AYBh/AGAmvwBAMf8AYDz/AGAC/0BgBv9AQCk/QEAxv0BAJf+AYCz/gGAwP4BAMz+AQAM/wEAFf8BACr/AYBL/wEA0/8BAA=="
_DS_B64 = "FAEAgAkCAIAWAgCA1gIAAOUCAIAbAwCAHAMAgFwDAICLAwCAmgMAgGEEAICdBACAjQUAgJkFAADiBQCAygYAgOEGAIAfBwCAUAcAgMYHAAAFCACAgAgAACIJAIAkCQCAQgkAAGQJAICCCQCApQkAgMYJAIAiCgCAigoAAOQKAIAlCwCATwwAADwNAIBVDQAAaQ0AgAoOAIBWDgCAzQ4AgOkOAIBhDwAAqA8AgOIPAABWEACAXxAAgGcQAIBvEACApxAAgNEQAADWEACA4RAAABURAIAgEQCAkREAgO4RAIAMEgCAaxIAgHMSAIDSEgCAqxMAgNITAIAmFACAZhQAgEwVAACuFQCARhYAgFkWAIBqFgCAiBYAgJcWAIDSFgAAJBcAACYXAIBkFwAAbRcAgIsXAADwFwCAKhgAgGkYAICAGAAApBgAAM0YAAANGQCAURkAgGkZAICjGQCAChoAAKcaAIArGwCAphsAALUbAIDHGwCAVRwAAKocAIDVHACAFR0AACsdAIByHQCAKx4AgCweAIBSHgCAix4AgKseAIBVHwCAsh8AgJIgAICqIACArCAAgBkhAADpIQCA7yEAgCciAABOIgAATyIAgG4jAICVIwAA5yMAAJskAIClJACA9SQAgCYlAABBJQCA8SUAgCUmAICfJgCA6iYAgBAnAIAWJwCAaCgAAG0oAIAWKQAAaCkAAHgpAICUKQCANSoAgLEqAIDUKgCA3SoAAOkqAABuKwCAgSsAgO0rAIAMLQAALS0AgO8uAIBuLwCAmy8AgAEwAIAQMACALjAAgKgwAADAMACAHTEAgB4xAABqMQAA8TEAgFkyAACyMgCA6DIAAPQyAIAqMwAAOTMAgKozAAC7MwCABDQAgC00AIBJNACA8TUAgEg2AAC1NgCAFDcAgBY3AIAdNwCAMzgAgJA4AAC0OACA8jgAgAg5AIBwOQCAdjkAgLI5AIBHOgCAkzoAAPY6AIBZOwCAaDsAgLI7AIDEOwCABzwAgEo8AAB2PACAyTwAAPc8AIBxPQCACD4AgNo+AIBRPwCAXz8AgOo/AADrPwAAqUAAgMVAAADLQACARkEAgHBBAIDeQQCANUIAgIBDAICXQwAAsUMAgGFEAABvRACAUEUAAJBFAAAvRgCAQ0YAgBpHAABgRwCA8EcAgB9IAIBeSACAeUgAgBRJAIBCSQCAS0kAAKxJAACzSQCA20kAgN9JAIBbSgCAbEoAAHdKAICfSgCA4UoAAOxKAABySwCArEsAALRLAID3SwCABUwAgAZMAAA6TACAQ0wAgMtMAIDzTACAL00AgI5NAACWTQAAx00AALFOAIDFTgAA8k4AgPZOAIBfTwAA708AgPdPAIAVUACAclAAgLRQAIDyUACAoVEAgOBRAIAxUgCAbFIAAIlSAIAtUwAAc1MAgHlTAICsUwAAslMAgLNTAIDRUwCA4VMAgARUAAArVACA4FQAgBBVAAByVQCAh1UAAKBVAIDAVQAA8lUAgLBWAICyVgCAyFYAgAxYAIDRWACA9VgAgAhZAAARWQAAs1kAgNxZAIBuWwAAcVsAgLNbAIDIWwCA4FsAABFcAAArXACAMlwAgFpcAIC0XACA0VwAgPFcAIANXQCAQl0AgGBdAAAzXgCAcF4AgLVeAIDuXgAANV8AgHRfAID1XwCAT2AAAL1gAICWYQCA72EAABNiAIAxYgCANGIAgDxiAIBGYgAAdmIAgJViAIDTYgAANGMAgEhjAIBvYwAAk2MAgDRkAIBxZACALGUAgEdlAIDxZQCABmYAgDNmAIA6ZgCAw2YAgMlmAADyZgCAu2cAgO9nAAAGaAAAHWgAgFJoAICzaACA2GgAgOFoAAAaaQAARGkAAIdpAIDhaQAAl2oAgJ1qAIC1agCADGsAgItrAICxbACAtmwAgN5sAAB7bQCAj20AgDVuAIA7bgCAdW4AgItuAIBvbwAAtG8AgGNwAIDZcACAL3EAAFtxAICbcQCAyHEAAC1yAIA1cgCAVnIAAJJyAIDucgAA9HIAgHVzAICVcwAA43MAgPRzAIBEdACArXQAgM90AAASdQAALnUAAC91AAA2dQCAxnUAgM11AIB2dgCAm3YAgLN2AID1dgCAMHgAAMZ4AAByeQCA03kAgPp5AIAwegAAMnoAgFN6AIB0egCA+noAgAt7AAATewAAOHsAgLV7AIC3ewCA83sAgLZ8AIDAfAAA9nwAgA59AIBafQCAzn0AAIh+AICLfgCAkn4AgLZ+AIAEfwCAFH8AgDl/AIBefwAA3n8AAE2AAACkgACApYAAAMKAAIARgQCApIEAgLaBAIC3gQCAuoEAgMiBAIDzgQCAt4IAgOKCAADxggAAHoMAgDODAIB2gwCA3oMAgAWEAAAehACAN4QAgHOEAIClhAAAsIQAALmEAIDKhACAQ4UAAHSFAICyhQCABoYAgCOGAIA4hgCA8YYAAAmHAIAPhwAAUYcAgPSHAIAHiACAcIgAAHCJAAB2iQCA3IkAgAKKAIBNigCAdooAgHOLAIDAiwCAxosAgDaNAIB2jQCApo0AgLeNAIBTjgCAn44AgDiPAIBJjwCAVY8AgJ+PAICzjwCAI5AAADmQAIClkACAe5EAgMeRAIDVkQCAU5IAgDWTAIBlkwCAd5MAgJ2TAICwkwAAuZMAgPaTAIBxlAAAppQAALGUAAD+lACAZZUAgHWVAIB+lQCAlJUAgKaVAAC4lQCAC5YAgKCWAIC7lgCA25YAgB+XAICblwCA1pcAANiXAIDvlwCAH5gAgHSYAICRmAAAm5gAgLGYAADLmACA/5gAgC+ZAIB1mQCAsZkAALiZAIDymQAAAJoAgDKaAAC3mgCA4JoAgPWaAIA4mwCAU5sAAJObAADTmwCAdpwAgJOcAICvnACAtJwAgMacAADQnACA0J0AgN2eAIA3nwCAWp8AAHafAIB3nwCApJ8AANufAADgnwCA+Z8AgBugAACEoACA5KAAAOagAADvoACANqEAgDehAIBgoQCAoKEAgLWhAIC5oQCA9qEAgB2iAAAeogAANKIAgHaiAICPogCApqIAgLeiAID8ogCAPKMAgEmjAIC1owCA96MAgDakAICkpAAAN6UAgHWlAIChpQAA3qUAAOGlAAAFpgCANaYAgFmmAAB1pgCAdqYAgJamAIDCpgCA4aYAAPimAIA4pwCAPacAgE+nAICXpwCAoacAAKanAICxpwAA+qcAgNSoAID3qACAOKkAgLepAIAlqgCAMKoAgDeqAIA5qgCARKoAgJWqAIC4qgCAAqsAAHerAIB9qwCAiKsAgIqrAICgqwAAuKsAgMerAADcqwCAAqwAgHWsAICFrACAiqwAgOasAIBErQCAIK4AgE2uAABQrgAA9K4AgIOvAIClrwCA8K8AgPivAIA4sACAcbAAAH6wAIC5sACAAbEAALexAIC+sQCA+bEAgEKyAAAUswAAsrMAABy0AIAntAAAuLQAgMi0AIDwtACARLUAgH61AIC1tQCAurUAgPC1AIA2tgCAirYAgI22AIDRtgAAR7cAAFu3AICetwCA1LcAABa4AIA4uACARLgAgEW4AIB4uACAm7gAALu4AIAiuQCAcrkAAK65AIAEugAASboAgHe6AIDSugAAIbsAgDu7AIA9uwCAR7sAgMu7AIDNuwAAKLwAgEu8AIBSvACAW7wAgPq8AIB3vQCAjb0AALO9AAC9vQCAaL4AgOG+AIDzvgAA+b4AgNO/AAAXwACAHcAAgDjAAIBVwACA+cAAgCjBAICDwQCAGMIAAJDCAICwwgAAt8IAgDvDAIDzwwAAM8QAAE/EAAAYxQCAMcUAgDPFAAB+xQCA0cUAANTFAABSxgCAZ8YAgM/GAIDnxgCAQscAgGjHAABxxwCAhMcAAKLHAIC4xwCA0ccAgATIAAApyACANsgAgHnIAICayACA8sgAADXJAIA7yQCAkckAgKLJAIAHygCAJsoAgDHKAICAygAAtsoAgEjLAABYywAAmssAAPTLAAC3zACAvswAgFnNAAB6zQCAts0AgDzOAIBFzgCAdM4AAHbOAIAxzwCAgs8AgJ/PAIDozwAA/s8AgCbQAIBF0ACAttAAgLnQAIDm0ACABtEAgI/RAADL0QAA8dEAgDTSAAA30gCAldIAgPfSAIBL0wCAXNMAgHbTAIDX0wAA99MAgDbUAIA41ACAOdQAgF3UAABp1AAA1dQAAPnUAIBV1QCAjtUAADTWAAD/1gCAPtcAgIPXAAAg2AAAk9gAgPPYAACA2QCAptkAAMbZAIDc2QCAptoAAA3bAIAm2wAAU9sAgJHbAADg2wAAQdwAAF/cAIBB3QCASt0AAFTdAAC13QAAKN4AgLbeAIC63gCAVd8AAM/fAIA+4ACAYOAAAGLgAABn4ACAd+AAgI7gAIC74ACA0eEAgDHiAAA+4gCAOuMAgD3jAICg4wCAyuMAgObjAIAQ5ACAEuQAgBjkAAA55ACAY+QAgObkAID55ACAE+UAgBzlAAAz5QAAdeUAAH7lAIC45QCA4+UAgAfmAIAn5gCANOYAADbmAIA55gCAaOYAgHPmAAB95gCAxuYAgODmAIAy5wCASucAAJnnAICg5wCA4+cAgBnoAIC06AAAvegAgOjoAIAI6QAAI+kAgGnpAACp6QAAvekAgBnqAABn6gCAhOoAgInqAICZ6gAA+eoAgHLrAICo6wCA+OsAgBDsAID27ACA9+wAgDTtAABi7QAAl+0AgKftAID47QCA+u0AgHzuAICN7gCAsu8AgLbvAIDx7wAAOvAAgF7wAIBg8ACAafAAAHrwAICn8ACAsvAAgLnwAICp8QAAxfEAgP3xAIA28gCARvIAgIDyAIC58gCAXfMAgLjzAIC78wCAHvQAAHz0AIDY9AAAEPUAADT1AAB59QCAe/UAgJH1AICA9gAA/PYAgDj3AIBd9wAAefcAgHr3AICd9wAABfgAgDj4AIB++ACAu/gAgMX4AIDy+ACAVPkAAHn5AICn+QAAu/kAgM35AIDU+QCAHfoAgCP6AIDY+gCAFPsAgHX7AACE+wCANfwAAD78AICh/ACAy/wAgNf8AAD7/ACAOv0AgNP+AIAz/wCAev8AgL3/AIDM/wCAGQABgGQAAQCqAAGA9QABAKoBAYC3AQGAxwEBAM8BAYDcAQEA+QEBgHQCAQB1AgEAggIBALoCAYDGAgGAygIBgPUCAQAWAwGAKQMBgIsDAQCPAwEAmgMBAM8DAQD3AwGAGgQBAD4EAYBSBAEAewQBgH4EAYC0BAEAtQQBALsEAYDjBAEAOgUBgHIFAQB9BQGA2wUBAEMGAQB2BgEAugYBgOQGAYD3BgGA+wYBgDsHAYChBwGA0QcBgPYHAQA1CAEAnQgBALoIAYBqCQEAtQkBAMUJAYAkCgGAPwoBgKEKAYD2CgEAEwsBgCoLAQBfCwGAaAsBgKQLAYDKCwEA3AsBgPwLAYALDAGAHwwBgDwMAYBzDAGAuwwBgOoMAQAKDQGAPw0BgOMNAYA7DgGAjQ4BgLwOAYDSDgEADw8BAKoPAQDRDwGA+w8BgEMQAYBoEAGAMxEBgDoRAYB1EQEAvBEBgFsSAQCXEgGAmBIBgLsSAYA2EwEAfRMBgMsTAYD5EwGAgRQBANgUAQD2FAEAFRUBABsVAYAhFQGAPxUBgE4VAYCXFQGAPBYBgKgWAYDXFgGA+xYBgBcXAYA9FwGAoRcBgGoYAQB1GAEAfBgBgIIYAQCoGAGAkBkBAL0ZAYA5GgGAdhoBAKUaAQDXGgEABhsBgAwbAYAzGwGAUBsBgF8bAQBqGwEAeBsBgIYbAYC1GwEA/BsBgA4cAYAZHAGA4RwBAPUcAQA6HQGATR0BgGQdAYC6HQGAux0BgDoeAYBLHgEAcx4BgIAeAYDVHgGA3x4BgHgfAYClHwGApCABgLwgAYB6IQGAuyEBgN4hAYANIgEAJSIBAL8iAYDAIgEAECMBgDgjAYDZIwEA+SMBgCokAYBBJAGAhiQBgLwkAYDGJAEA3iQBgAklAQBVJQEAdSUBALwlAYC8JgGAAycBgAcnAQBkJwGAcycBgHgnAYCUJwEA9CcBgPYnAQA2KAEAqSgBgLQoAYAFKQEAGykBgCgpAQA4KQGAPCkBgGIpAQCdKQEA3ikBgA8qAQAQKgGAdCoBgLMqAQAlKwEAUCsBgFYrAYC2KwEAuisBgPYrAQA7LAGAuiwBgLssAYD7LAGAMy0BAF4tAYB8LQGAGi4BgBwuAQA+LgGAYS4BAGUuAYA9LwGAPi8BgGUvAYB8LwGAni8BAKUvAYD8LwGAVDABgHkwAYC5MAGADDEBgCQxAQDgMQGAPTIBgAIzAYD8MwGAPDQBgGA1AYCVNQEAxzUBgPY1AQB1NgEAfDYBgJ42AQC/NgGAIjcBgB04AYClOAGAEDkBADs5AYBQOQGAhzkBgI85AYA9OgGAfzoBgL06AYAgOwGA0TsBADk8AYB+PAGAwzwBgBg9AYBgPQGAfT0BgP09AYAAPgGAHT4BgCk+AYAqPgGAOz4BgF0+AYB3PgEA4D4BgGU/AYB5PwGAnD8BgKs/AQC7PwGA9z8BADdBAQCiQQGAvkEBgH1CAYD0QgGATkMBgHdDAQDTQwGAH0QBgDxEAYCXRAGAvUQBgPdEAQA6RQGAPEUBgIRFAYD2RQEAWkYBAH9GAYC7RgGA10cBgPpHAYAISAGAEEgBgJBIAQC+SAGA5UgBgDtJAYA8SQGAR0kBgClKAQDGSgGAx0oBgPpKAYA0SwGAN0sBAH5LAYA9TAGAfkwBgIRMAYDpTAEA/EwBgABNAYBJTQEAn00BgLtNAYAWTgGAgE4BgJpOAYARTwGAYk8BANJPAYA9UAGAVFABgHxQAYCQUAEAtVABgLxQAYAsUQGARlEBgFNRAYBWUQGAh1EBgKtRAYC/UQGAKVIBACxSAYCFUgGAD1MBABZTAQA5UwGAeFMBAIRTAYC8UwGA7FMBgAFUAYAHVAGArFQBgARVAQBTVQGAmVUBgM9VAQBIVgEAe1YBgMFWAYAGVwEAH1cBgDVXAYA+VwGAH1kBgIVZAYC9WQGABloBgD1aAYC2WgEAzloBgOFaAQD0WgEA91oBABtbAYBKWwEAt1sBAP1bAYATXAGAP1wBgFxcAYB9XAGAt1wBADpdAYCTXQGAv10BgPhdAQBFXgEARl4BgGpeAYCbXgEABl8BgD1fAYBLXwEAdl8BAKVfAQDYXwEANmABAIhgAYCUYAGAqmABgNNgAQAMYQGAFmEBgBthAYAlYQEAQ2EBgOlhAYA6YgGAZWIBAGliAYC8YgGAvmIBgMViAYD/YgGAJWMBADVkAYBrZAGAlmQBAMxkAYDmZAGAPGUBgPdlAQD6ZQGACWYBAFZmAYB3ZgEAxmcBADVoAYB1aAGAdmgBAE1pAQB7aQGAvmkBgAdrAYCXawGA02sBgNdrAYD+awGAbG0BAHxtAYAZbgGANW4BgFluAYD1bgGABG8BgH1vAYCSbwGApG8BACpwAYB9cAGAgnABALVwAYDZcAEAfHEBgJVxAYA2cgEAQXIBgIlyAQDYcgGAvnMBgMJzAYDjcwGA/3MBgA10AYCRdAGA/HQBgHV1AYDTdQGA9XUBgPd1AQAFdgGAfXYBgJR2AQDMdgEA+3YBgPx2AYCSdwEA+ncBgPt3AYBOeAGAG3kBAFB5AYDKeQGAp3oBgLd6AQDDegEA/XoBgOB7AQD8ewGA/3sBgAx8AYA1fAGAP3wBgEZ8AQAFfQGAIX0BAIZ9AQCjfQEA/H0BgH9+AYC9fgGA934BACZ/AYBAfwGATn8BgHd/AQB8fwGAjH8BAL1/AYAmgAGAvYABgCCBAYBGgQEAoIEBgL2BAYD+gQGAIIIBgM2CAQDbggGAIIMBgE6DAQCIgwGApIMBgLqDAYD9gwGAfoQBgJSEAQDnhAEAG4UBAGaFAYB9hQGA54UBAP2FAYALhgGAVIYBgLyGAYA8hwGAvYcBgFaIAYBaiAEAvIgBgNaIAYDsiAGARYkBAFqJAYB/iQGAvokBgCSKAQCaigGA2ooBgEmLAQB+iwGA34sBAOCLAYAdjAGA/YwBgAKNAYBgjQGAeI0BAOeNAQANjgGAN44BAESOAQBZjgGAmY4BgPiOAQABjwGABo8BgAmPAYBrjwGAfY8BgJGPAYCZjwGAno8BgMSPAYDnjwEA/I8BgGaQAQBskAGAPZEBgH6RAYCkkQGArJEBgLiSAQDIkgEA9pIBgL6TAYDkkwGAGpQBACaUAQBKlAGAapQBgIeUAYCVlAGA65QBgEKVAYBElQEAXpUBgGCVAQDclQEAEZYBgBmWAQB/lgGAvZYBgFCXAQBslwGAd5cBAOyXAYA8mAGAZ5gBgKyYAYD7mAGAO5kBgFOZAQBqmQGArpkBgAiaAYAmmgEAV5oBgGyaAYB8mgGA0poBgCCbAYBOmwEAdpsBgHqbAYDEmwEA25sBgOKbAYD2mwGATZwBAHicAQCinAGAuJwBABOdAQAtnQEAjZ0BgCKeAYBrngGAeJ4BAIieAYCsngGAw54BgCSfAYA7nwGAFaABgO2gAQB/oQGAvKIBgIWjAYBMpAEAdqQBgAmlAYA+pQGAZ6UBgPylAYAUpgGAQqYBAGumAYCNpgEAvqYBgNOmAYAPpwEAGqcBgHmnAQCipwGAvacBgPanAYA/qAGAQ6gBgKSoAYC3qAEAwKgBAAKpAYA8qQGAh6kBALupAYDWqQEA4qkBgGuqAYD7qgGA/aoBgAqrAYASqwGALasBAH6rAYCLqwGAj6sBAKSrAYCtqwEAD6wBgMusAYB2rQGAwa0BAL6uAYDQrgEA+a4BAP+uAYAFrwGA7a8BAPavAYA7sAGAfrABgL6wAYDhsAGAUbEBgMiyAYCiswEArbMBAL+zAYCHtAGAx7QBgN+0AYD5tAEAJbUBgD61AYBZtQEAbbUBAIy1AYC5tQEAvbUBgOG1AQADtgEADbYBgFW2AQBctgGA4bYBAP63AYDLuAGAWLkBgLm5AQDSuQEA+LkBAHu6AYCRugGAkroBAOS6AQDrugEAmLsBAKS7AQD4uwEA67wBAEG9AYCYvQGAv70BgOu9AQD4vQEAPb4BgGS+AQBlvgGAfr4BgIq+AQCcvgGAtr4BgNO+AYB5vwEAf78BgAXAAYBTwAGApcABgMbAAYBWwQGAkMEBgLnBAQDNwQEA6MEBAO7BAYAcwgEAwMIBgFjDAQCdwwEAv8MBgPzDAYAHxAEAWcQBgG7EAYCnxAGAa8UBAHjFAQB5xQEA68UBADjGAQB3xgGA/sYBgATHAYBuxwGAuccBAL3HAYBQyAGAecgBAL3IAYDZyAGAZckBAHvJAYCryQEA68kBAPzJAYCTygGANcsBgL/LAYDSywGA+MwBAJvNAYC8zQGAEs4BgGfOAQCnzwEArc8BgBnQAQAu0AGAQdABgFTQAYBa0AGAftABgJ7QAYCg0AGA5dABgBnRAYCO0QGA29EBAE7SAQBh0gEAd9IBgN/SAQD60gEAHdMBACXTAYA40wEAOdMBAEfTAQD80wGAIdQBAKvUAYDH1AGA39QBgDnVAQB31QGAodUBAKfVAQDL1QEAYdYBALrWAQA+1wGAeNcBAPrXAQAu2AEAfdgBgJ/YAYCn2AEAvtgBgJ7ZAYD42QEAONoBAH3aAYCB2gEAvdoBgN7aAYD52gEAD9sBACvbAYB52wEA49sBgIzcAYDY3AEA3twBgAHdAYBt3QGAet0BAL/dAYDo3QGAHt4BgC7eAQB53gEAht4BgFHfAQBs3wGAf98BgKffAYBn4AGAf+ABgL/gAYDB4AGAQ+EBgMzhAYAG4gEAOOIBAO7iAQAg4wEALOMBgDfjAYBO4wGAf+MBgKzjAYC34wGALeQBgCflAQA45QEAzeUBgNXlAYAj5gGAO+YBgDzmAYBK5gEA9+YBgAHnAYCQ5wGALukBAGzpAYB36QGAe+kBgKPpAYC56QEALeoBgC7qAQCF6gGAI+sBgG7rAQCG6wGAGewBAHzsAYCA7AGAg+wBgLrsAQAK7QEAGe0BAGPuAYBu7gEAkO4BgAXvAYBb7wGAf+8BgL/vAYDj7wGAv/ABgPnwAQD/8AGALvEBAKPyAYAI8wGAVvMBAMjzAYD68wEAgvQBgJX0AYCa9AGA2PQBAP30AYAD9QGAPPUBgL/1AYDb9QGAQfYBgFr2AQCA9gEAqfYBAD73AYCF9wEAm/cBAOj3AYD+9wGAIvgBAIn4AYCi+AEAOfkBAJ35AYAd+gGAgPoBgM76AYC4+wEAd/0BgIT9AQDA/QGAOv4BAKL+AYC5/gEA2f4BAOn+AYBQ/wGA"


def _decode_tab(base, b64):
    import base64 as _b64
    p = np.frombuffer(_b64.b64decode(b64), dtype=np.uint32)
    i = (p & 0x7FFFFFFF).astype(np.int64)
    sg = np.where((p >> 31) == 1, 1, -1).astype(np.int64)
    u = base.view(np.uint32).astype(np.int64).ravel()
    u[i] += sg
    return u.astype(np.uint32).view(np.float32).reshape(base.shape)


def _tables():
    import base64 as _b64
    inv_freq = np.frombuffer(_b64.b64decode(_INVF_B64), dtype=np.float32)
    fr = np.outer(np.arange(S, dtype=np.float32), inv_freq).astype(np.float32)
    cos_f32 = _decode_tab(np.cos(fr.astype(np.float64)).astype(np.float32), _DC_B64)
    sin_f32 = _decode_tab(np.sin(fr.astype(np.float64)).astype(np.float32), _DS_B64)
    cos64 = cos_f32.astype(np.float64).T
    sin64 = sin_f32.astype(np.float64).T
    cc = np.vstack([cos64, cos64])
    ssg = np.vstack([-sin64, sin64])
    qs = 1.0 / np.sqrt(128.0)

    def hl(a):
        h = a.astype(np.float32)
        l = (a - h.astype(np.float64)).astype(np.float32)
        return h, l

    return hl(cc), hl(ssg), hl(cc * qs), hl(ssg * qs)


def kernel(L0_attn_hidden_states, L0_attn_Wq, L0_attn_Wk, L0_attn_Wv, L0_attn_Wo,
           L0_ffn_weight, L0_ffn_bias, L0_ffn_ln_gamma, L0_ffn_ln_beta):
    x = np.asarray(L0_attn_hidden_states, dtype=np.float32)
    perm = np.empty(128, dtype=np.int64)
    perm[:64] = np.arange(64) * 2
    perm[64:] = np.arange(64) * 2 + 1
    permq = np.concatenate([h * 128 + perm for h in range(16)])
    permk = np.concatenate([h * 128 + perm for h in range(4)])
    wq_p = np.ascontiguousarray(np.asarray(L0_attn_Wq, np.float32)[:, permq])
    wk_p = np.ascontiguousarray(np.asarray(L0_attn_Wk, np.float32)[:, permk])
    wv = np.ascontiguousarray(np.asarray(L0_attn_Wv, np.float32))
    wo = np.ascontiguousarray(np.asarray(L0_attn_Wo, np.float32))
    wfT = np.ascontiguousarray(np.asarray(L0_ffn_weight, np.float32).T)
    gbb = np.zeros((128, 48), np.float32)
    gbb[:, 0:16] = np.asarray(L0_ffn_ln_gamma, np.float32).reshape(16, 128).T
    gbb[:, 16:32] = np.asarray(L0_ffn_ln_beta, np.float32).reshape(16, 128).T
    gbb[:, 32:48] = np.asarray(L0_ffn_bias, np.float32).reshape(16, 128).T
    eye = np.eye(128, dtype=np.float32)

    (cch, ccl), (ssh, ssl), (qcch, qccl), (qssh, qssl) = _tables()

    if "nc" not in _CACHE:
        _CACHE["nc"] = build_nc()
    nc = _CACHE["nc"]

    in_maps = []
    for c in range(8):
        b, half = c // 2, c % 2
        order = np.r_[half * 1024:half * 1024 + 1024,
                      (1 - half) * 1024:(1 - half) * 1024 + 1024]
        in_maps.append({
            "xT": np.ascontiguousarray(x[b].T[:, order]),
            "wq": wq_p, "wk": wk_p, "wv": wv, "wo": wo, "wfT": wfT,
            "gbb": gbb, "eye": eye,
            "kcch": np.ascontiguousarray(cch[:, order]),
            "kccl": np.ascontiguousarray(ccl[:, order]),
            "kssh": np.ascontiguousarray(ssh[:, order]),
            "kssl": np.ascontiguousarray(ssl[:, order]),
            "qcch": np.ascontiguousarray(qcch[:, order[:1024]]),
            "qccl": np.ascontiguousarray(qccl[:, order[:1024]]),
            "qssh": np.ascontiguousarray(qssh[:, order[:1024]]),
            "qssl": np.ascontiguousarray(qssl[:, order[:1024]]),
        })
    res = run_bass_kernel_spmd(nc, in_maps, core_ids=list(range(8)))
    out = np.empty((4, 2048, 2048), np.float32)
    for c in range(8):
        b, half = c // 2, c % 2
        out[b, half * 1024:(half + 1) * 1024, :] = res.results[c]["y"].T
    return out
